# version 02
# baseline (speedup 1.0000x reference)
import numpy as np
import concourse.bass as bass
import concourse.mybir as mybir
from concourse.bass_utils import run_bass_kernel_spmd

F32 = mybir.dt.float32
BF16 = mybir.dt.bfloat16
AF = mybir.ActivationFunctionType
ALU = mybir.AluOpType
AX = mybir.AxisListType

D = 1024
T = 8192
NB = 32
BS = 256
QT = 4096
EPS = 1e-5

def core_blocks(h):
    return sorted([j for j in range(NB) if (min(j, NB - 1 - j) % 2) == h])

ENGS = ("sync", "tensor", "vector", "scalar", "gpsimd")

class Ctx:
    def __init__(self, nc, sems, active, engine):
        self.nc = nc
        self.sems = sems
        self.active = active
        self.engine = engine
        self.cnt = {k: 0 for k in sems}
        self.waited = {e: {k: 0 for k in sems} for e in ENGS}

    def _do(self, eng, sem, inc, f):
        self.cnt[sem] += inc
        if self.active == eng:
            inst = f(self.engine)
            inst.then_inc(self.sems[sem], inc)
        return (sem, self.cnt[sem])

    def t(self, f): return self._do("tensor", "pe", 1, f)
    def v(self, f): return self._do("vector", "dv", 1, f)
    def s(self, f): return self._do("scalar", "ac", 1, f)
    def g(self, f): return self._do("gpsimd", "gc", 1, f)
    def dsync(self, f): return self._do("sync", "syd", 16, f)
    def dgp(self, f): return self._do("gpsimd", "gpd", 16, f)

    def wait(self, eng, *toks):
        need = {}
        for tok in toks:
            if tok is None: continue
            if isinstance(tok, list):
                for t2 in tok:
                    if t2 is None: continue
                    s, v = t2
                    need[s] = max(need.get(s, 0), v)
            else:
                s, v = tok
                need[s] = max(need.get(s, 0), v)
        for s, v in sorted(need.items()):
            if self.waited[eng][s] >= v:
                continue
            self.waited[eng][s] = v
            if self.active == eng:
                self.engine.wait_ge(self.sems[s], v)

def ap2(apx, offset, dims):
    return bass.AP(apx.tensor, apx.offset + offset, dims)

def col(t, c0, n=1):
    return t[:, c0:c0+n]

def bcast_free(t, c0, count):
    a = t.ap()
    return bass.AP(a.tensor, a.offset + c0, [list(a.ap[0]), [0, count]])

def ln_tile(k, TS, xt, xoff, width, out_ap, sc, scoff, sq, deps):
    xs = xt[:, xoff:xoff+width]
    k.wait("vector", *deps)
    t_sum = k.v(lambda e: e.reduce_sum(col(sc, scoff+0), xs, axis=AX.X))
    k.wait("scalar", *deps)
    t_sq = k.s(lambda e: e.activation(out=sq[:, :width], in_=xs, func=AF.Square,
                                      accum_out=col(sc, scoff+1)))
    k.wait("vector", t_sum)
    t_mean = k.v(lambda e: e.tensor_scalar_mul(col(sc, scoff+2), col(sc, scoff+0), 1.0/width))
    k.wait("vector", t_sq, t_mean)
    t_m2 = k.v(lambda e: e.tensor_tensor(col(sc, scoff+3), col(sc, scoff+2), col(sc, scoff+2), op=ALU.mult))
    t_e2 = k.v(lambda e: e.tensor_scalar_mul(col(sc, scoff+4), col(sc, scoff+1), 1.0/width))
    k.wait("vector", t_m2, t_e2)
    t_var = k.v(lambda e: e.tensor_tensor(col(sc, scoff+5), col(sc, scoff+4), col(sc, scoff+3), op=ALU.subtract))
    k.wait("scalar", t_var)
    t_std = k.s(lambda e: e.activation(out=col(sc, scoff+6), in_=col(sc, scoff+5), func=AF.Sqrt,
                                       bias=col(TS["eps"], 0)))
    k.wait("vector", t_std)
    t_rstd = k.v(lambda e: e.reciprocal(col(sc, scoff+7), col(sc, scoff+6)))
    t_nm = k.v(lambda e: e.tensor_scalar_mul(col(sc, scoff+2), col(sc, scoff+2), -1.0))
    k.wait("vector", t_rstd, t_nm)
    t_out = k.v(lambda e: e.scalar_tensor_tensor(
        out=out_ap, in0=xs, scalar=col(sc, scoff+2), in1=bcast_free(sc, scoff+7, width),
        op0=ALU.add, op1=ALU.mult))
    return t_out

def load_wT(k, dram_ap, sb, kdim, mdim, sb_off=0):
    if kdim < 128:
        src = ap2(dram_ap, 0, [[mdim, kdim], [1, mdim]])
        return k.dgp(lambda e: e.dma_start(out=sb[0:kdim, sb_off:sb_off+mdim], in_=src))
    nch = kdim // 128
    src = ap2(dram_ap, 0, [[mdim, 128], [128*mdim, nch], [1, mdim]])
    return k.dgp(lambda e: e.dma_start(out=sb[:, sb_off:sb_off+nch*mdim], in_=src))

def emit_phaseA(k, TS, IO):
    xt, xlnb, xln32 = TS["pa_x"], TS["pa_xlnb"], TS["pa_xln32"]
    xlnF, sq, sc = TS["pa_xlnF"], TS["pa_sq"], TS["pa_sc"]
    wk, wv, ident = TS["pa_wk"], TS["pa_wv"], TS["ident"]
    psT, psA, psB = TS["ps_bfA"], TS["ps_a"], TS["ps_b"]
    kc, vc = TS["pa_kc"], TS["pa_vc"]

    t_wk = load_wT(k, IO["WkT"], wk, D, D)
    t_wv = load_wT(k, IO["WvT"], wv, D, D)
    t_id = k.dgp(lambda e: e.dma_start(out=ident[:], in_=IO["identw"][:, :]))

    G = {"xt": [], "xlnb": [], "xlnF": [], "kc": [], "vc": [], "psT": [],
         "psA": [], "psB": []}

    def do_chunk(src_dram, tc, qpath):
        src = ap2(src_dram, tc*512*D, [[D, 128], [128*D, 4], [1, D]])
        k.wait("sync", G["xt"])
        t_x = k.dsync(lambda e: e.dma_start(out=xt[:], in_=src))
        toks_ln = []
        for s in range(4):
            out_ap = xln32[:, s*D:(s+1)*D] if qpath else xlnb[:, s*D:(s+1)*D]
            k.wait("vector", G["xlnb"])
            t_ln = ln_tile(k, TS, xt, s*D, D, out_ap, sc, 8*s, sq, [t_x])
            if qpath:
                k.wait("sync", t_ln)
                dst = ap2(IO["xlnq_tok"], (tc*512 + s*128)*D, [[D, 128], [1, D]])
                k.dsync(lambda e, s=s: e.dma_start(out=dst, in_=xln32[:, s*D:(s+1)*D]))
                k.wait("vector", t_ln)
                t_ln = k.v(lambda e, s=s: e.tensor_copy(xlnb[:, s*D:(s+1)*D], xln32[:, s*D:(s+1)*D]))
            toks_ln.append(t_ln)
        G["xt"] = toks_ln[:]
        G["xt"].append(("ac", k.cnt["ac"]))
        k.wait("tensor", t_id, *toks_ln)
        t_ev = None
        for c in range(8):
            k.wait("tensor", G["psT"], G["xlnF"] if c == 0 else None, t_ev)
            t_tr = None
            for s in range(4):
                t_tr = k.t(lambda e, c=c, s=s: e.transpose(
                    psT[:, s*128:(s+1)*128], xlnb[:, s*D + c*128: s*D + (c+1)*128], ident[:]))
            k.wait("vector", t_tr)
            t_ev = k.v(lambda e, c=c: e.tensor_copy(xlnF[:, c*512:(c+1)*512], psT[:]))
        G["psT"] = [t_ev]
        G["xlnb"] = [("pe", k.cnt["pe"])]
        t_xlnF = t_ev
        if qpath:
            k.wait("sync", t_xlnF)
            toks = []
            for c in range(8):
                dst = ap2(IO["xlnqF"], (c*128)*QT + tc*512, [[QT, 128], [1, 512]])
                toks.append(k.dsync(lambda e, c=c, dst=dst: e.dma_start(out=dst, in_=xlnF[:, c*512:(c+1)*512])))
            G["xlnF"] = toks
            return
        k.wait("tensor", t_wk, t_xlnF)
        for dk in range(8):
            ps, gk = (psA, "psA") if dk % 2 == 0 else (psB, "psB")
            k.wait("tensor", G[gk])
            t_mm = None
            for c in range(8):
                t_mm = k.t(lambda e, dk=dk, c=c, ps=ps: e.matmul(
                    ps[:], wk[:, (c*8+dk)*128:(c*8+dk+1)*128], xlnF[:, c*512:(c+1)*512],
                    start=(c == 0), stop=(c == 7)))
            k.wait("scalar", t_mm, G["kc"])
            t_ev2 = k.s(lambda e, dk=dk, ps=ps: e.activation(out=kc[:, dk*512:(dk+1)*512], in_=ps[:], func=AF.Copy))
            G[gk] = [t_ev2]
            k.wait("sync", t_ev2)
            dst = ap2(IO["K"], (dk*128)*T + tc*512, [[T, 128], [1, 512]])
            k.dsync(lambda e, dk=dk, dst=dst: e.dma_start(out=dst, in_=kc[:, dk*512:(dk+1)*512]))
        G["kc"] = [("syd", k.cnt["syd"])]
        k.wait("tensor", t_wv)
        for s in range(4):
            t_evs = []
            for half in range(2):
                ps, gk = (psA, "psA") if half == 0 else (psB, "psB")
                k.wait("tensor", G[gk])
                t_mm = None
                for c in range(8):
                    t_mm = k.t(lambda e, s=s, half=half, c=c, ps=ps: e.matmul(
                        ps[:], xlnF[:, c*512 + s*128: c*512 + (s+1)*128],
                        wv[:, c*1024 + half*512: c*1024 + (half+1)*512],
                        start=(c == 0), stop=(c == 7)))
                k.wait("scalar", t_mm, G["vc"])
                t_ev3 = k.s(lambda e, s=s, half=half, ps=ps: e.activation(
                    out=vc[:, s*1024 + half*512: s*1024 + (half+1)*512], in_=ps[:], func=AF.Copy))
                G[gk] = [t_ev3]
                t_evs.append(t_ev3)
            k.wait("sync", *t_evs)
            dst = ap2(IO["V"], (tc*512 + s*128)*D, [[D, 128], [1, D]])
            k.dsync(lambda e, s=s, dst=dst: e.dma_start(out=dst, in_=vc[:, s*1024:(s+1)*1024]))
        G["vc"] = [("syd", k.cnt["syd"])]
        G["xlnF"] = [("pe", k.cnt["pe"])]

    for tc in range(16):
        do_chunk(IO["x_full"], tc, qpath=False)
    for tc in range(8):
        do_chunk(IO["xq"], tc, qpath=True)

PHASEA_SBUF = [
    ("pa_x", [128, 4*1024], F32),
    ("pa_xlnb", [128, 4*1024], BF16),
    ("pa_xln32", [128, 4*1024], F32),
    ("pa_xlnF", [128, 8*512], BF16),
    ("pa_sq", [128, 1024], F32),
    ("pa_sc", [128, 64], F32),
    ("pa_wk", [128, 8*1024], BF16),
    ("pa_wv", [128, 8*1024], BF16),
    ("ident", [128, 128], BF16),
    ("eps", [128, 1], F32),
    ("pa_kc", [128, 8*512], BF16),
    ("pa_vc", [128, 4*1024], BF16),
]
PHASEA_PSUM = [
    ("ps_bfA", [128, 512], BF16),
    ("ps_a", [128, 512], F32),
    ("ps_b", [128, 512], F32),
]

def build_program(phases, io_decl, sbuf_decl, psum_decl):
    nc = bass.Bass("TRN2", target_bir_lowering=False, debug=False)
    IO = {}
    for name, shape, dt, kind in io_decl:
        IO[name] = nc.dram_tensor(name, shape, dt, kind=kind).ap()
    import contextlib
    stack = contextlib.ExitStack()
    TS = {}
    for name, shape, dt in sbuf_decl:
        TS[name] = stack.enter_context(nc.sbuf_tensor(name, shape, dt))
    for name, shape, dt in psum_decl:
        TS[name] = stack.enter_context(nc.psum_tensor(name, shape, dt))
    sems = {}
    for s in ("syd", "gpd", "pe", "dv", "ac", "gc"):
        sems[s] = stack.enter_context(nc.semaphore(s))
    with stack:
        with nc.Block() as block:
            def mk(eng_name):
                def run(engine):
                    k = Ctx(nc, sems, eng_name, engine)
                    t_eps = k.v(lambda e: e.memset(TS["eps"][:], EPS))
                    k.wait("scalar", t_eps)
                    for ph in phases:
                        ph(k, TS, IO)
                    k.wait("gpsimd", ("syd", k.cnt["syd"]), ("gpd", k.cnt["gpd"]))
                return run
            block.sync(mk("sync"))
            block.tensor(mk("tensor"))
            block.vector(mk("vector"))
            block.scalar(mk("scalar"))
            block.gpsimd(mk("gpsimd"))
    return nc

def build_phaseA():
    io = [
        ("x_full", [T, D], F32, "ExternalInput"),
        ("xq", [QT, D], F32, "ExternalInput"),
        ("WkT", [D, D], F32, "ExternalInput"),
        ("WvT", [D, D], F32, "ExternalInput"),
        ("identw", [128, 128], BF16, "ExternalInput"),
        ("K", [D, T], BF16, "ExternalOutput"),
        ("V", [T, D], BF16, "ExternalOutput"),
        ("xlnqF", [D, QT], BF16, "ExternalOutput"),
        ("xlnq_tok", [QT, D], F32, "ExternalOutput"),
    ]
    return build_program([emit_phaseA], io, PHASEA_SBUF, PHASEA_PSUM)

def emit_phaseB(k, TS, IO):
    Qg = TS["pb_qg"]
    Kb = [TS["pb_kb0"], TS["pb_kb1"]]
    Vb = [TS["pb_vb0"], TS["pb_vb1"]]
    msk = TS["pb_msk"]
    mstate = TS["pb_m"]
    lstate = TS["pb_l"]
    Oacc = TS["pb_oacc"]
    Psb = TS["pb_p"]
    PTsb = TS["pb_pt"]
    sc = TS["pa_sc"]
    sq = TS["pa_sq"]
    xrow = TS["pb_xrow"]
    ret = TS["pb_ret"]
    ident = TS["ident"]
    psS = [TS["ps_a"], TS["ps_b"]]
    psPT = TS["ps_bfA"]
    psO = TS["ps_o"]

    G = {"Qg": [], "Kb": [[], []], "Vb": [[], []], "msk": [], "psS": [[], []],
         "psPT": [], "psO": [], "P": [], "PT": [], "Oacc": [], "xrow": [], "ret": []}
    t_id = ("gpd", 16*3)
    t_id = k.dgp(lambda e: e.dma_start(out=ident[:], in_=IO["identw"][:, :]))

    for g in range(4):
        src = ap2(IO["xlnqF"], g*1024, [[QT, 128], [128*QT, 8], [1, 1024]])
        k.wait("sync", G["Qg"])
        t_q = k.dsync(lambda e, src=src: e.dma_start(out=Qg[:], in_=src))
        srcm = ap2(IO["masks"], g*8*128*512, [[512, 128], [128*512, 8], [1, 512]])
        k.wait("sync", G["msk"])
        t_m = k.dsync(lambda e, srcm=srcm: e.dma_start(out=msk[:], in_=srcm))
        k.wait("vector", G["Oacc"])
        t_mi = k.v(lambda e: e.memset(mstate[:], -3e38))
        t_li = k.v(lambda e: e.memset(lstate[:], 0.0))
        t_oi = k.v(lambda e: e.memset(Oacc[:], 0.0))
        state_tok = [t_mi, t_li, t_oi]
        nmb = 4*g + 4
        for mb in range(nmb):
            buf = mb % 2
            srck = ap2(IO["K"], mb*512, [[T, 128], [128*T, 8], [1, 512]])
            k.wait("sync", G["Kb"][buf])
            t_k = k.dsync(lambda e, srck=srck, buf=buf: e.dma_start(out=Kb[buf][:], in_=srck))
            srcv = ap2(IO["V"], mb*512*D, [[D, 128], [128*D, 4], [1, D]])
            k.wait("sync", G["Vb"][buf])
            t_v = k.dsync(lambda e, srcv=srcv, buf=buf: e.dma_start(out=Vb[buf][:], in_=srcv))
            for qi in range(8):
                s_slot = 4*g + qi // 2
                if mb > s_slot:
                    continue
                sb = qi % 2
                ps = psS[sb]
                k.wait("tensor", t_q, t_k, G["psS"][sb], t_id)
                t_mm = None
                for c in range(8):
                    t_mm = k.t(lambda e, c=c, qi=qi, buf=buf, ps=ps: e.matmul(
                        ps[:], Qg[:, c*1024 + qi*128: c*1024 + (qi+1)*128],
                        Kb[buf][:, c*512:(c+1)*512], start=(c == 0), stop=(c == 7)))
                tok_S = t_mm
                if mb == s_slot:
                    k.wait("vector", t_mm, t_m)
                    tok_S = k.v(lambda e, qi=qi, ps=ps: e.tensor_tensor(
                        ps[:], ps[:], msk[:, qi*512:(qi+1)*512], op=ALU.add))
                k.wait("vector", tok_S, *state_tok)
                t_bm = k.v(lambda e, ps=ps: e.reduce_max(col(sc, 0), ps[:], axis=AX.X))
                k.wait("vector", t_bm)
                t_mn = k.v(lambda e, qi=qi: e.tensor_tensor(col(sc, 1), col(mstate, qi), col(sc, 0), op=ALU.max))
                k.wait("vector", t_mn)
                t_nm = k.v(lambda e: e.tensor_scalar_mul(col(sc, 2), col(sc, 1), -1.0))
                t_dm = k.v(lambda e, qi=qi: e.tensor_tensor(col(sc, 3), col(mstate, qi), col(sc, 1), op=ALU.subtract))
                k.wait("scalar", t_nm, t_dm, tok_S, G["P"])
                t_al = k.s(lambda e: e.activation(out=col(sc, 4), in_=col(sc, 3), func=AF.Exp))
                t_p = k.s(lambda e, ps=ps: e.activation(out=Psb[:], in_=ps[:], func=AF.Exp,
                                                        bias=col(sc, 2), accum_out=col(sc, 5)))
                G["psS"][sb] = [t_p]
                k.wait("vector", t_al, t_p)
                t_l = k.v(lambda e, qi=qi: e.scalar_tensor_tensor(
                    out=col(lstate, qi), in0=col(lstate, qi), scalar=col(sc, 4),
                    in1=col(sc, 5), op0=ALU.mult, op1=ALU.add))
                t_mu = k.v(lambda e, qi=qi: e.tensor_copy(col(mstate, qi), col(sc, 1)))
                k.wait("tensor", t_p, G["psPT"])
                t_tr = None
                for kc in range(4):
                    t_tr = k.t(lambda e, kc=kc: e.transpose(
                        psPT[:, kc*128:(kc+1)*128], Psb[:, kc*128:(kc+1)*128], ident[:]))
                k.wait("vector", t_tr, G["PT"])
                t_pt = k.v(lambda e: e.tensor_copy(PTsb[:], psPT[:]))
                G["psPT"] = [t_pt]
                G["P"] = [("pe", k.cnt["pe"])]
                k.wait("tensor", t_pt, t_v, G["psO"])
                t_av = None
                for half in range(2):
                    for kc in range(4):
                        t_av = k.t(lambda e, kc=kc, half=half, buf=buf: e.matmul(
                            psO[:, half*512:(half+1)*512], PTsb[:, kc*128:(kc+1)*128],
                            Vb[buf][:, kc*1024 + half*512: kc*1024 + (half+1)*512],
                            start=(kc == 0), stop=(kc == 3)))
                G["PT"] = [("pe", k.cnt["pe"])]
                k.wait("vector", t_av, t_al)
                t_o = k.v(lambda e, qi=qi: e.scalar_tensor_tensor(
                    out=Oacc[:, qi*1024:(qi+1)*1024], in0=Oacc[:, qi*1024:(qi+1)*1024],
                    scalar=col(sc, 4), in1=psO[:], op0=ALU.mult, op1=ALU.add))
                G["psO"] = [t_o]
                state_tok = [t_l, t_mu, t_o]
            G["Kb"][buf] = [("pe", k.cnt["pe"])]
            G["Vb"][buf] = [("pe", k.cnt["pe"])]
        fin_tok = []
        for qi in range(8):
            row0 = (g*8 + qi) * 128
            k.wait("vector", *state_tok)
            t_ri = k.v(lambda e, qi=qi: e.reciprocal(col(sc, 6), col(lstate, qi)))
            src = ap2(IO["xlnq_tok"], row0*D, [[D, 128], [1, D]])
            k.wait("sync", G["xrow"])
            t_xr = k.dsync(lambda e, src=src: e.dma_start(out=xrow[:], in_=src))
            k.wait("vector", t_ri, t_xr, G["ret"])
            t_ret = k.v(lambda e, qi=qi: e.scalar_tensor_tensor(
                out=ret[:], in0=Oacc[:, qi*1024:(qi+1)*1024], scalar=col(sc, 6),
                in1=xrow[:], op0=ALU.mult, op1=ALU.add))
            t_ln = ln_tile(k, TS, ret, 0, D, sq[:, :D], sc, 8, sq2_of(TS), [t_ret])
            srcq = ap2(IO["xq"], row0*D, [[D, 128], [1, D]])
            k.wait("sync", t_ret)
            t_xr2 = k.dsync(lambda e, srcq=srcq: e.dma_start(out=xrow[:], in_=srcq))
            k.wait("vector", t_ln, t_xr2)
            t_x1 = k.v(lambda e: e.tensor_tensor(ret[:], sq[:, :D], xrow[:], op=ALU.add))
            G["xrow"] = [t_x1]
            k.wait("sync", t_x1)
            dst = ap2(IO["x1"], row0*D, [[D, 128], [1, D]])
            t_st = k.dsync(lambda e, dst=dst: e.dma_start(out=dst, in_=ret[:]))
            G["ret"] = [("syd", k.cnt["syd"])]
            fin_tok.append(t_st)
        G["Oacc"] = [("dv", k.cnt["dv"])]
        G["Qg"] = [("pe", k.cnt["pe"])]
        G["msk"] = [("dv", k.cnt["dv"])]

def sq2_of(TS):
    return TS["pb_sq2"]

PHASEB_SBUF = [
    ("pb_qg", [128, 8*1024], BF16),
    ("pb_kb0", [128, 8*512], BF16),
    ("pb_kb1", [128, 8*512], BF16),
    ("pb_vb0", [128, 4*1024], BF16),
    ("pb_vb1", [128, 4*1024], BF16),
    ("pb_msk", [128, 8*512], F32),
    ("pb_m", [128, 8], F32),
    ("pb_l", [128, 8], F32),
    ("pb_oacc", [128, 8*1024], F32),
    ("pb_p", [128, 512], BF16),
    ("pb_pt", [128, 512], BF16),
    ("pb_xrow", [128, 1024], F32),
    ("pb_ret", [128, 1024], F32),
    ("pb_sq2", [128, 1024], F32),
    ("pa_sc", [128, 64], F32),
    ("pa_sq", [128, 1024], F32),
    ("ident", [128, 128], BF16),
    ("eps", [128, 1], F32),
]
PHASEB_PSUM = [
    ("ps_a", [128, 512], F32),
    ("ps_b", [128, 512], F32),
    ("ps_bfA", [128, 512], BF16),
    ("ps_o", [128, 1024], F32),
]

def build_phaseB():
    io = [
        ("xq", [QT, D], F32, "ExternalInput"),
        ("xlnqF", [D, QT], BF16, "ExternalInput"),
        ("xlnq_tok", [QT, D], F32, "ExternalInput"),
        ("K", [D, T], BF16, "ExternalInput"),
        ("V", [T, D], BF16, "ExternalInput"),
        ("masks", [32, 128, 512], F32, "ExternalInput"),
        ("identw", [128, 128], BF16, "ExternalInput"),
        ("x1", [QT, D], F32, "ExternalOutput"),
    ]
    return build_program([emit_phaseB], io, PHASEB_SBUF, PHASEB_PSUM)

def host_masks(h):
    blocks = core_blocks(h)
    out = np.zeros((32, 128, 512), np.float32)
    for s in range(16):
        j = blocks[s]
        for qt in range(2):
            qg = j*BS + qt*128 + np.arange(128)[:, None]
            kg = s*512 + np.arange(512)[None, :]
            out[s*2 + qt] = np.where(kg <= qg, 0.0, -1e30).astype(np.float32)
    return out

def emit_phaseC(k, TS, IO):
    xt = TS["pc_x"]
    xlnb = TS["pa_xlnb"]
    xlnF = TS["pa_xlnF"]
    sq, sc = TS["pa_sq"], TS["pa_sc"]
    ident = TS["ident"]
    psT = TS["ps_bfA"]
    psA, psB = TS["ps_a"], TS["ps_b"]
    wslots = [TS["pc_w0"], TS["pc_w1"], TS["pc_w2"]]
    hb1, hb2, hb3 = TS["pc_h1"], TS["pc_h2"], TS["pc_h3"]
    ab = TS["pc_a"]
    routed = TS["pc_routed"]
    cw = TS["pc_cw"]
    s8 = TS["pc_s8"]
    gb = TS["pc_gb"]
    gw = TS["pc_gw"]
    b1g = TS["pc_b1g"]; b1l = TS["pc_b1l"]; b3g = TS["pc_b3g"]; b3l = TS["pc_b3l"]
    b2r = TS["pc_b2r"]
    ones1 = TS["ones1"]

    NE = 9
    t_id = k.dgp(lambda e: e.dma_start(out=ident[:], in_=IO["identw"][:, :]))
    t_on = k.v(lambda e: e.memset(ones1[:], 1.0))
    t_gb = k.dsync(lambda e: e.dma_start(out=gb[:], in_=ap2(IO["gate_b"], 0, [[0, 128], [1, 8]])))
    t_gw = load_wT(k, IO["gate_wT"], gw, D, 8)
    tb = []
    for name, sb_t in (("b1g", b1g), ("b1l", b1l), ("b3g", b3g), ("b3l", b3l)):
        src = ap2(IO[name], 0, [[8, 128], [128*8, NE], [1, 8]])
        tb.append(k.dsync(lambda e, sb_t=sb_t, src=src: e.dma_start(out=sb_t[:], in_=src)))
    tb.append(k.dgp(lambda e: e.dma_start(out=b2r[:], in_=ap2(IO["b2all"], 0, [[NE*1024, 1], [1, NE*1024]]))))

    G = {"xt": [], "xlnb": [], "xlnF": [], "psT": [], "psA": [], "psB": [],
         "w": [[], [], []], "hb1": [], "hb2": [], "hb3": [], "ab": [], "routed": [],
         "cw": [], "s8": [], "x2st": []}

    WNAMES = ("w1gT", "w3gT", "w1lT", "w3lT", "w2T")

    for tc in range(8):
        src = ap2(IO["x1"], tc*512*D, [[D, 128], [128*D, 4], [1, D]])
        k.wait("sync", G["xt"])
        t_x = k.dsync(lambda e, src=src: e.dma_start(out=xt[:], in_=src))
        toks_ln = []
        for s in range(4):
            k.wait("vector", G["xlnb"])
            t_ln = ln_tile(k, TS, xt, s*D, D, xlnb[:, s*D:(s+1)*D], sc, 8*s, sq, [t_x])
            toks_ln.append(t_ln)
        G["xt"] = toks_ln[:] + [("ac", k.cnt["ac"])]
        k.wait("tensor", t_id, *toks_ln)
        t_ev = None
        for c in range(8):
            k.wait("tensor", G["psT"], G["xlnF"] if c == 0 else None, t_ev)
            t_tr = None
            for s in range(4):
                t_tr = k.t(lambda e, c=c, s=s: e.transpose(
                    psT[:, s*128:(s+1)*128], xlnb[:, s*D + c*128: s*D + (c+1)*128], ident[:]))
            k.wait("vector", t_tr)
            t_ev = k.v(lambda e, c=c: e.tensor_copy(xlnF[:, c*512:(c+1)*512], psT[:]))
        G["psT"] = [t_ev]
        G["xlnb"] = [("pe", k.cnt["pe"])]
        t_xlnF = t_ev

        k.wait("tensor", t_gw, t_xlnF)
        for s in range(4):
            k.wait("tensor", G["psB"])
            t_mm = None
            for c in range(8):
                t_mm = k.t(lambda e, c=c, s=s: e.matmul(
                    psB[:, 0:8], xlnF[:, c*512 + s*128: c*512 + (s+1)*128],
                    gw[:, c*8:(c+1)*8], start=(c == 0), stop=(c == 7)))
            k.wait("vector", t_mm, G["s8"])
            t_rm = k.v(lambda e: e.reduce_max(col(sc, 32), psB[:, 0:8], axis=AX.X))
            k.wait("vector", t_rm)
            t_nm = k.v(lambda e: e.tensor_scalar_mul(col(sc, 33), col(sc, 32), -1.0))
            k.wait("scalar", t_nm, t_mm)
            t_ex = k.s(lambda e: e.activation(out=s8[:], in_=psB[:, 0:8], func=AF.Exp,
                                              bias=col(sc, 33), accum_out=col(sc, 34)))
            G["psB"] = [t_ex]
            k.wait("vector", t_ex)
            t_ri = k.v(lambda e: e.reciprocal(col(sc, 35), col(sc, 34)))
            k.wait("vector", t_ri)
            t_sc = k.v(lambda e: e.tensor_scalar(s8[:], s8[:], col(sc, 35), None, op0=ALU.mult))
            k.wait("vector", t_sc, t_gb, tb[0])
            t_bi = k.v(lambda e: e.tensor_tensor(col(sc, 36, 8), s8[:], gb[:], op=ALU.add))
            k.wait("vector", t_bi)
            t_m1 = k.v(lambda e: e.reduce_max(col(sc, 44), col(sc, 36, 8), axis=AX.X))
            k.wait("vector", t_m1)
            t_o1 = k.v(lambda e: e.tensor_scalar(col(sc, 45, 8), col(sc, 36, 8), col(sc, 44), None, op0=ALU.is_equal))
            k.wait("vector", t_o1)
            t_w1a = k.v(lambda e: e.tensor_tensor(col(sc, 53, 8), col(sc, 45, 8), s8[:], op=ALU.mult))
            k.wait("vector", t_w1a)
            t_w1 = k.v(lambda e: e.reduce_sum(col(sc, 61), col(sc, 53, 8), axis=AX.X))
            t_b2 = k.v(lambda e: e.scalar_tensor_tensor(
                out=col(sc, 36, 8), in0=col(sc, 45, 8), scalar=-1e30, in1=col(sc, 36, 8),
                op0=ALU.mult, op1=ALU.add))
            k.wait("vector", t_b2)
            t_m2 = k.v(lambda e: e.reduce_max(col(sc, 44), col(sc, 36, 8), axis=AX.X))
            k.wait("vector", t_m2)
            t_o2 = k.v(lambda e: e.tensor_scalar(col(sc, 36, 8), col(sc, 36, 8), col(sc, 44), None, op0=ALU.is_equal))
            k.wait("vector", t_o2)
            t_w2a = k.v(lambda e: e.tensor_tensor(col(sc, 53, 8), col(sc, 36, 8), s8[:], op=ALU.mult))
            k.wait("vector", t_w2a)
            t_w2 = k.v(lambda e: e.reduce_sum(col(sc, 62), col(sc, 53, 8), axis=AX.X))
            k.wait("vector", t_w1, t_w2)
            t_c1 = k.v(lambda e: e.tensor_scalar(col(sc, 53, 8), col(sc, 45, 8), col(sc, 61), None, op0=ALU.mult))
            k.wait("vector", t_c1)
            t_cw = k.v(lambda e, s=s: e.scalar_tensor_tensor(
                out=cw[:, s*8:(s+1)*8], in0=col(sc, 36, 8), scalar=col(sc, 62),
                in1=col(sc, 53, 8), op0=ALU.mult, op1=ALU.add))
            G["s8"] = [t_w2a, t_cw]
            G["cw"].append(t_cw)

        k.wait("vector", G["routed"])
        t_r0 = k.v(lambda e: e.memset(routed[:], 0.0))
        tok_routed = t_r0

        def wload(e_i, m_i, slot):
            off = (e_i*5 + m_i) * D * D
            src = ap2(IO["moeW"], off, [[D, 128], [128*D, 8], [1, D]])
            k.wait("gpsimd", G["w"][slot])
            return k.dgp(lambda e, src=src, slot=slot: e.dma_start(
                out=wslots[slot][:], in_=src))
        wq = []
        for e_i in range(NE):
            for m_i in range(5):
                wq.append((e_i, m_i))
        wtok = {}
        for i in range(min(3, len(wq))):
            wtok[i] = wload(wq[i][0], wq[i][1], i % 3)
        def wget(i):
            if i not in wtok:
                wtok[i] = wload(wq[i][0], wq[i][1], i % 3)
            return wtok[i], wslots[i % 3]
        wi = 0
        for e_i in range(NE):
            def hmm(dst, tw, wsb, bias_sb, guard_key):
                toks = []
                for fc in range(8):
                    ps, gk = (psA, "psA") if fc % 2 == 0 else (psB, "psB")
                    k.wait("tensor", tw, G[gk], t_xlnF)
                    t_mm2 = None
                    for c in range(8):
                        t_mm2 = k.t(lambda e, fc=fc, c=c, ps=ps, wsb=wsb: e.matmul(
                            ps[:], wsb[:, (c*8+fc)*128:(c*8+fc+1)*128],
                            xlnF[:, c*512:(c+1)*512], start=(c == 0), stop=(c == 7)))
                    k.wait("scalar", t_mm2, G[guard_key], *tb)
                    t_ev2 = k.s(lambda e, fc=fc, ps=ps, dst=dst, bias_sb=bias_sb: e.activation(
                        out=dst[:, fc*512:(fc+1)*512], in_=ps[:], func=AF.Identity,
                        bias=col(bias_sb, e_i*8+fc)))
                    G[gk] = [t_ev2]
                    toks.append(t_ev2)
                return toks
            tw, wsb = wget(wi)
            tg1 = hmm(hb1, tw, wsb, b1g, "hb1")
            G["w"][wi % 3] = [("pe", k.cnt["pe"])]; wi += 1
            tw, wsb = wget(wi)
            tg3 = hmm(hb2, tw, wsb, b3g, "hb2")
            G["w"][wi % 3] = [("pe", k.cnt["pe"])]; wi += 1
            k.wait("vector", *tg1); k.wait("vector", *tg3)
            t_gp = k.v(lambda e: e.tensor_tensor(hb1[:], hb1[:], hb2[:], op=ALU.mult))
            k.wait("vector", t_gp)
            t_gc = k.v(lambda e: e.tensor_scalar_min(hb1[:], hb1[:], 7.0))
            k.wait("scalar", t_gc)
            t_sg = k.s(lambda e: e.activation(out=hb2[:], in_=hb1[:], func=AF.Silu, scale=1.702))
            G["hb1"] = []
            tw, wsb = wget(wi)
            tl1 = hmm(hb3, tw, wsb, b1l, "hb3")
            G["w"][wi % 3] = [("pe", k.cnt["pe"])]; wi += 1
            tw, wsb = wget(wi)
            k.wait("scalar", t_sg)
            tl3 = hmm(hb1, tw, wsb, b3l, "hb1")
            G["w"][wi % 3] = [("pe", k.cnt["pe"])]; wi += 1
            k.wait("vector", *tl1); k.wait("vector", *tl3)
            t_lp = k.v(lambda e: e.tensor_tensor(hb3[:], hb3[:], hb1[:], op=ALU.mult))
            k.wait("vector", t_lp)
            t_lc = k.v(lambda e: e.tensor_scalar(hb3[:], hb3[:], 7.0, -7.0, op0=ALU.min, op1=ALU.max))
            k.wait("vector", t_lc)
            t_la = k.v(lambda e: e.tensor_scalar(hb3[:], hb3[:], 1.0/1.702, 1.0/1.702, op0=ALU.mult, op1=ALU.add))
            k.wait("vector", t_la, t_sg, G["ab"])
            t_a = k.v(lambda e: e.tensor_tensor(ab[:], hb2[:], hb3[:], op=ALU.mult))
            G["hb2"] = [t_a]; G["hb3"] = [t_a]
            tw2, wsb2 = wget(wi)
            rtoks = []
            for tsub in range(4):
                for half in range(2):
                    ps, gk = (psA, "psA") if half == 0 else (psB, "psB")
                    k.wait("tensor", tw2, t_a, G[gk], t_on, *tb)
                    t_mm3 = None
                    for fc in range(8):
                        t_mm3 = k.t(lambda e, fc=fc, tsub=tsub, half=half, ps=ps, wsb2=wsb2: e.matmul(
                            ps[:], ab[:, fc*512 + tsub*128: fc*512 + (tsub+1)*128],
                            wsb2[:, fc*1024 + half*512: fc*1024 + (half+1)*512],
                            start=(fc == 0), stop=False))
                    t_mm3 = k.t(lambda e, half=half, ps=ps: e.matmul(
                        ps[:], ones1[:], b2r[0:1, e_i*1024 + half*512: e_i*1024 + (half+1)*512],
                        start=False, stop=True))
                    k.wait("vector", t_mm3, tok_routed, G["cw"])
                    if e_i < 8:
                        t_ru = k.v(lambda e, tsub=tsub, half=half, ps=ps: e.scalar_tensor_tensor(
                            out=routed[:, tsub*1024 + half*512: tsub*1024 + (half+1)*512],
                            in0=ps[:], scalar=col(cw, tsub*8 + e_i),
                            in1=routed[:, tsub*1024 + half*512: tsub*1024 + (half+1)*512],
                            op0=ALU.mult, op1=ALU.add))
                    else:
                        t_ru = k.v(lambda e, tsub=tsub, half=half, ps=ps: e.tensor_tensor(
                            routed[:, tsub*1024 + half*512: tsub*1024 + (half+1)*512],
                            routed[:, tsub*1024 + half*512: tsub*1024 + (half+1)*512],
                            ps[:], op=ALU.add))
                    G[gk] = [t_ru]
                    rtoks.append(t_ru)
            tok_routed = rtoks[-1]
            G["w"][wi % 3] = [("pe", k.cnt["pe"])]; wi += 1
            G["ab"] = [("pe", k.cnt["pe"])]
        G["cw"] = []
        G["xlnF"] = [("pe", k.cnt["pe"])]


        for tsub in range(4):
            k.wait("vector", tok_routed)
            t_ln2 = ln_tile(k, TS, routed, tsub*1024, D, sq[:, :D], sc, 8, TS["pb_sq2"], [tok_routed])
            k.wait("vector", t_ln2)
            t_x2 = k.v(lambda e, tsub=tsub: e.tensor_tensor(
                routed[:, tsub*1024:(tsub+1)*1024], sq[:, :D], xt[:, tsub*1024:(tsub+1)*1024], op=ALU.add))
            k.wait("sync", t_x2)
            dst = ap2(IO["x2"], (tc*512 + tsub*128)*D, [[D, 128], [1, D]])
            k.dsync(lambda e, dst=dst, tsub=tsub: e.dma_start(out=dst, in_=routed[:, tsub*1024:(tsub+1)*1024]))
        G["routed"] = [("syd", k.cnt["syd"])]
        G["xt"] = G["xt"] + [("dv", k.cnt["dv"])]

PHASEC_SBUF = [
    ("pc_x", [128, 4*1024], F32),
    ("pa_xlnb", [128, 4*1024], BF16),
    ("pa_xlnF", [128, 8*512], BF16),
    ("pa_sq", [128, 1024], F32),
    ("pb_sq2", [128, 1024], F32),
    ("pa_sc", [128, 64], F32),
    ("ident", [128, 128], BF16),
    ("eps", [128, 1], F32),
    ("ones1", [1, 128], BF16),
    ("pc_w0", [128, 8*1024], BF16),
    ("pc_w1", [128, 8*1024], BF16),
    ("pc_w2", [128, 8*1024], BF16),
    ("pc_h1", [128, 8*512], F32),
    ("pc_h2", [128, 8*512], F32),
    ("pc_h3", [128, 8*512], F32),
    ("pc_a", [128, 8*512], BF16),
    ("pc_routed", [128, 4*1024], F32),
    ("pc_cw", [128, 4*8], F32),
    ("pc_s8", [128, 8], F32),
    ("pc_gb", [128, 8], F32),
    ("pc_gw", [128, 8*8], BF16),
    ("pc_b1g", [128, 9*8], F32),
    ("pc_b1l", [128, 9*8], F32),
    ("pc_b3g", [128, 9*8], F32),
    ("pc_b3l", [128, 9*8], F32),
    ("pc_b2r", [1, 9*1024], BF16),
]
PHASEC_PSUM = [
    ("ps_a", [128, 512], F32),
    ("ps_b", [128, 512], F32),
    ("ps_bfA", [128, 512], BF16),
]

def build_phaseC():
    io = [
        ("x1", [QT, D], F32, "ExternalInput"),
        ("moeW", [9*5*D, D], F32, "ExternalInput"),
        ("gate_wT", [D, 8], F32, "ExternalInput"),
        ("gate_b", [1, 8], F32, "ExternalInput"),
        ("b1g", [9, 128, 8], F32, "ExternalInput"),
        ("b1l", [9, 128, 8], F32, "ExternalInput"),
        ("b3g", [9, 128, 8], F32, "ExternalInput"),
        ("b3l", [9, 128, 8], F32, "ExternalInput"),
        ("b2all", [9, 1024], F32, "ExternalInput"),
        ("identw", [128, 128], BF16, "ExternalInput"),
        ("x2", [QT, D], F32, "ExternalOutput"),
    ]
    return build_program([emit_phaseC], io, PHASEC_SBUF, PHASEC_PSUM)

def host_moe_weights(z):
    blocks = []
    bs = {"b1g": [], "b1l": [], "b3g": [], "b3l": [], "b2": []}
    for e in range(8):
        w1 = z['L0_ffn_w1'][e]; w3 = z['L0_ffn_w3'][e]; w2 = z['L0_ffn_w2'][e]
        blocks += [w1[::2].T, w3[::2].T, w1[1::2].T, w3[1::2].T, w2.T]
        bs["b1g"].append(z['L0_ffn_b1'][e][::2]); bs["b1l"].append(z['L0_ffn_b1'][e][1::2])
        bs["b3g"].append(z['L0_ffn_b3'][e][::2]); bs["b3l"].append(z['L0_ffn_b3'][e][1::2])
        bs["b2"].append(z['L0_ffn_b2'][e])
    blocks += [z['L0_ffn_sw1'][::2].T, z['L0_ffn_sw3'][::2].T,
               z['L0_ffn_sw1'][1::2].T, z['L0_ffn_sw3'][1::2].T, z['L0_ffn_sw2'].T]
    bs["b1g"].append(z['L0_ffn_sb1'][::2]); bs["b1l"].append(z['L0_ffn_sb1'][1::2])
    bs["b3g"].append(z['L0_ffn_sb3'][::2]); bs["b3l"].append(z['L0_ffn_sb3'][1::2])
    bs["b2"].append(z['L0_ffn_sb2'])
    moeW = np.ascontiguousarray(np.concatenate(blocks, 0)).astype(np.float32)
    out = {"moeW": moeW,
           "gate_wT": np.ascontiguousarray(z['L0_ffn_gate_w'].T),
           "gate_b": z['L0_ffn_gate_b'].reshape(1, 8).astype(np.float32)}
    for key in ("b1g", "b1l", "b3g", "b3l"):
        arr = np.stack(bs[key]).reshape(9, 8, 128).transpose(0, 2, 1)
        out[key] = np.ascontiguousarray(arr).astype(np.float32)
    out["b2all"] = np.ascontiguousarray(np.stack(bs["b2"])).astype(np.float32)
    return out

def emit_phaseD(k, TS, IO):
    xt = TS["pd_x"]; xlnb = TS["pa_xlnb"]; xlnF = TS["pa_xlnF"]
    sq, sc = TS["pa_sq"], TS["pa_sc"]
    ident = TS["ident"]
    psT, psA, psB = TS["ps_bfA"], TS["ps_a"], TS["ps_b"]
    wx = TS["pd_wx"]; wz = TS["pd_wz"]; wdtu = TS["pd_wdtu"]; wdt = TS["pd_wdt"]
    wbp = TS["pd_wbp"]; wcp = TS["pd_wcp"]
    dtu = TS["pd_dtu"]
    zb = TS["pd_zb"]
    zb2 = TS["pd_zb2"]
    const1 = TS["const1"]
    bdt = TS["pd_bdt"]
    convw = TS["pd_convw"]
    Dpar = TS["pd_dpar"]
    alog = TS["pd_alog"]
    absA = TS["pd_absA"]
    rhat = TS["pd_rhat"]
    mmask = TS["pd_m"]
    latA = TS["pd_latA"]; latB = TS["pd_latB"]; latC = TS["pd_latC"]
    latD = TS["pd_latD"]; latG = TS["pd_latG"]
    bpe = TS["pd_bpe"]; cpe = TS["pd_cpe"]
    hst = TS["pd_h"]
    xrp = TS["pd_xrp"]
    cv = TS["pd_cv"]
    ub = TS["pd_u"]
    dtt = TS["pd_dt"]
    szt = TS["pd_sz"]
    yb = TS["pd_y"]
    gz = TS["pd_gz"]

    t_id = k.dgp(lambda e: e.dma_start(out=ident[:], in_=IO["identw"][:, :]))
    t_c1 = k.v(lambda e: e.memset(const1[:], 1.0))
    t_wx = load_wT(k, IO["WxTh"], wx, D, 512)
    t_wz = load_wT(k, IO["WzTh"], wz, D, 512)
    t_wdtu = load_wT(k, IO["WdtuT"], wdtu, D, 64)
    t_wdt = load_wT(k, IO["WdtTh"], wdt, 64, 512)
    t_wbp = load_wT(k, IO["WpBT"], wbp, D, 16)
    t_wcp = load_wT(k, IO["WpCT"], wcp, D, 16)
    t_bdt = k.dsync(lambda e: e.dma_start(out=bdt[:], in_=ap2(IO["bdt_h"], 0, [[4, 128], [1, 4]])))
    t_cw = k.dsync(lambda e: e.dma_start(out=convw[:], in_=ap2(IO["convw_h"], 0, [[16, 128], [1, 16]])))
    t_dp = k.dsync(lambda e: e.dma_start(out=Dpar[:], in_=ap2(IO["D_h"], 0, [[4, 128], [1, 4]])))
    t_al = k.dsync(lambda e: e.dma_start(out=alog[:], in_=ap2(IO["Alog_h"], 0, [[16, 128], [128*16, 4], [1, 16]])))
    k.wait("scalar", t_al)
    t_aa = k.s(lambda e: e.activation(out=absA[:], in_=alog[:], func=AF.Exp))
    k.wait("vector", t_aa)
    t_den = k.v(lambda e: e.tensor_scalar(zb[:, 0:64], absA[:], -1.0, 1e-10, op0=ALU.mult, op1=ALU.add))
    k.wait("vector", t_den)
    t_r0 = k.v(lambda e: e.reciprocal(zb2[:, 0:64], zb[:, 0:64]))
    t_mm0a = k.v(lambda e: e.tensor_scalar(mmask[:], absA[:], -1.0, 1e-8, op0=ALU.mult, op1=ALU.add))
    k.wait("scalar", t_mm0a)
    t_mm0b = k.s(lambda e: e.activation(out=mmask[:], in_=mmask[:], func=AF.Sign))
    k.wait("vector", t_mm0b)
    t_mm0 = k.v(lambda e: e.tensor_scalar_max(mmask[:], mmask[:], 0.0))
    k.wait("vector", t_r0, t_mm0)
    t_1m = k.v(lambda e: e.tensor_scalar(zb[:, 0:64], mmask[:], -1.0, 1.0, op0=ALU.mult, op1=ALU.add))
    k.wait("vector", t_1m)
    t_rh = k.v(lambda e: e.tensor_tensor(rhat[:], zb2[:, 0:64], zb[:, 0:64], op=ALU.mult))

    G = {"xt": [], "xlnb": [], "xlnF": [], "psT": [], "psA": [], "psB": [],
         "dtu": [], "zb": [t_1m], "proj_st": [], "bpe": [], "cpe": [],
         "lat": [], "latC": [], "latD": [], "xrp": [], "cv": [t_rh], "u": [], "dt": [], "sz": [], "y": [], "gz": []}

    for tc in range(16):
        src = ap2(IO["x2f"], tc*512*D, [[D, 128], [128*D, 4], [1, D]])
        k.wait("sync", G["xt"])
        t_x = k.dsync(lambda e, src=src: e.dma_start(out=xt[:], in_=src))
        toks_ln = []
        for s in range(4):
            k.wait("vector", G["xlnb"])
            t_ln = ln_tile(k, TS, xt, s*D, D, xlnb[:, s*D:(s+1)*D], sc, 8*s, sq, [t_x])
            toks_ln.append(t_ln)
        G["xt"] = toks_ln[:] + [("ac", k.cnt["ac"])]
        k.wait("tensor", t_id, *toks_ln)
        t_ev = None
        for c in range(8):
            k.wait("tensor", G["psT"], G["xlnF"] if c == 0 else None, t_ev)
            t_tr = None
            for s in range(4):
                t_tr = k.t(lambda e, c=c, s=s: e.transpose(
                    psT[:, s*128:(s+1)*128], xlnb[:, s*D + c*128: s*D + (c+1)*128], ident[:]))
            k.wait("vector", t_tr)
            t_ev = k.v(lambda e, c=c: e.tensor_copy(xlnF[:, c*512:(c+1)*512], psT[:]))
        G["psT"] = [t_ev]
        G["xlnb"] = [("pe", k.cnt["pe"])]
        t_xlnF = t_ev
        for which, wsb, twl, outdram, func in (("xres", wx, t_wx, "xres", AF.Identity),
                                              ("z", wz, t_wz, "sz", AF.Silu)):
            for dc in range(4):
                ps, gk = (psA, "psA") if dc % 2 == 0 else (psB, "psB")
                k.wait("tensor", twl, t_xlnF, G[gk])
                t_mm = None
                for c in range(8):
                    t_mm = k.t(lambda e, c=c, dc=dc, ps=ps, wsb=wsb: e.matmul(
                        ps[:], wsb[:, c*512 + dc*128: c*512 + (dc+1)*128],
                        xlnF[:, c*512:(c+1)*512], start=(c == 0), stop=(c == 7)))
                k.wait("scalar", t_mm, G["zb"])
                t_ev2 = k.s(lambda e, ps=ps, func=func: e.activation(out=zb[:, 0:512], in_=ps[:], func=func))
                G[gk] = [t_ev2]
                k.wait("sync", t_ev2)
                dst = ap2(IO[outdram], (dc*128)*T + tc*512, [[T, 128], [1, 512]])
                k.dsync(lambda e, dst=dst: e.dma_start(out=dst, in_=zb[:, 0:512]))
                G["zb"] = [("syd", k.cnt["syd"])]
        k.wait("tensor", t_wdtu, G["psA"])
        t_mm = None
        for c in range(8):
            t_mm = k.t(lambda e, c=c: e.matmul(
                psA[0:64, :], wdtu[:, c*64:(c+1)*64], xlnF[:, c*512:(c+1)*512],
                start=(c == 0), stop=(c == 7)))
        k.wait("vector", t_mm, G["dtu"])
        t_dtu = k.v(lambda e: e.tensor_copy(dtu[:], psA[0:64, :]))
        G["psA"] = [t_dtu]
        for dc in range(4):
            ps, gk = (psB, "psB") if dc % 2 == 0 else (psA, "psA")
            k.wait("tensor", t_wdt, t_dtu, G[gk])
            t_mm = k.t(lambda e, dc=dc, ps=ps: e.matmul(
                ps[:], wdt[0:64, dc*128:(dc+1)*128], dtu[:], start=True, stop=True))
            k.wait("scalar", t_mm, t_bdt, G["zb"])
            t_z0 = k.s(lambda e, dc=dc, ps=ps: e.activation(out=zb[:, 0:512], in_=ps[:],
                                                            func=AF.Identity, bias=col(bdt, dc)))
            G[gk] = [t_z0]
            k.wait("vector", t_z0, G["cv"])
            t_m0 = k.v(lambda e: e.tensor_scalar_max(zb2[:, 0:512], zb[:, 0:512], 0.0))
            k.wait("scalar", t_m0)
            t_ab = k.s(lambda e: e.activation(out=zb[:, 0:512], in_=zb[:, 0:512], func=AF.Abs))
            k.wait("scalar", t_ab)
            t_e0 = k.s(lambda e: e.activation(out=zb[:, 0:512], in_=zb[:, 0:512], func=AF.Exp, scale=-1.0))
            k.wait("scalar", t_e0, t_c1)
            t_l0 = k.s(lambda e: e.activation(out=zb[:, 0:512], in_=zb[:, 0:512], func=AF.Ln, bias=col(const1, 0)))
            k.wait("vector", t_l0, t_m0)
            t_dt = k.v(lambda e: e.tensor_tensor(zb[:, 0:512], zb2[:, 0:512], zb[:, 0:512], op=ALU.add))
            k.wait("sync", t_dt)
            dst = ap2(IO["dtd"], (dc*128)*T + tc*512, [[T, 128], [1, 512]])
            k.dsync(lambda e, dst=dst: e.dma_start(out=dst, in_=zb[:, 0:512]))
            G["zb"] = [("syd", k.cnt["syd"])]
            G["cv"] = [t_dt]
        G["dtu"] = [("pe", k.cnt["pe"])]
        for which, wsb, twl, outdram in (("bp", wbp, t_wbp, "bpd"), ("cp", wcp, t_wcp, "cpd")):
            for s in range(4):
                ps, gk = (psA, "psA") if s % 2 == 0 else (psB, "psB")
                k.wait("tensor", twl, G[gk])
                t_mm = None
                for c in range(8):
                    t_mm = k.t(lambda e, c=c, s=s, ps=ps, wsb=wsb: e.matmul(
                        ps[:, 0:16], xlnF[:, c*512 + s*128: c*512 + (s+1)*128],
                        wsb[:, c*16:(c+1)*16], start=(c == 0), stop=(c == 7)))
                k.wait("vector", t_mm, G["y"])
                t_ev3 = k.v(lambda e, ps=ps: e.tensor_copy(yb[:, 0:16], ps[:, 0:16]))
                G[gk] = [t_ev3]
                k.wait("sync", t_ev3)
                dst = ap2(IO[outdram], (tc*512 + s*128)*16, [[16, 128], [1, 16]])
                k.dsync(lambda e, dst=dst: e.dma_start(out=dst, in_=yb[:, 0:16]))
                G["y"] = [("syd", k.cnt["syd"])]
        G["xlnF"] = [("pe", k.cnt["pe"])]

    Tc = 256
    t_h0 = k.v(lambda e: e.memset(hst[:], 0.0))
    hs_tok = [t_h0]
    for tci in range(T // Tc):
        k.wait("sync", G["bpe"])
        t_bpe = k.dsync(lambda e, tci=tci: e.dma_start(
            out=bpe[:], in_=ap2(IO["bpd"], tci*Tc*16, [[0, 128], [1, Tc*16]])))
        k.wait("sync", G["cpe"])
        t_cpe = k.dsync(lambda e, tci=tci: e.dma_start(
            out=cpe[:], in_=ap2(IO["cpd"], tci*Tc*16, [[0, 128], [1, Tc*16]])))
        for dc in range(4):
            k.wait("sync", G["xrp"])
            if tci == 0:
                t_z3 = k.v(lambda e: e.memset(xrp[:, 0:3], 0.0))
                k.wait("sync", t_z3)
                t_xr = k.dsync(lambda e, dc=dc: e.dma_start(
                    out=xrp[:, 3:3+Tc], in_=ap2(IO["xres"], dc*128*T, [[T, 128], [1, Tc]])))
            else:
                t_xr = k.dsync(lambda e, dc=dc, tci=tci: e.dma_start(
                    out=xrp[:], in_=ap2(IO["xres"], dc*128*T + tci*Tc - 3, [[T, 128], [1, Tc+3]])))
            k.wait("vector", t_xr, t_cw, G["cv"], G["u"])
            t_c0 = k.v(lambda e, dc=dc: e.tensor_scalar(cv[:], xrp[:, 0:Tc], col(convw, dc*4+0), None, op0=ALU.mult))
            tprev = t_c0
            for tap in range(1, 4):
                k.wait("vector", tprev)
                tprev = k.v(lambda e, dc=dc, tap=tap: e.scalar_tensor_tensor(
                    out=cv[:], in0=xrp[:, tap:tap+Tc], scalar=col(convw, dc*4+tap),
                    in1=cv[:], op0=ALU.mult, op1=ALU.add))
            k.wait("scalar", tprev, G["u"])
            t_u = k.s(lambda e: e.activation(out=ub[:], in_=cv[:], func=AF.Silu))
            G["cv"] = [t_u]
            k.wait("sync", G["dt"])
            t_dt2 = k.dsync(lambda e, dc=dc, tci=tci: e.dma_start(
                out=dtt[:], in_=ap2(IO["dtd"], dc*128*T + tci*Tc, [[T, 128], [1, Tc]])))
            dt_b = ap2(dtt.ap(), 0, [[Tc, 128], [1, Tc], [0, 16]])
            aA_b = ap2(absA.ap(), dc*16, [[64, 128], [0, Tc], [1, 16]])
            rh_b = ap2(rhat.ap(), dc*16, [[64, 128], [0, Tc], [1, 16]])
            mm_b = ap2(mmask.ap(), dc*16, [[64, 128], [0, Tc], [1, 16]])
            u_b = ap2(ub.ap(), 0, [[Tc, 128], [1, Tc], [0, 16]])
            k.wait("vector", t_dt2, t_aa, G["lat"])
            t_lA = k.v(lambda e: e.tensor_tensor(latA[:], dt_b, aA_b, op=ALU.mult))
            k.wait("scalar", t_lA)
            t_lB = k.s(lambda e: e.activation(out=latB[:], in_=latA[:], func=AF.Exp, scale=-1.0))
            k.wait("gpsimd", t_dt2, t_rh, G["lat"])
            t_lG = k.g(lambda e: e.tensor_tensor(latG[:], mm_b, dt_b, op=ALU.mult))
            k.wait("vector", t_lB, t_rh, G["latC"])
            t_lC = k.v(lambda e: e.scalar_tensor_tensor(
                out=latC[:], in0=latB[:], scalar=-1.0, in1=rh_b, op0=ALU.add, op1=ALU.mult))
            k.wait("gpsimd", t_lC, t_lG)
            t_lG2 = k.g(lambda e: e.tensor_tensor(latG[:], latC[:], latG[:], op=ALU.add))
            G["latC"] = [t_lG2]
            k.wait("vector", t_lG2, t_u, G["latD"])
            t_lD = k.v(lambda e: e.tensor_tensor(latD[:], latG[:], u_b, op=ALU.mult))
            k.wait("vector", t_lD, t_bpe)
            t_X = k.v(lambda e: e.tensor_tensor(latG[:], latD[:], bpe[:], op=ALU.mult))
            k.wait("vector", t_X, *hs_tok)
            t_sc2 = None
            for n in range(16):
                a_v = ap2(latB.ap(), n, [[Tc*16, 128], [16, Tc]])
                x_v = ap2(latG.ap(), n, [[Tc*16, 128], [16, Tc]])
                o_v = ap2(latA.ap(), n, [[Tc*16, 128], [16, Tc]])
                t_sc2 = k.v(lambda e, n=n, a_v=a_v, x_v=x_v, o_v=o_v, dc=dc: e.tensor_tensor_scan(
                    out=o_v, data0=a_v, data1=x_v, initial=col(hst, dc*16+n),
                    op0=ALU.mult, op1=ALU.add))
            k.wait("vector", t_sc2)
            t_hs = k.v(lambda e, dc=dc: e.tensor_copy(hst[:, dc*16:(dc+1)*16], latA[:, (Tc-1)*16:Tc*16]))
            hs_tok = [t_hs]
            k.wait("vector", t_cpe)
            t_pr = k.v(lambda e: e.tensor_tensor(latD[:], latA[:], cpe[:], op=ALU.mult))
            k.wait("vector", t_pr, G["y"])
            in3 = ap2(latD.ap(), 0, [[Tc*16, 128], [16, Tc], [1, 16]])
            t_y = k.v(lambda e, in3=in3: e.tensor_reduce(yb[:, 0:Tc], in3, axis=AX.X, op=ALU.add))
            G["latD"] = [t_y]
            k.wait("vector", t_y, t_dp, t_u)
            t_yd = k.v(lambda e, dc=dc: e.scalar_tensor_tensor(
                out=yb[:, 0:Tc], in0=ub[:], scalar=col(Dpar, dc), in1=yb[:, 0:Tc],
                op0=ALU.mult, op1=ALU.add))
            G["u"] = [t_yd]
            k.wait("sync", G["sz"])
            t_sz = k.dsync(lambda e, dc=dc, tci=tci: e.dma_start(
                out=szt[:], in_=ap2(IO["sz"], dc*128*T + tci*Tc, [[T, 128], [1, Tc]])))
            k.wait("vector", t_yd, t_sz, G["gz"])
            t_g = k.v(lambda e: e.tensor_tensor(gz[:], yb[:, 0:Tc], szt[:], op=ALU.mult))
            G["sz"] = [t_g]; G["y"] = [t_g]; G["dt"] = [t_lA, t_lG]
            G["lat"] = [t_g, t_pr, t_sc2]
            G["xrp"] = [tprev]
            k.wait("sync", t_g)
            dst = ap2(IO["gh"], dc*128*T + tci*Tc, [[T, 128], [1, Tc]])
            k.dsync(lambda e, dst=dst: e.dma_start(out=dst, in_=gz[:]))
            G["gz"] = [("syd", k.cnt["syd"])]
        G["bpe"] = [("dv", k.cnt["dv"])]
        G["cpe"] = [("dv", k.cnt["dv"])]

PHASED_SBUF = [
    ("pd_x", [128, 4*1024], F32),
    ("pa_xlnb", [128, 4*1024], BF16),
    ("pa_xlnF", [128, 8*512], BF16),
    ("pa_sq", [128, 1024], F32),
    ("pa_sc", [128, 64], F32),
    ("ident", [128, 128], BF16),
    ("eps", [128, 1], F32),
    ("const1", [128, 1], F32),
    ("pd_wx", [128, 8*512], BF16),
    ("pd_wz", [128, 8*512], BF16),
    ("pd_wdtu", [128, 8*64], BF16),
    ("pd_wdt", [64, 512], BF16),
    ("pd_wbp", [128, 8*16], BF16),
    ("pd_wcp", [128, 8*16], BF16),
    ("pd_dtu", [64, 512], BF16),
    ("pd_zb", [128, 512], F32),
    ("pd_zb2", [128, 512], F32),
    ("pd_bdt", [128, 4], F32),
    ("pd_convw", [128, 16], F32),
    ("pd_dpar", [128, 4], F32),
    ("pd_alog", [128, 64], F32),
    ("pd_absA", [128, 64], F32),
    ("pd_rhat", [128, 64], F32),
    ("pd_m", [128, 64], F32),
    ("pd_latA", [128, 4096], F32),
    ("pd_latB", [128, 4096], F32),
    ("pd_latC", [128, 4096], F32),
    ("pd_latD", [128, 4096], F32),
    ("pd_latG", [128, 4096], F32),
    ("pd_bpe", [128, 4096], F32),
    ("pd_cpe", [128, 4096], F32),
    ("pd_h", [128, 64], F32),
    ("pd_xrp", [128, 259], F32),
    ("pd_cv", [128, 256], F32),
    ("pd_u", [128, 256], F32),
    ("pd_dt", [128, 256], F32),
    ("pd_sz", [128, 256], F32),
    ("pd_y", [128, 256], F32),
    ("pd_gz", [128, 256], BF16),
]
PHASED_PSUM = [
    ("ps_a", [128, 512], F32),
    ("ps_b", [128, 512], F32),
    ("ps_bfA", [128, 512], BF16),
]

def build_phaseD(dbg=False):
    io = [
        ("x2f", [T, D], F32, "ExternalInput"),
        ("WxTh", [D, 512], F32, "ExternalInput"),
        ("WzTh", [D, 512], F32, "ExternalInput"),
        ("WdtuT", [D, 64], F32, "ExternalInput"),
        ("WdtTh", [64, 512], F32, "ExternalInput"),
        ("WpBT", [D, 16], F32, "ExternalInput"),
        ("WpCT", [D, 16], F32, "ExternalInput"),
        ("bdt_h", [128, 4], F32, "ExternalInput"),
        ("convw_h", [128, 16], F32, "ExternalInput"),
        ("D_h", [128, 4], F32, "ExternalInput"),
        ("Alog_h", [4, 128, 16], F32, "ExternalInput"),
        ("identw", [128, 128], BF16, "ExternalInput"),
        ("xres", [512, T], F32, "ExternalOutput" if dbg else "Internal"),
        ("sz", [512, T], F32, "ExternalOutput" if dbg else "Internal"),
        ("dtd", [512, T], F32, "ExternalOutput" if dbg else "Internal"),
        ("bpd", [T*16], F32, "ExternalOutput" if dbg else "Internal"),
        ("cpd", [T*16], F32, "ExternalOutput" if dbg else "Internal"),
        ("gh", [512, T], BF16, "ExternalOutput"),
    ]
    return build_program([emit_phaseD], io, PHASED_SBUF, PHASED_PSUM)

def host_phaseD_weights(z, h):
    dh = slice(h*512, (h+1)*512)
    Wp = z['L1_attn_W_params']
    out = {
        "WxTh": np.ascontiguousarray(z['L1_attn_W_x'][dh].T),
        "WzTh": np.ascontiguousarray(z['L1_attn_W_z'][dh].T),
        "WdtuT": np.ascontiguousarray(Wp[:64].T),
        "WdtTh": np.ascontiguousarray(z['L1_attn_W_dt'][dh].T),
        "WpBT": np.ascontiguousarray(Wp[64:80].T),
        "WpCT": np.ascontiguousarray(Wp[80:96].T),
        "bdt_h": np.ascontiguousarray(z['L1_attn_b_dt'][dh].reshape(4, 128).T),
        "convw_h": np.ascontiguousarray(
            z['L1_attn_conv_w'][dh, 0, :].reshape(4, 128, 4).transpose(1, 0, 2).reshape(128, 16)),
        "D_h": np.ascontiguousarray(z['L1_attn_D_param'][dh].reshape(4, 128).T),
        "Alog_h": np.ascontiguousarray(z['L1_attn_A_log'][dh].reshape(4, 128, 16)),
    }
    return out

def emit_phaseE(k, TS, IO):
    gch = TS["pe_g"]
    wo = TS["pe_wo"]
    sq, sc = TS["pa_sq"], TS["pa_sc"]
    at = TS["pe_at"]
    xrow = TS["pb_xrow"]
    psA, psB = TS["ps_a"], TS["ps_b"]
    t_wo = load_wT(k, IO["WoutT"], wo, D, D)
    G = {"g": [], "at": [], "xrow": [], "psA": [], "psB": []}
    for tc in range(8):
        src = ap2(IO["gf"], tc*512, [[QT, 128], [128*QT, 8], [1, 512]])
        k.wait("sync", G["g"])
        t_g = k.dsync(lambda e, src=src: e.dma_start(out=gch[:], in_=src))
        for s in range(4):
            t_evs = []
            for half in range(2):
                ps, gk = (psA, "psA") if half == 0 else (psB, "psB")
                k.wait("tensor", t_g, t_wo, G[gk])
                t_mm = None
                for c in range(8):
                    t_mm = k.t(lambda e, c=c, s=s, half=half, ps=ps: e.matmul(
                        ps[:], gch[:, c*512 + s*128: c*512 + (s+1)*128],
                        wo[:, c*1024 + half*512: c*1024 + (half+1)*512],
                        start=(c == 0), stop=(c == 7)))
                k.wait("scalar", t_mm, G["at"])
                t_ev = k.s(lambda e, half=half, ps=ps: e.activation(
                    out=at[:, half*512:(half+1)*512], in_=ps[:], func=AF.Identity, scale=1.0))
                G[gk] = [t_ev]
                t_evs.append(t_ev)
            t_ln = ln_tile(k, TS, at, 0, D, sq[:, :D], sc, 8, TS["pb_sq2"], t_evs)
            src2 = ap2(IO["x2rows"], (tc*512 + s*128)*D, [[D, 128], [1, D]])
            k.wait("sync", G["xrow"])
            t_xr = k.dsync(lambda e, src2=src2: e.dma_start(out=xrow[:], in_=src2))
            k.wait("vector", t_ln, t_xr)
            t_x3 = k.v(lambda e: e.tensor_tensor(at[:, 0:D], sq[:, :D], xrow[:], op=ALU.add))
            G["xrow"] = [t_x3]
            k.wait("sync", t_x3)
            dst = ap2(IO["x3"], (tc*512 + s*128)*D, [[D, 128], [1, D]])
            k.dsync(lambda e, dst=dst: e.dma_start(out=dst, in_=at[:, 0:D]))
            G["at"] = [("syd", k.cnt["syd"])]
        G["g"] = [("pe", k.cnt["pe"])]

PHASEE_SBUF = [
    ("pe_g", [128, 8*512], BF16),
    ("pe_wo", [128, 8*1024], BF16),
    ("pe_at", [128, 1024], F32),
    ("pb_xrow", [128, 1024], F32),
    ("pa_sq", [128, 1024], F32),
    ("pb_sq2", [128, 1024], F32),
    ("pa_sc", [128, 64], F32),
    ("eps", [128, 1], F32),
]
PHASEE_PSUM = [
    ("ps_a", [128, 512], F32),
    ("ps_b", [128, 512], F32),
]

def build_phaseE():
    io = [
        ("gf", [D, QT], BF16, "ExternalInput"),
        ("WoutT", [D, D], F32, "ExternalInput"),
        ("x2rows", [QT, D], F32, "ExternalInput"),
        ("x3", [QT, D], F32, "ExternalOutput"),
    ]
    return build_program([emit_phaseE], io, PHASEE_SBUF, PHASEE_PSUM)

def emit_phaseF(k, TS, IO):
    xt = TS["pf_x"]; xlnb = TS["pa_xlnb"]; xlnF = TS["pa_xlnF"]
    sq, sc = TS["pa_sq"], TS["pa_sc"]
    ident = TS["ident"]
    psT, psA, psB = TS["ps_bfA"], TS["ps_a"], TS["ps_b"]
    wsm = [TS["pf_wsm0"], TS["pf_wsm1"]]
    wmed = [TS["pf_wm0"], TS["pf_wm0"]]
    pF = [TS["pf_p1"], TS["pf_p2"], TS["pf_p3"]]
    s1 = TS["pf_s1"]; s2 = TS["pf_s2"]; sh1 = TS["pf_sh1"]; sh2 = TS["pf_sh2"]
    s3e = TS["pf_s3e"]
    yt = TS["pf_y"]

    t_id = k.dgp(lambda e: e.dma_start(out=ident[:], in_=IO["identw"][:, :]))
    t_s = []
    for name, sb_t in (("s1", s1), ("s2", s2), ("sh1", sh1), ("sh2", sh2)):
        t_s.append(k.dsync(lambda e, sb_t=sb_t, name=name: e.dma_start(
            out=sb_t[:], in_=ap2(IO[name], 0, [[32, 128], [1, 32]]))))
    t_s3 = k.dsync(lambda e: e.dma_start(out=s3e[:], in_=ap2(IO["s3"], 0, [[0, 128], [1, 1024]])))

    G = {"xt": [], "xlnb": [], "xlnF": [], "psT": [], "psA": [], "psB": [],
         "wmed": [[]], "wsm": [[], []], "p1": [], "p2": [], "p3": [], "y": []}

    for tc in range(8):
        src = ap2(IO["x3"], tc*512*D, [[D, 128], [128*D, 4], [1, D]])
        k.wait("sync", G["xt"])
        t_x = k.dsync(lambda e, src=src: e.dma_start(out=xt[:], in_=src))
        toks_ln = []
        for s in range(4):
            k.wait("vector", G["xlnb"])
            t_ln = ln_tile(k, TS, xt, s*D, D, xlnb[:, s*D:(s+1)*D], sc, 8*s, sq, [t_x])
            toks_ln.append(t_ln)
        G["xt"] = toks_ln[:] + [("ac", k.cnt["ac"])]
        k.wait("tensor", t_id, *toks_ln)
        t_ev = None
        for c in range(8):
            k.wait("tensor", G["psT"], G["xlnF"] if c == 0 else None, t_ev)
            t_tr = None
            for s in range(4):
                t_tr = k.t(lambda e, c=c, s=s: e.transpose(
                    psT[:, s*128:(s+1)*128], xlnb[:, s*D + c*128: s*D + (c+1)*128], ident[:]))
            k.wait("vector", t_tr)
            t_ev = k.v(lambda e, c=c: e.tensor_copy(xlnF[:, c*512:(c+1)*512], psT[:]))
        G["psT"] = [t_ev]
        G["xlnb"] = [("pe", k.cnt["pe"])]
        t_xlnF = t_ev

        def stage_big(dst, dstkey, srcsb, src_nch, wio, scale_sb, func, t_src, extra_guard=None):
            toks = []
            tw_pipe = {}
            def wld(fo):
                slot = fo % 2
                k.wait("gpsimd", G["wsm"][slot])
                src2 = ap2(IO[wio], fo*128, [[4096, 128], [128*4096, src_nch], [1, 128]])
                return (k.dgp(lambda e, src2=src2, slot=slot: e.dma_start(
                    out=wsm[slot][:, 0:src_nch*128], in_=src2)), slot)
            tw_pipe[0] = wld(0)
            for fo in range(32):
                if fo + 1 < 32:
                    tw_pipe[fo+1] = wld(fo+1)
                t_w, slot = tw_pipe.pop(fo)
                ps, gk = (psA, "psA") if fo % 2 == 0 else (psB, "psB")
                k.wait("tensor", t_w, t_src, G[gk], extra_guard if fo == 0 else None)
                t_mm = None
                for c in range(src_nch):
                    t_mm = k.t(lambda e, c=c, ps=ps, slot=slot, srcsb=srcsb: e.matmul(
                        ps[:], wsm[slot][:, c*128:(c+1)*128],
                        srcsb[:, c*512:(c+1)*512], start=(c == 0), stop=(c == src_nch-1)))
                G["wsm"][slot] = [("pe", k.cnt["pe"])]
                k.wait("scalar", t_mm, G[dstkey] if fo == 0 else None)
                t_e = k.s(lambda e, fo=fo, ps=ps, dst=dst, scale_sb=scale_sb, func=func: e.activation(
                    out=dst[:, fo*512:(fo+1)*512], in_=ps[:], func=func, scale=col(scale_sb, fo)))
                G[gk] = [t_e]
                toks.append(t_e)
            return toks
        k.wait("scalar", *t_s)
        tk1 = stage_big(pF[0], "p1", xlnF, 8, "w1T", s1, AF.Silu, t_xlnF)
        tk2 = stage_big(pF[1], "p2", xlnF, 8, "w2T", s2, AF.Identity, t_xlnF)
        k.wait("vector", tk1[-1], tk2[-1])
        k.wait("vector", G["p3"])
        t_h = k.v(lambda e: e.tensor_tensor(pF[2][:], pF[0][:], pF[1][:], op=ALU.mult))
        G["p1"] = [t_h]; G["p2"] = [t_h]
        tg1 = stage_big(pF[0], "p1", pF[2], 32, "h1T", sh1, AF.Gelu, t_h, [t_h])
        tg2 = stage_big(pF[1], "p2", pF[0], 32, "h2T", sh2, AF.Gelu, tg1[-1], tg1)
        G["p3"] = [("pe", k.cnt["pe"])]
        k.wait("tensor", tg2[-1])
        yev = {}
        for half in range(2):
            k.wait("gpsimd", G["wmed"][0])
            src3 = ap2(IO["w3T"], half*512, [[1024, 128], [128*1024, 32], [1, 512]])
            t_w3h = k.dgp(lambda e, src3=src3: e.dma_start(out=wmed[0][:], in_=src3))
            for s in range(4):
                ps, gk = (psA, "psA") if s % 2 == 0 else (psB, "psB")
                k.wait("tensor", t_w3h, G[gk])
                t_mm = None
                for c in range(32):
                    t_mm = k.t(lambda e, c=c, s=s, ps=ps: e.matmul(
                        ps[:], pF[1][:, c*512 + s*128: c*512 + (s+1)*128],
                        wmed[0][:, c*512:(c+1)*512],
                        start=(c == 0), stop=(c == 31)))
                k.wait("vector", t_mm, G["y"], t_s3, yev.get(("g", s)))
                t_ev2 = k.v(lambda e, half=half, s=s, ps=ps: e.scalar_tensor_tensor(
                    out=yt[:, s*1024 + half*512: s*1024 + (half+1)*512], in0=ps[:],
                    scalar=1.0, in1=s3e_b(TS, half), op0=ALU.bypass, op1=ALU.mult))
                G[gk] = [t_ev2]
                yev[(half, s)] = t_ev2
            G["wmed"] = [[("pe", k.cnt["pe"])]]
        for s in range(4):
            t_evs = [yev[(0, s)], yev[(1, s)]]
            t_ln = ln_tile(k, TS, yt, s*1024, D, sq[:, :D], sc, 8, TS["pb_sq2"], t_evs)
            k.wait("vector", t_ln)
            t_x4 = k.v(lambda e, s=s: e.tensor_tensor(yt[:, s*1024:(s+1)*1024], sq[:, :D], xt[:, s*1024:(s+1)*1024], op=ALU.add))
            yev[("g", s)] = t_x4
            k.wait("sync", t_x4)
            dst = ap2(IO["x4"], (tc*512 + s*128)*D, [[D, 128], [1, D]])
            k.dsync(lambda e, dst=dst, s=s: e.dma_start(out=dst, in_=yt[:, s*1024:(s+1)*1024]))
        G["y"] = [("syd", k.cnt["syd"])]
        G["xt"] = G["xt"] + [("dv", k.cnt["dv"])]
        G["p2"] = [("pe", k.cnt["pe"])]
        G["wmed"] = [[("pe", k.cnt["pe"])], [("pe", k.cnt["pe"])]]
        G["xlnF"] = [("pe", k.cnt["pe"])]

PHASEF_SBUF = [
    ("pf_x", [128, 4*1024], F32),
    ("pa_xlnb", [128, 4*1024], BF16),
    ("pa_xlnF", [128, 8*512], BF16),
    ("pa_sq", [128, 1024], F32),
    ("pb_sq2", [128, 1024], F32),
    ("pa_sc", [128, 64], F32),
    ("ident", [128, 128], BF16),
    ("eps", [128, 1], F32),
    ("pf_wm0", [128, 32*512], BF16),
    ("pf_wsm0", [128, 32*128], BF16),
    ("pf_wsm1", [128, 32*128], BF16),
    ("pf_p1", [128, 32*512], BF16),
    ("pf_p2", [128, 32*512], BF16),
    ("pf_p3", [128, 32*512], BF16),
    ("pf_s1", [128, 32], F32),
    ("pf_s2", [128, 32], F32),
    ("pf_sh1", [128, 32], F32),
    ("pf_sh2", [128, 32], F32),
    ("pf_s3e", [128, 1024], F32),
    ("pf_y", [128, 4*1024], F32),
]
PHASEF_PSUM = [
    ("ps_a", [128, 512], F32),
    ("ps_b", [128, 512], F32),
    ("ps_bfA", [128, 512], BF16),
]

def s3e_b(TS, half):
    return TS["pf_s3e"][:, half*512:(half+1)*512]

def build_phaseF():
    io = [
        ("x3", [QT, D], F32, "ExternalInput"),
        ("w1T", [D, 4096], F32, "ExternalInput"),
        ("w2T", [D, 4096], F32, "ExternalInput"),
        ("h1T", [4096, 4096], F32, "ExternalInput"),
        ("h2T", [4096, 4096], F32, "ExternalInput"),
        ("w3T", [4096, D], F32, "ExternalInput"),
        ("s1", [128, 32], F32, "ExternalInput"),
        ("s2", [128, 32], F32, "ExternalInput"),
        ("sh1", [128, 32], F32, "ExternalInput"),
        ("sh2", [128, 32], F32, "ExternalInput"),
        ("s3", [1, 1024], F32, "ExternalInput"),
        ("identw", [128, 128], BF16, "ExternalInput"),
        ("x4", [QT, D], F32, "ExternalOutput"),
    ]
    return build_program([emit_phaseF], io, PHASEF_SBUF, PHASEF_PSUM)

def host_phaseF_weights(z):
    def sc128(v):
        return np.ascontiguousarray(v.reshape(32, 128).T).astype(np.float32)
    return {
        "w1T": np.ascontiguousarray(z['L1_ffn_w1_w'].T),
        "w2T": np.ascontiguousarray(z['L1_ffn_w2_w'].T),
        "h1T": np.ascontiguousarray(z['L1_ffn_h1_w'].T),
        "h2T": np.ascontiguousarray(z['L1_ffn_h2_w'].T),
        "w3T": np.ascontiguousarray(z['L1_ffn_w3_w'].T),
        "s1": sc128(z['L1_ffn_w1_s']), "s2": sc128(z['L1_ffn_w2_s']),
        "sh1": sc128(z['L1_ffn_h1_s']), "sh2": sc128(z['L1_ffn_h2_s']),
        "s3": z['L1_ffn_w3_s'].reshape(1, 1024).astype(np.float32),
    }


_BUILD_CACHE = {}

def _get(b, f):
    if b not in _BUILD_CACHE:
        _BUILD_CACHE[b] = f()
    return _BUILD_CACHE[b]

def kernel(**inputs):
    import ml_dtypes
    bf = ml_dtypes.bfloat16
    z = inputs
    x0 = np.asarray(z['L0_attn_x'])
    ident = np.eye(128, dtype=np.float32).astype(bf)
    cores = list(range(8))
    rows_h = {h: np.array([r for j in core_blocks(h) for r in range(j*BS, (j+1)*BS)]) for h in (0, 1)}
    W = z['L0_attn_W']
    WkT = np.ascontiguousarray(W[:1024].T).astype(np.float32)
    WvT = np.ascontiguousarray(W[1024:].T).astype(np.float32)

    ncA = _get('A', build_phaseA)
    inm = []
    for c in cores:
        b, h = c // 2, c % 2
        inm.append({"x_full": np.ascontiguousarray(x0[b]),
                    "xq": np.ascontiguousarray(x0[b][rows_h[h]]),
                    "WkT": WkT, "WvT": WvT, "identw": ident})
    resA = run_bass_kernel_spmd(ncA, inm, core_ids=cores)
    rA = resA.results

    ncB = _get('B', build_phaseB)
    masks = {h: host_masks(h) for h in (0, 1)}
    inm = []
    for c in cores:
        b, h = c // 2, c % 2
        inm.append({"xq": np.ascontiguousarray(x0[b][rows_h[h]]),
                    "xlnqF": np.asarray(rA[c]["xlnqF"]),
                    "xlnq_tok": np.asarray(rA[c]["xlnq_tok"]),
                    "K": np.asarray(rA[c]["K"]), "V": np.asarray(rA[c]["V"]),
                    "masks": masks[h], "identw": ident})
    resB = run_bass_kernel_spmd(ncB, inm, core_ids=cores)
    rB = resB.results

    ncC = _get('C', build_phaseC)
    moe = host_moe_weights(z)
    inm = []
    for c in cores:
        d = dict(moe)
        d["x1"] = np.asarray(rB[c]["x1"])
        d["identw"] = ident
        inm.append(d)
    resC = run_bass_kernel_spmd(ncC, inm, core_ids=cores)
    rC = resC.results

    x2full = {}
    for b in range(4):
        xf = np.empty((T, D), np.float32)
        xf[rows_h[0]] = np.asarray(rC[2*b]["x2"])
        xf[rows_h[1]] = np.asarray(rC[2*b+1]["x2"])
        x2full[b] = xf

    ncD = _get('D', build_phaseD)
    wD = {h: host_phaseD_weights(z, h) for h in (0, 1)}
    inm = []
    for c in cores:
        b, h = c // 2, c % 2
        d = dict(wD[h])
        d["x2f"] = x2full[b]
        d["identw"] = ident
        inm.append(d)
    resD = run_bass_kernel_spmd(ncD, inm, core_ids=cores)
    rD = resD.results

    ncE = _get('E', build_phaseE)
    WoutT = np.ascontiguousarray(z['L1_attn_W_out'].T).astype(np.float32)
    inm = []
    for c in cores:
        b, h = c // 2, c % 2
        gf = np.concatenate([np.asarray(rD[2*b]["gh"]), np.asarray(rD[2*b+1]["gh"])], 0)
        gf = gf[:, h*4096:(h+1)*4096]
        inm.append({"gf": np.ascontiguousarray(gf), "WoutT": WoutT,
                    "x2rows": np.ascontiguousarray(x2full[b][h*4096:(h+1)*4096])})
    resE = run_bass_kernel_spmd(ncE, inm, core_ids=cores)
    rE = resE.results

    ncF = _get('F', build_phaseF)
    wF = host_phaseF_weights(z)
    inm = []
    for c in cores:
        d = dict(wF)
        d["x3"] = np.asarray(rE[c]["x3"])
        d["identw"] = ident
        inm.append(d)
    resF = run_bass_kernel_spmd(ncF, inm, core_ids=cores)
    rF = resF.results

    out = np.empty((4, T, D), np.float32)
    for c in cores:
        b, h = c // 2, c % 2
        out[b, h*4096:(h+1)*4096] = np.asarray(rF[c]["x4"])
    return out


# version 03
# speedup vs baseline: 1.1210x; 1.1210x over previous
import numpy as np
import concourse.bass as bass
import concourse.mybir as mybir
from concourse.bass_utils import run_bass_kernel_spmd

F32 = mybir.dt.float32
BF16 = mybir.dt.bfloat16
AF = mybir.ActivationFunctionType
ALU = mybir.AluOpType
AX = mybir.AxisListType

D = 1024
T = 8192
NB = 32
BS = 256
QT = 4096
EPS = 1e-5

def core_blocks(h):
    return sorted([j for j in range(NB) if (min(j, NB - 1 - j) % 2) == h])

ENGS = ("sync", "tensor", "vector", "scalar", "gpsimd")

class Ctx:
    def __init__(self, nc, sems, active, engine):
        self.nc = nc
        self.sems = sems
        self.active = active
        self.engine = engine
        self.cnt = {k: 0 for k in sems}
        self.waited = {e: {k: 0 for k in sems} for e in ENGS}

    def _do(self, eng, sem, inc, f):
        self.cnt[sem] += inc
        if self.active == eng:
            inst = f(self.engine)
            inst.then_inc(self.sems[sem], inc)
        return (sem, self.cnt[sem])

    def t(self, f): return self._do("tensor", "pe", 1, f)
    def v(self, f): return self._do("vector", "dv", 1, f)
    def s(self, f): return self._do("scalar", "ac", 1, f)
    def g(self, f): return self._do("gpsimd", "gc", 1, f)
    def dsync(self, f): return self._do("sync", "syd", 16, f)
    def dgp(self, f): return self._do("gpsimd", "gpd", 16, f)

    def wait(self, eng, *toks):
        need = {}
        for tok in toks:
            if tok is None: continue
            if isinstance(tok, list):
                for t2 in tok:
                    if t2 is None: continue
                    s, v = t2
                    need[s] = max(need.get(s, 0), v)
            else:
                s, v = tok
                need[s] = max(need.get(s, 0), v)
        for s, v in sorted(need.items()):
            if self.waited[eng][s] >= v:
                continue
            self.waited[eng][s] = v
            if self.active == eng:
                self.engine.wait_ge(self.sems[s], v)

def ap2(apx, offset, dims):
    return bass.AP(apx.tensor, apx.offset + offset, dims)

def col(t, c0, n=1):
    return t[:, c0:c0+n]

def bcast_free(t, c0, count):
    a = t.ap()
    return bass.AP(a.tensor, a.offset + c0, [list(a.ap[0]), [0, count]])

def ln_tile(k, TS, xt, xoff, width, out_ap, sc, scoff, sq, deps):
    xs = xt[:, xoff:xoff+width]
    k.wait("vector", *deps)
    t_sum = k.v(lambda e: e.reduce_sum(col(sc, scoff+0), xs, axis=AX.X))
    k.wait("scalar", *deps)
    t_sq = k.s(lambda e: e.activation(out=sq[:, :width], in_=xs, func=AF.Square,
                                      accum_out=col(sc, scoff+1)))
    k.wait("vector", t_sum)
    t_mean = k.v(lambda e: e.tensor_scalar_mul(col(sc, scoff+2), col(sc, scoff+0), 1.0/width))
    k.wait("vector", t_sq, t_mean)
    t_m2 = k.v(lambda e: e.tensor_tensor(col(sc, scoff+3), col(sc, scoff+2), col(sc, scoff+2), op=ALU.mult))
    t_e2 = k.v(lambda e: e.tensor_scalar_mul(col(sc, scoff+4), col(sc, scoff+1), 1.0/width))
    k.wait("vector", t_m2, t_e2)
    t_var = k.v(lambda e: e.tensor_tensor(col(sc, scoff+5), col(sc, scoff+4), col(sc, scoff+3), op=ALU.subtract))
    k.wait("scalar", t_var)
    t_std = k.s(lambda e: e.activation(out=col(sc, scoff+6), in_=col(sc, scoff+5), func=AF.Sqrt,
                                       bias=col(TS["eps"], 0)))
    k.wait("vector", t_std)
    t_rstd = k.v(lambda e: e.reciprocal(col(sc, scoff+7), col(sc, scoff+6)))
    t_nm = k.v(lambda e: e.tensor_scalar_mul(col(sc, scoff+2), col(sc, scoff+2), -1.0))
    k.wait("vector", t_rstd, t_nm)
    t_out = k.v(lambda e: e.scalar_tensor_tensor(
        out=out_ap, in0=xs, scalar=col(sc, scoff+2), in1=bcast_free(sc, scoff+7, width),
        op0=ALU.add, op1=ALU.mult))
    return t_out

def load_wT(k, dram_ap, sb, kdim, mdim, sb_off=0):
    if kdim < 128:
        src = ap2(dram_ap, 0, [[mdim, kdim], [1, mdim]])
        return k.dgp(lambda e: e.dma_start(out=sb[0:kdim, sb_off:sb_off+mdim], in_=src))
    nch = kdim // 128
    src = ap2(dram_ap, 0, [[mdim, 128], [128*mdim, nch], [1, mdim]])
    return k.dgp(lambda e: e.dma_start(out=sb[:, sb_off:sb_off+nch*mdim], in_=src))

def emit_phaseA(k, TS, IO):
    xt, xlnb, xln32 = TS["pa_x"], TS["pa_xlnb"], TS["pa_xln32"]
    xlnF, sq, sc = TS["pa_xlnF"], TS["pa_sq"], TS["pa_sc"]
    wk, wv, ident = TS["pa_wk"], TS["pa_wv"], TS["ident"]
    psT, psA, psB = TS["ps_bfA"], TS["ps_a"], TS["ps_b"]
    kc, vc = TS["pa_kc"], TS["pa_vc"]

    t_wk = load_wT(k, IO["WkT"], wk, D, D)
    t_wv = load_wT(k, IO["WvT"], wv, D, D)
    t_id = k.dgp(lambda e: e.dma_start(out=ident[:], in_=IO["identw"][:, :]))

    G = {"xt": [], "xlnb": [], "xlnF": [], "kc": [], "vc": [], "psT": [],
         "psA": [], "psB": []}

    def do_chunk(src_dram, tc, qpath):
        src = ap2(src_dram, tc*512*D, [[D, 128], [128*D, 4], [1, D]])
        k.wait("sync", G["xt"])
        t_x = k.dsync(lambda e: e.dma_start(out=xt[:], in_=src))
        toks_ln = []
        for s in range(4):
            out_ap = xln32[:, s*D:(s+1)*D] if qpath else xlnb[:, s*D:(s+1)*D]
            k.wait("vector", G["xlnb"])
            t_ln = ln_tile(k, TS, xt, s*D, D, out_ap, sc, 8*s, sq, [t_x])
            if qpath:
                k.wait("sync", t_ln)
                dst = ap2(IO["xlnq_tok"], (tc*512 + s*128)*D, [[D, 128], [1, D]])
                k.dsync(lambda e, s=s: e.dma_start(out=dst, in_=xln32[:, s*D:(s+1)*D]))
                k.wait("vector", t_ln)
                t_ln = k.v(lambda e, s=s: e.tensor_copy(xlnb[:, s*D:(s+1)*D], xln32[:, s*D:(s+1)*D]))
            toks_ln.append(t_ln)
        G["xt"] = toks_ln[:]
        G["xt"].append(("ac", k.cnt["ac"]))
        k.wait("tensor", t_id, *toks_ln)
        t_ev = None
        for c in range(8):
            k.wait("tensor", G["psT"], G["xlnF"] if c == 0 else None, t_ev)
            t_tr = None
            for s in range(4):
                t_tr = k.t(lambda e, c=c, s=s: e.transpose(
                    psT[:, s*128:(s+1)*128], xlnb[:, s*D + c*128: s*D + (c+1)*128], ident[:]))
            k.wait("vector", t_tr)
            t_ev = k.v(lambda e, c=c: e.tensor_copy(xlnF[:, c*512:(c+1)*512], psT[:]))
        G["psT"] = [t_ev]
        G["xlnb"] = [("pe", k.cnt["pe"])]
        t_xlnF = t_ev
        if qpath:
            k.wait("sync", t_xlnF)
            toks = []
            for c in range(8):
                dst = ap2(IO["xlnqF"], (c*128)*QT + tc*512, [[QT, 128], [1, 512]])
                toks.append(k.dsync(lambda e, c=c, dst=dst: e.dma_start(out=dst, in_=xlnF[:, c*512:(c+1)*512])))
            G["xlnF"] = toks
            return
        k.wait("tensor", t_wk, t_xlnF)
        for dk in range(8):
            ps, gk = (psA, "psA") if dk % 2 == 0 else (psB, "psB")
            k.wait("tensor", G[gk])
            t_mm = None
            for c in range(8):
                t_mm = k.t(lambda e, dk=dk, c=c, ps=ps: e.matmul(
                    ps[:], wk[:, (c*8+dk)*128:(c*8+dk+1)*128], xlnF[:, c*512:(c+1)*512],
                    start=(c == 0), stop=(c == 7)))
            k.wait("scalar", t_mm, G["kc"])
            t_ev2 = k.s(lambda e, dk=dk, ps=ps: e.activation(out=kc[:, dk*512:(dk+1)*512], in_=ps[:], func=AF.Copy))
            G[gk] = [t_ev2]
            k.wait("sync", t_ev2)
            dst = ap2(IO["K"], (dk*128)*T + tc*512, [[T, 128], [1, 512]])
            k.dsync(lambda e, dk=dk, dst=dst: e.dma_start(out=dst, in_=kc[:, dk*512:(dk+1)*512]))
        G["kc"] = [("syd", k.cnt["syd"])]
        k.wait("tensor", t_wv)
        for s in range(4):
            t_evs = []
            for half in range(2):
                ps, gk = (psA, "psA") if half == 0 else (psB, "psB")
                k.wait("tensor", G[gk])
                t_mm = None
                for c in range(8):
                    t_mm = k.t(lambda e, s=s, half=half, c=c, ps=ps: e.matmul(
                        ps[:], xlnF[:, c*512 + s*128: c*512 + (s+1)*128],
                        wv[:, c*1024 + half*512: c*1024 + (half+1)*512],
                        start=(c == 0), stop=(c == 7)))
                k.wait("scalar", t_mm, G["vc"])
                t_ev3 = k.s(lambda e, s=s, half=half, ps=ps: e.activation(
                    out=vc[:, s*1024 + half*512: s*1024 + (half+1)*512], in_=ps[:], func=AF.Copy))
                G[gk] = [t_ev3]
                t_evs.append(t_ev3)
            k.wait("sync", *t_evs)
            dst = ap2(IO["V"], (tc*512 + s*128)*D, [[D, 128], [1, D]])
            k.dsync(lambda e, s=s, dst=dst: e.dma_start(out=dst, in_=vc[:, s*1024:(s+1)*1024]))
        G["vc"] = [("syd", k.cnt["syd"])]
        G["xlnF"] = [("pe", k.cnt["pe"])]

    for tc in range(16):
        do_chunk(IO["x_full"], tc, qpath=False)
    for tc in range(8):
        do_chunk(IO["xq"], tc, qpath=True)

PHASEA_SBUF = [
    ("pa_x", [128, 4*1024], F32),
    ("pa_xlnb", [128, 4*1024], BF16),
    ("pa_xln32", [128, 4*1024], F32),
    ("pa_xlnF", [128, 8*512], BF16),
    ("pa_sq", [128, 1024], F32),
    ("pa_sc", [128, 64], F32),
    ("pa_wk", [128, 8*1024], BF16),
    ("pa_wv", [128, 8*1024], BF16),
    ("ident", [128, 128], BF16),
    ("eps", [128, 1], F32),
    ("pa_kc", [128, 8*512], BF16),
    ("pa_vc", [128, 4*1024], BF16),
]
PHASEA_PSUM = [
    ("ps_bfA", [128, 512], BF16),
    ("ps_a", [128, 512], F32),
    ("ps_b", [128, 512], F32),
]

def build_program(phases, io_decl, sbuf_decl, psum_decl):
    nc = bass.Bass("TRN2", target_bir_lowering=False, debug=False)
    IO = {}
    for name, shape, dt, kind in io_decl:
        IO[name] = nc.dram_tensor(name, shape, dt, kind=kind).ap()
    import contextlib
    stack = contextlib.ExitStack()
    TS = {}
    for name, shape, dt in sbuf_decl:
        TS[name] = stack.enter_context(nc.sbuf_tensor(name, shape, dt))
    for name, shape, dt in psum_decl:
        TS[name] = stack.enter_context(nc.psum_tensor(name, shape, dt))
    sems = {}
    for s in ("syd", "gpd", "pe", "dv", "ac", "gc"):
        sems[s] = stack.enter_context(nc.semaphore(s))
    with stack:
        with nc.Block() as block:
            def mk(eng_name):
                def run(engine):
                    k = Ctx(nc, sems, eng_name, engine)
                    t_eps = k.v(lambda e: e.memset(TS["eps"][:], EPS))
                    k.wait("scalar", t_eps)
                    for ph in phases:
                        ph(k, TS, IO)
                    k.wait("gpsimd", ("syd", k.cnt["syd"]), ("gpd", k.cnt["gpd"]))
                return run
            block.sync(mk("sync"))
            block.tensor(mk("tensor"))
            block.vector(mk("vector"))
            block.scalar(mk("scalar"))
            block.gpsimd(mk("gpsimd"))
    return nc

def build_phaseA():
    io = [
        ("x_full", [T, D], F32, "ExternalInput"),
        ("xq", [QT, D], F32, "ExternalInput"),
        ("WkT", [D, D], F32, "ExternalInput"),
        ("WvT", [D, D], F32, "ExternalInput"),
        ("identw", [128, 128], BF16, "ExternalInput"),
        ("K", [D, T], BF16, "ExternalOutput"),
        ("V", [T, D], BF16, "ExternalOutput"),
        ("xlnqF", [D, QT], BF16, "ExternalOutput"),
        ("xlnq_tok", [QT, D], F32, "ExternalOutput"),
    ]
    return build_program([emit_phaseA], io, PHASEA_SBUF, PHASEA_PSUM)

def emit_phaseB(k, TS, IO):
    Qg = TS["pb_qg"]
    Kb = [TS["pb_kb0"], TS["pb_kb1"]]
    Vb = [TS["pb_vb0"], TS["pb_vb1"]]
    msk = TS["pb_msk"]
    mstate = TS["pb_m"]
    lstate = TS["pb_l"]
    Oacc = TS["pb_oacc"]
    Psb = TS["pb_p"]
    PTsb = TS["pb_pt"]
    sc = TS["pa_sc"]
    sq = TS["pa_sq"]
    xrow = TS["pb_xrow"]
    ret = TS["pb_ret"]
    ident = TS["ident"]
    psS = [TS["ps_a"], TS["ps_b"]]
    psPT = TS["ps_bfA"]
    psO = TS["ps_o"]

    G = {"Qg": [], "Kb": [[], []], "Vb": [[], []], "msk": [], "psS": [[], []],
         "psPT": [], "psO": [], "P": [], "PT": [], "Oacc": [], "xrow": [], "ret": []}
    t_id = ("gpd", 16*3)
    t_id = k.dgp(lambda e: e.dma_start(out=ident[:], in_=IO["identw"][:, :]))

    for g in range(4):
        src = ap2(IO["xlnqF"], g*1024, [[QT, 128], [128*QT, 8], [1, 1024]])
        k.wait("sync", G["Qg"])
        t_q = k.dsync(lambda e, src=src: e.dma_start(out=Qg[:], in_=src))
        srcm = ap2(IO["masks"], g*8*128*512, [[512, 128], [128*512, 8], [1, 512]])
        k.wait("sync", G["msk"])
        t_m = k.dsync(lambda e, srcm=srcm: e.dma_start(out=msk[:], in_=srcm))
        k.wait("vector", G["Oacc"])
        t_mi = k.v(lambda e: e.memset(mstate[:], -3e38))
        t_li = k.v(lambda e: e.memset(lstate[:], 0.0))
        t_oi = k.v(lambda e: e.memset(Oacc[:], 0.0))
        state_tok = [t_mi, t_li, t_oi]
        nmb = 4*g + 4
        for mb in range(nmb):
            buf = mb % 2
            srck = ap2(IO["K"], mb*512, [[T, 128], [128*T, 8], [1, 512]])
            k.wait("sync", G["Kb"][buf])
            t_k = k.dsync(lambda e, srck=srck, buf=buf: e.dma_start(out=Kb[buf][:], in_=srck))
            srcv = ap2(IO["V"], mb*512*D, [[D, 128], [128*D, 4], [1, D]])
            k.wait("sync", G["Vb"][buf])
            t_v = k.dsync(lambda e, srcv=srcv, buf=buf: e.dma_start(out=Vb[buf][:], in_=srcv))
            for qi in range(8):
                s_slot = 4*g + qi // 2
                if mb > s_slot:
                    continue
                sb = qi % 2
                ps = psS[sb]
                k.wait("tensor", t_q, t_k, G["psS"][sb], t_id)
                t_mm = None
                for c in range(8):
                    t_mm = k.t(lambda e, c=c, qi=qi, buf=buf, ps=ps: e.matmul(
                        ps[:], Qg[:, c*1024 + qi*128: c*1024 + (qi+1)*128],
                        Kb[buf][:, c*512:(c+1)*512], start=(c == 0), stop=(c == 7)))
                tok_S = t_mm
                if mb == s_slot:
                    k.wait("vector", t_mm, t_m)
                    tok_S = k.v(lambda e, qi=qi, ps=ps: e.tensor_tensor(
                        ps[:], ps[:], msk[:, qi*512:(qi+1)*512], op=ALU.add))
                k.wait("vector", tok_S, *state_tok)
                t_bm = k.v(lambda e, ps=ps: e.reduce_max(col(sc, 0), ps[:], axis=AX.X))
                k.wait("vector", t_bm)
                t_mn = k.v(lambda e, qi=qi: e.tensor_tensor(col(sc, 1), col(mstate, qi), col(sc, 0), op=ALU.max))
                k.wait("vector", t_mn)
                t_nm = k.v(lambda e: e.tensor_scalar_mul(col(sc, 2), col(sc, 1), -1.0))
                t_dm = k.v(lambda e, qi=qi: e.tensor_tensor(col(sc, 3), col(mstate, qi), col(sc, 1), op=ALU.subtract))
                k.wait("scalar", t_nm, t_dm, tok_S, G["P"])
                t_al = k.s(lambda e: e.activation(out=col(sc, 4), in_=col(sc, 3), func=AF.Exp))
                t_p = k.s(lambda e, ps=ps: e.activation(out=Psb[:], in_=ps[:], func=AF.Exp,
                                                        bias=col(sc, 2), accum_out=col(sc, 5)))
                G["psS"][sb] = [t_p]
                k.wait("vector", t_al, t_p)
                t_l = k.v(lambda e, qi=qi: e.scalar_tensor_tensor(
                    out=col(lstate, qi), in0=col(lstate, qi), scalar=col(sc, 4),
                    in1=col(sc, 5), op0=ALU.mult, op1=ALU.add))
                t_mu = k.v(lambda e, qi=qi: e.tensor_copy(col(mstate, qi), col(sc, 1)))
                k.wait("tensor", t_p, G["psPT"])
                t_tr = None
                for kc in range(4):
                    t_tr = k.t(lambda e, kc=kc: e.transpose(
                        psPT[:, kc*128:(kc+1)*128], Psb[:, kc*128:(kc+1)*128], ident[:]))
                k.wait("vector", t_tr, G["PT"])
                t_pt = k.v(lambda e: e.tensor_copy(PTsb[:], psPT[:]))
                G["psPT"] = [t_pt]
                G["P"] = [("pe", k.cnt["pe"])]
                k.wait("tensor", t_pt, t_v, G["psO"])
                t_av = None
                for half in range(2):
                    for kc in range(4):
                        t_av = k.t(lambda e, kc=kc, half=half, buf=buf: e.matmul(
                            psO[:, half*512:(half+1)*512], PTsb[:, kc*128:(kc+1)*128],
                            Vb[buf][:, kc*1024 + half*512: kc*1024 + (half+1)*512],
                            start=(kc == 0), stop=(kc == 3)))
                G["PT"] = [("pe", k.cnt["pe"])]
                k.wait("vector", t_av, t_al)
                t_o = k.v(lambda e, qi=qi: e.scalar_tensor_tensor(
                    out=Oacc[:, qi*1024:(qi+1)*1024], in0=Oacc[:, qi*1024:(qi+1)*1024],
                    scalar=col(sc, 4), in1=psO[:], op0=ALU.mult, op1=ALU.add))
                G["psO"] = [t_o]
                state_tok = [t_l, t_mu, t_o]
            G["Kb"][buf] = [("pe", k.cnt["pe"])]
            G["Vb"][buf] = [("pe", k.cnt["pe"])]
        fin_tok = []
        for qi in range(8):
            row0 = (g*8 + qi) * 128
            k.wait("vector", *state_tok)
            t_ri = k.v(lambda e, qi=qi: e.reciprocal(col(sc, 6), col(lstate, qi)))
            src = ap2(IO["xlnq_tok"], row0*D, [[D, 128], [1, D]])
            k.wait("sync", G["xrow"])
            t_xr = k.dsync(lambda e, src=src: e.dma_start(out=xrow[:], in_=src))
            k.wait("vector", t_ri, t_xr, G["ret"])
            t_ret = k.v(lambda e, qi=qi: e.scalar_tensor_tensor(
                out=ret[:], in0=Oacc[:, qi*1024:(qi+1)*1024], scalar=col(sc, 6),
                in1=xrow[:], op0=ALU.mult, op1=ALU.add))
            t_ln = ln_tile(k, TS, ret, 0, D, sq[:, :D], sc, 8, sq2_of(TS), [t_ret])
            srcq = ap2(IO["xq"], row0*D, [[D, 128], [1, D]])
            k.wait("sync", t_ret)
            t_xr2 = k.dsync(lambda e, srcq=srcq: e.dma_start(out=xrow[:], in_=srcq))
            k.wait("vector", t_ln, t_xr2)
            t_x1 = k.v(lambda e: e.tensor_tensor(ret[:], sq[:, :D], xrow[:], op=ALU.add))
            G["xrow"] = [t_x1]
            k.wait("sync", t_x1)
            dst = ap2(IO["x1"], row0*D, [[D, 128], [1, D]])
            t_st = k.dsync(lambda e, dst=dst: e.dma_start(out=dst, in_=ret[:]))
            G["ret"] = [("syd", k.cnt["syd"])]
            fin_tok.append(t_st)
        G["Oacc"] = [("dv", k.cnt["dv"])]
        G["Qg"] = [("pe", k.cnt["pe"])]
        G["msk"] = [("dv", k.cnt["dv"])]

def sq2_of(TS):
    return TS["pb_sq2"]

PHASEB_SBUF = [
    ("pb_qg", [128, 8*1024], BF16),
    ("pb_kb0", [128, 8*512], BF16),
    ("pb_kb1", [128, 8*512], BF16),
    ("pb_vb0", [128, 4*1024], BF16),
    ("pb_vb1", [128, 4*1024], BF16),
    ("pb_msk", [128, 8*512], F32),
    ("pb_m", [128, 8], F32),
    ("pb_l", [128, 8], F32),
    ("pb_oacc", [128, 8*1024], F32),
    ("pb_p", [128, 512], BF16),
    ("pb_pt", [128, 512], BF16),
    ("pb_xrow", [128, 1024], F32),
    ("pb_ret", [128, 1024], F32),
    ("pb_sq2", [128, 1024], F32),
    ("pa_sc", [128, 64], F32),
    ("pa_sq", [128, 1024], F32),
    ("ident", [128, 128], BF16),
    ("eps", [128, 1], F32),
]
PHASEB_PSUM = [
    ("ps_a", [128, 512], F32),
    ("ps_b", [128, 512], F32),
    ("ps_bfA", [128, 512], BF16),
    ("ps_o", [128, 1024], F32),
]

def build_phaseB():
    io = [
        ("xq", [QT, D], F32, "ExternalInput"),
        ("xlnqF", [D, QT], BF16, "ExternalInput"),
        ("xlnq_tok", [QT, D], F32, "ExternalInput"),
        ("K", [D, T], BF16, "ExternalInput"),
        ("V", [T, D], BF16, "ExternalInput"),
        ("masks", [32, 128, 512], F32, "ExternalInput"),
        ("identw", [128, 128], BF16, "ExternalInput"),
        ("x1", [QT, D], F32, "ExternalOutput"),
    ]
    return build_program([emit_phaseB], io, PHASEB_SBUF, PHASEB_PSUM)

def host_masks(h):
    blocks = core_blocks(h)
    out = np.zeros((32, 128, 512), np.float32)
    for s in range(16):
        j = blocks[s]
        for qt in range(2):
            qg = j*BS + qt*128 + np.arange(128)[:, None]
            kg = s*512 + np.arange(512)[None, :]
            out[s*2 + qt] = np.where(kg <= qg, 0.0, -1e30).astype(np.float32)
    return out

def emit_phaseC(k, TS, IO):
    xt = TS["pc_x"]
    xlnb = TS["pa_xlnb"]
    xlnF = TS["pa_xlnF"]
    sq, sc = TS["pa_sq"], TS["pa_sc"]
    ident = TS["ident"]
    psT = TS["ps_bfA"]
    psA, psB = TS["ps_a"], TS["ps_b"]
    wslots = [TS["pc_w0"], TS["pc_w1"], TS["pc_w2"]]
    hb1, hb2, hb3 = TS["pc_h1"], TS["pc_h2"], TS["pc_h3"]
    ab = TS["pc_a"]
    routed = TS["pc_routed"]
    cw = TS["pc_cw"]
    s8 = TS["pc_s8"]
    gb = TS["pc_gb"]
    gw = TS["pc_gw"]
    b1g = TS["pc_b1g"]; b1l = TS["pc_b1l"]; b3g = TS["pc_b3g"]; b3l = TS["pc_b3l"]
    b2r = TS["pc_b2r"]
    ones1 = TS["ones1"]

    NE = 9
    t_id = k.dgp(lambda e: e.dma_start(out=ident[:], in_=IO["identw"][:, :]))
    t_on = k.v(lambda e: e.memset(ones1[:], 1.0))
    t_gb = k.dsync(lambda e: e.dma_start(out=gb[:], in_=ap2(IO["gate_b"], 0, [[0, 128], [1, 8]])))
    t_gw = load_wT(k, IO["gate_wT"], gw, D, 8)
    tb = []
    for name, sb_t in (("b1g", b1g), ("b1l", b1l), ("b3g", b3g), ("b3l", b3l)):
        src = ap2(IO[name], 0, [[8, 128], [128*8, NE], [1, 8]])
        tb.append(k.dsync(lambda e, sb_t=sb_t, src=src: e.dma_start(out=sb_t[:], in_=src)))
    tb.append(k.dgp(lambda e: e.dma_start(out=b2r[:], in_=ap2(IO["b2all"], 0, [[NE*1024, 1], [1, NE*1024]]))))

    G = {"xt": [], "xlnb": [], "xlnF": [], "psT": [], "psA": [], "psB": [],
         "w": [[], [], []], "hb1": [], "hb2": [], "hb3": [], "ab": [], "routed": [],
         "cw": [], "s8": [], "x2st": []}

    WNAMES = ("w1gT", "w3gT", "w1lT", "w3lT", "w2T")

    for tc in range(8):
        src = ap2(IO["x1"], tc*512*D, [[D, 128], [128*D, 4], [1, D]])
        k.wait("sync", G["xt"])
        t_x = k.dsync(lambda e, src=src: e.dma_start(out=xt[:], in_=src))
        toks_ln = []
        for s in range(4):
            k.wait("vector", G["xlnb"])
            t_ln = ln_tile(k, TS, xt, s*D, D, xlnb[:, s*D:(s+1)*D], sc, 8*s, sq, [t_x])
            toks_ln.append(t_ln)
        G["xt"] = toks_ln[:] + [("ac", k.cnt["ac"])]
        k.wait("tensor", t_id, *toks_ln)
        t_ev = None
        for c in range(8):
            k.wait("tensor", G["psT"], G["xlnF"] if c == 0 else None, t_ev)
            t_tr = None
            for s in range(4):
                t_tr = k.t(lambda e, c=c, s=s: e.transpose(
                    psT[:, s*128:(s+1)*128], xlnb[:, s*D + c*128: s*D + (c+1)*128], ident[:]))
            k.wait("vector", t_tr)
            t_ev = k.v(lambda e, c=c: e.tensor_copy(xlnF[:, c*512:(c+1)*512], psT[:]))
        G["psT"] = [t_ev]
        G["xlnb"] = [("pe", k.cnt["pe"])]
        t_xlnF = t_ev

        k.wait("tensor", t_gw, t_xlnF)
        for s in range(4):
            k.wait("tensor", G["psB"])
            t_mm = None
            for c in range(8):
                t_mm = k.t(lambda e, c=c, s=s: e.matmul(
                    psB[:, 0:8], xlnF[:, c*512 + s*128: c*512 + (s+1)*128],
                    gw[:, c*8:(c+1)*8], start=(c == 0), stop=(c == 7)))
            k.wait("vector", t_mm, G["s8"])
            t_rm = k.v(lambda e: e.reduce_max(col(sc, 32), psB[:, 0:8], axis=AX.X))
            k.wait("vector", t_rm)
            t_nm = k.v(lambda e: e.tensor_scalar_mul(col(sc, 33), col(sc, 32), -1.0))
            k.wait("scalar", t_nm, t_mm)
            t_ex = k.s(lambda e: e.activation(out=s8[:], in_=psB[:, 0:8], func=AF.Exp,
                                              bias=col(sc, 33), accum_out=col(sc, 34)))
            G["psB"] = [t_ex]
            k.wait("vector", t_ex)
            t_ri = k.v(lambda e: e.reciprocal(col(sc, 35), col(sc, 34)))
            k.wait("vector", t_ri)
            t_sc = k.v(lambda e: e.tensor_scalar(s8[:], s8[:], col(sc, 35), None, op0=ALU.mult))
            k.wait("vector", t_sc, t_gb, tb[0])
            t_bi = k.v(lambda e: e.tensor_tensor(col(sc, 36, 8), s8[:], gb[:], op=ALU.add))
            k.wait("vector", t_bi)
            t_m1 = k.v(lambda e: e.reduce_max(col(sc, 44), col(sc, 36, 8), axis=AX.X))
            k.wait("vector", t_m1)
            t_o1 = k.v(lambda e: e.tensor_scalar(col(sc, 45, 8), col(sc, 36, 8), col(sc, 44), None, op0=ALU.is_equal))
            k.wait("vector", t_o1)
            t_w1a = k.v(lambda e: e.tensor_tensor(col(sc, 53, 8), col(sc, 45, 8), s8[:], op=ALU.mult))
            k.wait("vector", t_w1a)
            t_w1 = k.v(lambda e: e.reduce_sum(col(sc, 61), col(sc, 53, 8), axis=AX.X))
            t_b2 = k.v(lambda e: e.scalar_tensor_tensor(
                out=col(sc, 36, 8), in0=col(sc, 45, 8), scalar=-1e30, in1=col(sc, 36, 8),
                op0=ALU.mult, op1=ALU.add))
            k.wait("vector", t_b2)
            t_m2 = k.v(lambda e: e.reduce_max(col(sc, 44), col(sc, 36, 8), axis=AX.X))
            k.wait("vector", t_m2)
            t_o2 = k.v(lambda e: e.tensor_scalar(col(sc, 36, 8), col(sc, 36, 8), col(sc, 44), None, op0=ALU.is_equal))
            k.wait("vector", t_o2)
            t_w2a = k.v(lambda e: e.tensor_tensor(col(sc, 53, 8), col(sc, 36, 8), s8[:], op=ALU.mult))
            k.wait("vector", t_w2a)
            t_w2 = k.v(lambda e: e.reduce_sum(col(sc, 62), col(sc, 53, 8), axis=AX.X))
            k.wait("vector", t_w1, t_w2)
            t_c1 = k.v(lambda e: e.tensor_scalar(col(sc, 53, 8), col(sc, 45, 8), col(sc, 61), None, op0=ALU.mult))
            k.wait("vector", t_c1)
            t_cw = k.v(lambda e, s=s: e.scalar_tensor_tensor(
                out=cw[:, s*8:(s+1)*8], in0=col(sc, 36, 8), scalar=col(sc, 62),
                in1=col(sc, 53, 8), op0=ALU.mult, op1=ALU.add))
            G["s8"] = [t_w2a, t_cw]
            G["cw"].append(t_cw)

        k.wait("vector", G["routed"])
        t_r0 = k.v(lambda e: e.memset(routed[:], 0.0))
        tok_routed = t_r0

        def wload(e_i, m_i, slot):
            off = (e_i*5 + m_i) * D * D
            src = ap2(IO["moeW"], off, [[D, 128], [128*D, 8], [1, D]])
            k.wait("gpsimd", G["w"][slot])
            return k.dgp(lambda e, src=src, slot=slot: e.dma_start(
                out=wslots[slot][:], in_=src))
        wq = []
        for e_i in range(NE):
            for m_i in range(5):
                wq.append((e_i, m_i))
        wtok = {}
        for i in range(min(3, len(wq))):
            wtok[i] = wload(wq[i][0], wq[i][1], i % 3)
        def wget(i):
            if i not in wtok:
                wtok[i] = wload(wq[i][0], wq[i][1], i % 3)
            return wtok[i], wslots[i % 3]
        wi = 0
        for e_i in range(NE):
            def hmm(dst, tw, wsb, bias_sb, guard_key):
                toks = []
                for fc in range(8):
                    ps, gk = (psA, "psA") if fc % 2 == 0 else (psB, "psB")
                    k.wait("tensor", tw, G[gk], t_xlnF)
                    t_mm2 = None
                    for c in range(8):
                        t_mm2 = k.t(lambda e, fc=fc, c=c, ps=ps, wsb=wsb: e.matmul(
                            ps[:], wsb[:, (c*8+fc)*128:(c*8+fc+1)*128],
                            xlnF[:, c*512:(c+1)*512], start=(c == 0), stop=(c == 7)))
                    k.wait("scalar", t_mm2, G[guard_key], *tb)
                    t_ev2 = k.s(lambda e, fc=fc, ps=ps, dst=dst, bias_sb=bias_sb: e.activation(
                        out=dst[:, fc*512:(fc+1)*512], in_=ps[:], func=AF.Identity,
                        bias=col(bias_sb, e_i*8+fc)))
                    G[gk] = [t_ev2]
                    toks.append(t_ev2)
                return toks
            tw, wsb = wget(wi)
            tg1 = hmm(hb1, tw, wsb, b1g, "hb1")
            G["w"][wi % 3] = [("pe", k.cnt["pe"])]; wi += 1
            tw, wsb = wget(wi)
            tg3 = hmm(hb2, tw, wsb, b3g, "hb2")
            G["w"][wi % 3] = [("pe", k.cnt["pe"])]; wi += 1
            k.wait("vector", *tg1); k.wait("vector", *tg3)
            t_gp = k.v(lambda e: e.tensor_tensor(hb1[:], hb1[:], hb2[:], op=ALU.mult))
            k.wait("vector", t_gp)
            t_gc = k.v(lambda e: e.tensor_scalar_min(hb1[:], hb1[:], 7.0))
            k.wait("scalar", t_gc)
            t_sg = k.s(lambda e: e.activation(out=hb2[:], in_=hb1[:], func=AF.Silu, scale=1.702))
            G["hb1"] = []
            tw, wsb = wget(wi)
            tl1 = hmm(hb3, tw, wsb, b1l, "hb3")
            G["w"][wi % 3] = [("pe", k.cnt["pe"])]; wi += 1
            tw, wsb = wget(wi)
            k.wait("scalar", t_sg)
            tl3 = hmm(hb1, tw, wsb, b3l, "hb1")
            G["w"][wi % 3] = [("pe", k.cnt["pe"])]; wi += 1
            k.wait("vector", *tl1); k.wait("vector", *tl3)
            t_lp = k.v(lambda e: e.tensor_tensor(hb3[:], hb3[:], hb1[:], op=ALU.mult))
            k.wait("vector", t_lp)
            t_lc = k.v(lambda e: e.tensor_scalar(hb3[:], hb3[:], 7.0, -7.0, op0=ALU.min, op1=ALU.max))
            k.wait("vector", t_lc)
            t_la = k.v(lambda e: e.tensor_scalar(hb3[:], hb3[:], 1.0/1.702, 1.0/1.702, op0=ALU.mult, op1=ALU.add))
            k.wait("vector", t_la, t_sg, G["ab"])
            t_a = k.v(lambda e: e.tensor_tensor(ab[:], hb2[:], hb3[:], op=ALU.mult))
            G["hb2"] = [t_a]; G["hb3"] = [t_a]
            tw2, wsb2 = wget(wi)
            rtoks = []
            for tsub in range(4):
                for half in range(2):
                    ps, gk = (psA, "psA") if half == 0 else (psB, "psB")
                    k.wait("tensor", tw2, t_a, G[gk], t_on, *tb)
                    t_mm3 = None
                    for fc in range(8):
                        t_mm3 = k.t(lambda e, fc=fc, tsub=tsub, half=half, ps=ps, wsb2=wsb2: e.matmul(
                            ps[:], ab[:, fc*512 + tsub*128: fc*512 + (tsub+1)*128],
                            wsb2[:, fc*1024 + half*512: fc*1024 + (half+1)*512],
                            start=(fc == 0), stop=False))
                    t_mm3 = k.t(lambda e, half=half, ps=ps: e.matmul(
                        ps[:], ones1[:], b2r[0:1, e_i*1024 + half*512: e_i*1024 + (half+1)*512],
                        start=False, stop=True))
                    k.wait("vector", t_mm3, tok_routed, G["cw"])
                    if e_i < 8:
                        t_ru = k.v(lambda e, tsub=tsub, half=half, ps=ps: e.scalar_tensor_tensor(
                            out=routed[:, tsub*1024 + half*512: tsub*1024 + (half+1)*512],
                            in0=ps[:], scalar=col(cw, tsub*8 + e_i),
                            in1=routed[:, tsub*1024 + half*512: tsub*1024 + (half+1)*512],
                            op0=ALU.mult, op1=ALU.add))
                    else:
                        t_ru = k.v(lambda e, tsub=tsub, half=half, ps=ps: e.tensor_tensor(
                            routed[:, tsub*1024 + half*512: tsub*1024 + (half+1)*512],
                            routed[:, tsub*1024 + half*512: tsub*1024 + (half+1)*512],
                            ps[:], op=ALU.add))
                    G[gk] = [t_ru]
                    rtoks.append(t_ru)
            tok_routed = rtoks[-1]
            G["w"][wi % 3] = [("pe", k.cnt["pe"])]; wi += 1
            G["ab"] = [("pe", k.cnt["pe"])]
        G["cw"] = []
        G["xlnF"] = [("pe", k.cnt["pe"])]


        for tsub in range(4):
            k.wait("vector", tok_routed)
            t_ln2 = ln_tile(k, TS, routed, tsub*1024, D, sq[:, :D], sc, 8, TS["pb_sq2"], [tok_routed])
            k.wait("vector", t_ln2)
            t_x2 = k.v(lambda e, tsub=tsub: e.tensor_tensor(
                routed[:, tsub*1024:(tsub+1)*1024], sq[:, :D], xt[:, tsub*1024:(tsub+1)*1024], op=ALU.add))
            k.wait("sync", t_x2)
            dst = ap2(IO["x2"], (tc*512 + tsub*128)*D, [[D, 128], [1, D]])
            k.dsync(lambda e, dst=dst, tsub=tsub: e.dma_start(out=dst, in_=routed[:, tsub*1024:(tsub+1)*1024]))
        G["routed"] = [("syd", k.cnt["syd"])]
        G["xt"] = G["xt"] + [("dv", k.cnt["dv"])]

PHASEC_SBUF = [
    ("pc_x", [128, 4*1024], F32),
    ("pa_xlnb", [128, 4*1024], BF16),
    ("pa_xlnF", [128, 8*512], BF16),
    ("pa_sq", [128, 1024], F32),
    ("pb_sq2", [128, 1024], F32),
    ("pa_sc", [128, 64], F32),
    ("ident", [128, 128], BF16),
    ("eps", [128, 1], F32),
    ("ones1", [1, 128], BF16),
    ("pc_w0", [128, 8*1024], BF16),
    ("pc_w1", [128, 8*1024], BF16),
    ("pc_w2", [128, 8*1024], BF16),
    ("pc_h1", [128, 8*512], F32),
    ("pc_h2", [128, 8*512], F32),
    ("pc_h3", [128, 8*512], F32),
    ("pc_a", [128, 8*512], BF16),
    ("pc_routed", [128, 4*1024], F32),
    ("pc_cw", [128, 4*8], F32),
    ("pc_s8", [128, 8], F32),
    ("pc_gb", [128, 8], F32),
    ("pc_gw", [128, 8*8], BF16),
    ("pc_b1g", [128, 9*8], F32),
    ("pc_b1l", [128, 9*8], F32),
    ("pc_b3g", [128, 9*8], F32),
    ("pc_b3l", [128, 9*8], F32),
    ("pc_b2r", [1, 9*1024], BF16),
]
PHASEC_PSUM = [
    ("ps_a", [128, 512], F32),
    ("ps_b", [128, 512], F32),
    ("ps_bfA", [128, 512], BF16),
]

def build_phaseC():
    io = [
        ("x1", [QT, D], F32, "ExternalInput"),
        ("moeW", [9*5*D, D], F32, "ExternalInput"),
        ("gate_wT", [D, 8], F32, "ExternalInput"),
        ("gate_b", [1, 8], F32, "ExternalInput"),
        ("b1g", [9, 128, 8], F32, "ExternalInput"),
        ("b1l", [9, 128, 8], F32, "ExternalInput"),
        ("b3g", [9, 128, 8], F32, "ExternalInput"),
        ("b3l", [9, 128, 8], F32, "ExternalInput"),
        ("b2all", [9, 1024], F32, "ExternalInput"),
        ("identw", [128, 128], BF16, "ExternalInput"),
        ("x2", [QT, D], F32, "ExternalOutput"),
    ]
    return build_program([emit_phaseC], io, PHASEC_SBUF, PHASEC_PSUM)

def host_moe_weights(z):
    blocks = []
    bs = {"b1g": [], "b1l": [], "b3g": [], "b3l": [], "b2": []}
    for e in range(8):
        w1 = z['L0_ffn_w1'][e]; w3 = z['L0_ffn_w3'][e]; w2 = z['L0_ffn_w2'][e]
        blocks += [w1[::2].T, w3[::2].T, w1[1::2].T, w3[1::2].T, w2.T]
        bs["b1g"].append(z['L0_ffn_b1'][e][::2]); bs["b1l"].append(z['L0_ffn_b1'][e][1::2])
        bs["b3g"].append(z['L0_ffn_b3'][e][::2]); bs["b3l"].append(z['L0_ffn_b3'][e][1::2])
        bs["b2"].append(z['L0_ffn_b2'][e])
    blocks += [z['L0_ffn_sw1'][::2].T, z['L0_ffn_sw3'][::2].T,
               z['L0_ffn_sw1'][1::2].T, z['L0_ffn_sw3'][1::2].T, z['L0_ffn_sw2'].T]
    bs["b1g"].append(z['L0_ffn_sb1'][::2]); bs["b1l"].append(z['L0_ffn_sb1'][1::2])
    bs["b3g"].append(z['L0_ffn_sb3'][::2]); bs["b3l"].append(z['L0_ffn_sb3'][1::2])
    bs["b2"].append(z['L0_ffn_sb2'])
    moeW = np.ascontiguousarray(np.concatenate(blocks, 0)).astype(np.float32)
    out = {"moeW": moeW,
           "gate_wT": np.ascontiguousarray(z['L0_ffn_gate_w'].T),
           "gate_b": z['L0_ffn_gate_b'].reshape(1, 8).astype(np.float32)}
    for key in ("b1g", "b1l", "b3g", "b3l"):
        arr = np.stack(bs[key]).reshape(9, 8, 128).transpose(0, 2, 1)
        out[key] = np.ascontiguousarray(arr).astype(np.float32)
    out["b2all"] = np.ascontiguousarray(np.stack(bs["b2"])).astype(np.float32)
    return out

def emit_phaseD(k, TS, IO):
    xt = TS["pd_x"]; xlnb = TS["pa_xlnb"]; xlnF = TS["pa_xlnF"]
    sq, sc = TS["pa_sq"], TS["pa_sc"]
    ident = TS["ident"]
    psT, psA, psB = TS["ps_bfA"], TS["ps_a"], TS["ps_b"]
    wx = TS["pd_wx"]; wz = TS["pd_wz"]; wdtu = TS["pd_wdtu"]; wdt = TS["pd_wdt"]
    wbp = TS["pd_wbp"]; wcp = TS["pd_wcp"]
    dtu = TS["pd_dtu"]
    zb = TS["pd_zb"]
    zb2 = TS["pd_zb2"]
    const1 = TS["const1"]
    bdt = TS["pd_bdt"]
    convw = TS["pd_convw"]
    Dpar = TS["pd_dpar"]
    alog = TS["pd_alog"]
    absA = TS["pd_absA"]
    rhat = TS["pd_rhat"]
    mmask = TS["pd_m"]
    latA = TS["pd_latA"]; latB = TS["pd_latB"]; latC = TS["pd_latC"]
    latD = TS["pd_latD"]; latG = TS["pd_latG"]
    bpe = TS["pd_bpe"]; cpe = TS["pd_cpe"]
    hst = TS["pd_h"]
    xrp = TS["pd_xrp"]
    cv = TS["pd_cv"]
    ub = TS["pd_u"]
    dtt = TS["pd_dt"]
    szt = TS["pd_sz"]
    yb = TS["pd_y"]
    gz = TS["pd_gz"]

    t_id = k.dgp(lambda e: e.dma_start(out=ident[:], in_=IO["identw"][:, :]))
    t_c1 = k.v(lambda e: e.memset(const1[:], 1.0))
    t_wx = load_wT(k, IO["WxTh"], wx, D, 512)
    t_wz = load_wT(k, IO["WzTh"], wz, D, 512)
    t_wdtu = load_wT(k, IO["WdtuT"], wdtu, D, 64)
    t_wdt = load_wT(k, IO["WdtTh"], wdt, 64, 512)
    t_wbp = load_wT(k, IO["WpBT"], wbp, D, 16)
    t_wcp = load_wT(k, IO["WpCT"], wcp, D, 16)
    t_bdt = k.dsync(lambda e: e.dma_start(out=bdt[:], in_=ap2(IO["bdt_h"], 0, [[4, 128], [1, 4]])))
    t_cw = k.dsync(lambda e: e.dma_start(out=convw[:], in_=ap2(IO["convw_h"], 0, [[16, 128], [1, 16]])))
    t_dp = k.dsync(lambda e: e.dma_start(out=Dpar[:], in_=ap2(IO["D_h"], 0, [[4, 128], [1, 4]])))
    t_al = k.dsync(lambda e: e.dma_start(out=alog[:], in_=ap2(IO["Alog_h"], 0, [[16, 128], [128*16, 4], [1, 16]])))
    k.wait("scalar", t_al)
    t_aa = k.s(lambda e: e.activation(out=absA[:], in_=alog[:], func=AF.Exp))
    k.wait("vector", t_aa)
    t_den = k.v(lambda e: e.tensor_scalar(zb[:, 0:64], absA[:], -1.0, 1e-10, op0=ALU.mult, op1=ALU.add))
    k.wait("vector", t_den)
    t_r0 = k.v(lambda e: e.reciprocal(zb2[:, 0:64], zb[:, 0:64]))
    t_mm0a = k.v(lambda e: e.tensor_scalar(mmask[:], absA[:], -1.0, 1e-8, op0=ALU.mult, op1=ALU.add))
    k.wait("scalar", t_mm0a)
    t_mm0b = k.s(lambda e: e.activation(out=mmask[:], in_=mmask[:], func=AF.Sign))
    k.wait("vector", t_mm0b)
    t_mm0 = k.v(lambda e: e.tensor_scalar_max(mmask[:], mmask[:], 0.0))
    k.wait("vector", t_r0, t_mm0)
    t_1m = k.v(lambda e: e.tensor_scalar(zb[:, 0:64], mmask[:], -1.0, 1.0, op0=ALU.mult, op1=ALU.add))
    k.wait("vector", t_1m)
    t_rh = k.v(lambda e: e.tensor_tensor(rhat[:], zb2[:, 0:64], zb[:, 0:64], op=ALU.mult))

    G = {"xt": [], "xlnb": [], "xlnF": [], "psT": [], "psA": [], "psB": [],
         "dtu": [], "zb": [t_1m], "proj_st": [], "bpe": [], "cpe": [],
         "lat": [], "latC": [], "latD": [], "xrp": [], "cv": [t_rh], "u": [], "dt": [], "sz": [], "y": [], "gz": []}

    for tc in range(16):
        src = ap2(IO["x2f"], tc*512*D, [[D, 128], [128*D, 4], [1, D]])
        k.wait("sync", G["xt"])
        t_x = k.dsync(lambda e, src=src: e.dma_start(out=xt[:], in_=src))
        toks_ln = []
        for s in range(4):
            k.wait("vector", G["xlnb"])
            t_ln = ln_tile(k, TS, xt, s*D, D, xlnb[:, s*D:(s+1)*D], sc, 8*s, sq, [t_x])
            toks_ln.append(t_ln)
        G["xt"] = toks_ln[:] + [("ac", k.cnt["ac"])]
        k.wait("tensor", t_id, *toks_ln)
        t_ev = None
        for c in range(8):
            k.wait("tensor", G["psT"], G["xlnF"] if c == 0 else None, t_ev)
            t_tr = None
            for s in range(4):
                t_tr = k.t(lambda e, c=c, s=s: e.transpose(
                    psT[:, s*128:(s+1)*128], xlnb[:, s*D + c*128: s*D + (c+1)*128], ident[:]))
            k.wait("vector", t_tr)
            t_ev = k.v(lambda e, c=c: e.tensor_copy(xlnF[:, c*512:(c+1)*512], psT[:]))
        G["psT"] = [t_ev]
        G["xlnb"] = [("pe", k.cnt["pe"])]
        t_xlnF = t_ev
        for which, wsb, twl, outdram, func in (("xres", wx, t_wx, "xres", AF.Identity),
                                              ("z", wz, t_wz, "sz", AF.Silu)):
            for dc in range(4):
                ps, gk = (psA, "psA") if dc % 2 == 0 else (psB, "psB")
                k.wait("tensor", twl, t_xlnF, G[gk])
                t_mm = None
                for c in range(8):
                    t_mm = k.t(lambda e, c=c, dc=dc, ps=ps, wsb=wsb: e.matmul(
                        ps[:], wsb[:, c*512 + dc*128: c*512 + (dc+1)*128],
                        xlnF[:, c*512:(c+1)*512], start=(c == 0), stop=(c == 7)))
                k.wait("scalar", t_mm, G["zb"])
                t_ev2 = k.s(lambda e, ps=ps, func=func: e.activation(out=zb[:, 0:512], in_=ps[:], func=func))
                G[gk] = [t_ev2]
                k.wait("sync", t_ev2)
                dst = ap2(IO[outdram], (dc*128)*T + tc*512, [[T, 128], [1, 512]])
                k.dsync(lambda e, dst=dst: e.dma_start(out=dst, in_=zb[:, 0:512]))
                G["zb"] = [("syd", k.cnt["syd"])]
        k.wait("tensor", t_wdtu, G["psA"])
        t_mm = None
        for c in range(8):
            t_mm = k.t(lambda e, c=c: e.matmul(
                psA[0:64, :], wdtu[:, c*64:(c+1)*64], xlnF[:, c*512:(c+1)*512],
                start=(c == 0), stop=(c == 7)))
        k.wait("vector", t_mm, G["dtu"])
        t_dtu = k.v(lambda e: e.tensor_copy(dtu[:], psA[0:64, :]))
        G["psA"] = [t_dtu]
        for dc in range(4):
            ps, gk = (psB, "psB") if dc % 2 == 0 else (psA, "psA")
            k.wait("tensor", t_wdt, t_dtu, G[gk])
            t_mm = k.t(lambda e, dc=dc, ps=ps: e.matmul(
                ps[:], wdt[0:64, dc*128:(dc+1)*128], dtu[:], start=True, stop=True))
            k.wait("scalar", t_mm, t_bdt, G["zb"])
            t_z0 = k.s(lambda e, dc=dc, ps=ps: e.activation(out=zb[:, 0:512], in_=ps[:],
                                                            func=AF.Identity, bias=col(bdt, dc)))
            G[gk] = [t_z0]
            k.wait("vector", t_z0, G["cv"])
            t_m0 = k.v(lambda e: e.tensor_scalar_max(zb2[:, 0:512], zb[:, 0:512], 0.0))
            k.wait("scalar", t_m0)
            t_ab = k.s(lambda e: e.activation(out=zb[:, 0:512], in_=zb[:, 0:512], func=AF.Abs))
            k.wait("scalar", t_ab)
            t_e0 = k.s(lambda e: e.activation(out=zb[:, 0:512], in_=zb[:, 0:512], func=AF.Exp, scale=-1.0))
            k.wait("scalar", t_e0, t_c1)
            t_l0 = k.s(lambda e: e.activation(out=zb[:, 0:512], in_=zb[:, 0:512], func=AF.Ln, bias=col(const1, 0)))
            k.wait("vector", t_l0, t_m0)
            t_dt = k.v(lambda e: e.tensor_tensor(zb[:, 0:512], zb2[:, 0:512], zb[:, 0:512], op=ALU.add))
            k.wait("sync", t_dt)
            dst = ap2(IO["dtd"], (dc*128)*T + tc*512, [[T, 128], [1, 512]])
            k.dsync(lambda e, dst=dst: e.dma_start(out=dst, in_=zb[:, 0:512]))
            G["zb"] = [("syd", k.cnt["syd"])]
            G["cv"] = [t_dt]
        G["dtu"] = [("pe", k.cnt["pe"])]
        for which, wsb, twl, outdram in (("bp", wbp, t_wbp, "bpd"), ("cp", wcp, t_wcp, "cpd")):
            for s in range(4):
                ps, gk = (psA, "psA") if s % 2 == 0 else (psB, "psB")
                k.wait("tensor", twl, G[gk])
                t_mm = None
                for c in range(8):
                    t_mm = k.t(lambda e, c=c, s=s, ps=ps, wsb=wsb: e.matmul(
                        ps[:, 0:16], xlnF[:, c*512 + s*128: c*512 + (s+1)*128],
                        wsb[:, c*16:(c+1)*16], start=(c == 0), stop=(c == 7)))
                k.wait("vector", t_mm, G["y"])
                t_ev3 = k.v(lambda e, ps=ps: e.tensor_copy(yb[:, 0:16], ps[:, 0:16]))
                G[gk] = [t_ev3]
                k.wait("sync", t_ev3)
                dst = ap2(IO[outdram], (tc*512 + s*128)*16, [[16, 128], [1, 16]])
                k.dsync(lambda e, dst=dst: e.dma_start(out=dst, in_=yb[:, 0:16]))
                G["y"] = [("syd", k.cnt["syd"])]
        G["xlnF"] = [("pe", k.cnt["pe"])]

    Tc = 256
    t_h0 = k.v(lambda e: e.memset(hst[:], 0.0))
    hs_tok = [t_h0]
    for tci in range(T // Tc):
        k.wait("sync", G["bpe"])
        t_bpe = k.dsync(lambda e, tci=tci: e.dma_start(
            out=bpe[:], in_=ap2(IO["bpd"], tci*Tc*16, [[0, 128], [1, Tc*16]])))
        k.wait("sync", G["cpe"])
        t_cpe = k.dsync(lambda e, tci=tci: e.dma_start(
            out=cpe[:], in_=ap2(IO["cpd"], tci*Tc*16, [[0, 128], [1, Tc*16]])))
        for dc in range(4):
            k.wait("sync", G["xrp"])
            if tci == 0:
                t_z3 = k.v(lambda e: e.memset(xrp[:, 0:3], 0.0))
                k.wait("sync", t_z3)
                t_xr = k.dsync(lambda e, dc=dc: e.dma_start(
                    out=xrp[:, 3:3+Tc], in_=ap2(IO["xres"], dc*128*T, [[T, 128], [1, Tc]])))
            else:
                t_xr = k.dsync(lambda e, dc=dc, tci=tci: e.dma_start(
                    out=xrp[:], in_=ap2(IO["xres"], dc*128*T + tci*Tc - 3, [[T, 128], [1, Tc+3]])))
            k.wait("vector", t_xr, t_cw, G["cv"], G["u"])
            t_c0 = k.v(lambda e, dc=dc: e.tensor_scalar(cv[:], xrp[:, 0:Tc], col(convw, dc*4+0), None, op0=ALU.mult))
            tprev = t_c0
            for tap in range(1, 4):
                k.wait("vector", tprev)
                tprev = k.v(lambda e, dc=dc, tap=tap: e.scalar_tensor_tensor(
                    out=cv[:], in0=xrp[:, tap:tap+Tc], scalar=col(convw, dc*4+tap),
                    in1=cv[:], op0=ALU.mult, op1=ALU.add))
            k.wait("scalar", tprev, G["u"])
            t_u = k.s(lambda e: e.activation(out=ub[:], in_=cv[:], func=AF.Silu))
            G["cv"] = [t_u]
            k.wait("sync", G["dt"])
            t_dt2 = k.dsync(lambda e, dc=dc, tci=tci: e.dma_start(
                out=dtt[:], in_=ap2(IO["dtd"], dc*128*T + tci*Tc, [[T, 128], [1, Tc]])))
            dt_b = ap2(dtt.ap(), 0, [[Tc, 128], [1, Tc], [0, 16]])
            aA_b = ap2(absA.ap(), dc*16, [[64, 128], [0, Tc], [1, 16]])
            rh_b = ap2(rhat.ap(), dc*16, [[64, 128], [0, Tc], [1, 16]])
            mm_b = ap2(mmask.ap(), dc*16, [[64, 128], [0, Tc], [1, 16]])
            u_b = ap2(ub.ap(), 0, [[Tc, 128], [1, Tc], [0, 16]])
            k.wait("vector", t_dt2, t_aa, G["lat"])
            t_lA = k.v(lambda e: e.tensor_tensor(latA[:], dt_b, aA_b, op=ALU.mult))
            k.wait("scalar", t_lA)
            t_lB = k.s(lambda e: e.activation(out=latB[:], in_=latA[:], func=AF.Exp, scale=-1.0))
            k.wait("gpsimd", t_dt2, t_rh, G["lat"])
            t_lG = k.g(lambda e: e.tensor_tensor(latG[:], mm_b, dt_b, op=ALU.mult))
            k.wait("vector", t_lB, t_rh, G["latC"])
            t_lC = k.v(lambda e: e.scalar_tensor_tensor(
                out=latC[:], in0=latB[:], scalar=-1.0, in1=rh_b, op0=ALU.add, op1=ALU.mult))
            k.wait("gpsimd", t_lC, t_lG)
            t_lG2 = k.g(lambda e: e.tensor_tensor(latG[:], latC[:], latG[:], op=ALU.add))
            G["latC"] = [t_lG2]
            k.wait("vector", t_lG2, t_u, G["latD"])
            t_lD = k.v(lambda e: e.tensor_tensor(latD[:], latG[:], u_b, op=ALU.mult))
            k.wait("vector", t_lD, t_bpe)
            t_X = k.v(lambda e: e.tensor_tensor(latG[:], latD[:], bpe[:], op=ALU.mult))
            k.wait("vector", t_X, *hs_tok)
            t_sc2 = None
            for n in range(16):
                a_v = ap2(latB.ap(), n, [[Tc*16, 128], [16, Tc]])
                x_v = ap2(latG.ap(), n, [[Tc*16, 128], [16, Tc]])
                o_v = ap2(latA.ap(), n, [[Tc*16, 128], [16, Tc]])
                t_sc2 = k.v(lambda e, n=n, a_v=a_v, x_v=x_v, o_v=o_v, dc=dc: e.tensor_tensor_scan(
                    out=o_v, data0=a_v, data1=x_v, initial=col(hst, dc*16+n),
                    op0=ALU.mult, op1=ALU.add))
            k.wait("vector", t_sc2)
            t_hs = k.v(lambda e, dc=dc: e.tensor_copy(hst[:, dc*16:(dc+1)*16], latA[:, (Tc-1)*16:Tc*16]))
            hs_tok = [t_hs]
            k.wait("vector", t_cpe)
            t_pr = k.v(lambda e: e.tensor_tensor(latD[:], latA[:], cpe[:], op=ALU.mult))
            k.wait("vector", t_pr, G["y"])
            in3 = ap2(latD.ap(), 0, [[Tc*16, 128], [16, Tc], [1, 16]])
            t_y = k.v(lambda e, in3=in3: e.tensor_reduce(yb[:, 0:Tc], in3, axis=AX.X, op=ALU.add))
            G["latD"] = [t_y]
            k.wait("vector", t_y, t_dp, t_u)
            t_yd = k.v(lambda e, dc=dc: e.scalar_tensor_tensor(
                out=yb[:, 0:Tc], in0=ub[:], scalar=col(Dpar, dc), in1=yb[:, 0:Tc],
                op0=ALU.mult, op1=ALU.add))
            G["u"] = [t_yd]
            k.wait("sync", G["sz"])
            t_sz = k.dsync(lambda e, dc=dc, tci=tci: e.dma_start(
                out=szt[:], in_=ap2(IO["sz"], dc*128*T + tci*Tc, [[T, 128], [1, Tc]])))
            k.wait("vector", t_yd, t_sz, G["gz"])
            t_g = k.v(lambda e: e.tensor_tensor(gz[:], yb[:, 0:Tc], szt[:], op=ALU.mult))
            G["sz"] = [t_g]; G["y"] = [t_g]; G["dt"] = [t_lA, t_lG]
            G["lat"] = [t_g, t_pr, t_sc2]
            G["xrp"] = [tprev]
            k.wait("sync", t_g)
            dst = ap2(IO["gh"], dc*128*T + tci*Tc, [[T, 128], [1, Tc]])
            k.dsync(lambda e, dst=dst: e.dma_start(out=dst, in_=gz[:]))
            G["gz"] = [("syd", k.cnt["syd"])]
        G["bpe"] = [("dv", k.cnt["dv"])]
        G["cpe"] = [("dv", k.cnt["dv"])]

PHASED_SBUF = [
    ("pd_x", [128, 4*1024], F32),
    ("pa_xlnb", [128, 4*1024], BF16),
    ("pa_xlnF", [128, 8*512], BF16),
    ("pa_sq", [128, 1024], F32),
    ("pa_sc", [128, 64], F32),
    ("ident", [128, 128], BF16),
    ("eps", [128, 1], F32),
    ("const1", [128, 1], F32),
    ("pd_wx", [128, 8*512], BF16),
    ("pd_wz", [128, 8*512], BF16),
    ("pd_wdtu", [128, 8*64], BF16),
    ("pd_wdt", [64, 512], BF16),
    ("pd_wbp", [128, 8*16], BF16),
    ("pd_wcp", [128, 8*16], BF16),
    ("pd_dtu", [64, 512], BF16),
    ("pd_zb", [128, 512], F32),
    ("pd_zb2", [128, 512], F32),
    ("pd_bdt", [128, 4], F32),
    ("pd_convw", [128, 16], F32),
    ("pd_dpar", [128, 4], F32),
    ("pd_alog", [128, 64], F32),
    ("pd_absA", [128, 64], F32),
    ("pd_rhat", [128, 64], F32),
    ("pd_m", [128, 64], F32),
    ("pd_latA", [128, 4096], F32),
    ("pd_latB", [128, 4096], F32),
    ("pd_latC", [128, 4096], F32),
    ("pd_latD", [128, 4096], F32),
    ("pd_latG", [128, 4096], F32),
    ("pd_bpe", [128, 4096], F32),
    ("pd_cpe", [128, 4096], F32),
    ("pd_h", [128, 64], F32),
    ("pd_xrp", [128, 259], F32),
    ("pd_cv", [128, 256], F32),
    ("pd_u", [128, 256], F32),
    ("pd_dt", [128, 256], F32),
    ("pd_sz", [128, 256], F32),
    ("pd_y", [128, 256], F32),
    ("pd_gz", [128, 256], BF16),
]
PHASED_PSUM = [
    ("ps_a", [128, 512], F32),
    ("ps_b", [128, 512], F32),
    ("ps_bfA", [128, 512], BF16),
]

def build_phaseD(dbg=False):
    io = [
        ("x2f", [T, D], F32, "ExternalInput"),
        ("WxTh", [D, 512], F32, "ExternalInput"),
        ("WzTh", [D, 512], F32, "ExternalInput"),
        ("WdtuT", [D, 64], F32, "ExternalInput"),
        ("WdtTh", [64, 512], F32, "ExternalInput"),
        ("WpBT", [D, 16], F32, "ExternalInput"),
        ("WpCT", [D, 16], F32, "ExternalInput"),
        ("bdt_h", [128, 4], F32, "ExternalInput"),
        ("convw_h", [128, 16], F32, "ExternalInput"),
        ("D_h", [128, 4], F32, "ExternalInput"),
        ("Alog_h", [4, 128, 16], F32, "ExternalInput"),
        ("identw", [128, 128], BF16, "ExternalInput"),
        ("xres", [512, T], F32, "ExternalOutput" if dbg else "Internal"),
        ("sz", [512, T], F32, "ExternalOutput" if dbg else "Internal"),
        ("dtd", [512, T], F32, "ExternalOutput" if dbg else "Internal"),
        ("bpd", [T*16], F32, "ExternalOutput" if dbg else "Internal"),
        ("cpd", [T*16], F32, "ExternalOutput" if dbg else "Internal"),
        ("gh", [512, T], BF16, "ExternalOutput"),
    ]
    return build_program([emit_phaseD], io, PHASED_SBUF, PHASED_PSUM)

def host_phaseD_weights(z, h):
    dh = slice(h*512, (h+1)*512)
    Wp = z['L1_attn_W_params']
    out = {
        "WxTh": np.ascontiguousarray(z['L1_attn_W_x'][dh].T),
        "WzTh": np.ascontiguousarray(z['L1_attn_W_z'][dh].T),
        "WdtuT": np.ascontiguousarray(Wp[:64].T),
        "WdtTh": np.ascontiguousarray(z['L1_attn_W_dt'][dh].T),
        "WpBT": np.ascontiguousarray(Wp[64:80].T),
        "WpCT": np.ascontiguousarray(Wp[80:96].T),
        "bdt_h": np.ascontiguousarray(z['L1_attn_b_dt'][dh].reshape(4, 128).T),
        "convw_h": np.ascontiguousarray(
            z['L1_attn_conv_w'][dh, 0, :].reshape(4, 128, 4).transpose(1, 0, 2).reshape(128, 16)),
        "D_h": np.ascontiguousarray(z['L1_attn_D_param'][dh].reshape(4, 128).T),
        "Alog_h": np.ascontiguousarray(z['L1_attn_A_log'][dh].reshape(4, 128, 16)),
    }
    return out

def emit_phaseE(k, TS, IO):
    gch = TS["pe_g"]
    wo = TS["pe_wo"]
    sq, sc = TS["pa_sq"], TS["pa_sc"]
    at = TS["pe_at"]
    xrow = TS["pb_xrow"]
    psA, psB = TS["ps_a"], TS["ps_b"]
    t_wo = load_wT(k, IO["WoutT"], wo, D, D)
    G = {"g": [], "at": [], "xrow": [], "psA": [], "psB": []}
    for tc in range(8):
        src = ap2(IO["gf"], tc*512, [[QT, 128], [128*QT, 8], [1, 512]])
        k.wait("sync", G["g"])
        t_g = k.dsync(lambda e, src=src: e.dma_start(out=gch[:], in_=src))
        for s in range(4):
            t_evs = []
            for half in range(2):
                ps, gk = (psA, "psA") if half == 0 else (psB, "psB")
                k.wait("tensor", t_g, t_wo, G[gk])
                t_mm = None
                for c in range(8):
                    t_mm = k.t(lambda e, c=c, s=s, half=half, ps=ps: e.matmul(
                        ps[:], gch[:, c*512 + s*128: c*512 + (s+1)*128],
                        wo[:, c*1024 + half*512: c*1024 + (half+1)*512],
                        start=(c == 0), stop=(c == 7)))
                k.wait("scalar", t_mm, G["at"])
                t_ev = k.s(lambda e, half=half, ps=ps: e.activation(
                    out=at[:, half*512:(half+1)*512], in_=ps[:], func=AF.Identity, scale=1.0))
                G[gk] = [t_ev]
                t_evs.append(t_ev)
            t_ln = ln_tile(k, TS, at, 0, D, sq[:, :D], sc, 8, TS["pb_sq2"], t_evs)
            src2 = ap2(IO["x2rows"], (tc*512 + s*128)*D, [[D, 128], [1, D]])
            k.wait("sync", G["xrow"])
            t_xr = k.dsync(lambda e, src2=src2: e.dma_start(out=xrow[:], in_=src2))
            k.wait("vector", t_ln, t_xr)
            t_x3 = k.v(lambda e: e.tensor_tensor(at[:, 0:D], sq[:, :D], xrow[:], op=ALU.add))
            G["xrow"] = [t_x3]
            k.wait("sync", t_x3)
            dst = ap2(IO["x3"], (tc*512 + s*128)*D, [[D, 128], [1, D]])
            k.dsync(lambda e, dst=dst: e.dma_start(out=dst, in_=at[:, 0:D]))
            G["at"] = [("syd", k.cnt["syd"])]
        G["g"] = [("pe", k.cnt["pe"])]

PHASEE_SBUF = [
    ("pe_g", [128, 8*512], BF16),
    ("pe_wo", [128, 8*1024], BF16),
    ("pe_at", [128, 1024], F32),
    ("pb_xrow", [128, 1024], F32),
    ("pa_sq", [128, 1024], F32),
    ("pb_sq2", [128, 1024], F32),
    ("pa_sc", [128, 64], F32),
    ("eps", [128, 1], F32),
]
PHASEE_PSUM = [
    ("ps_a", [128, 512], F32),
    ("ps_b", [128, 512], F32),
]

def build_phaseE():
    io = [
        ("gf", [D, QT], BF16, "ExternalInput"),
        ("WoutT", [D, D], F32, "ExternalInput"),
        ("x2rows", [QT, D], F32, "ExternalInput"),
        ("x3", [QT, D], F32, "ExternalOutput"),
    ]
    return build_program([emit_phaseE], io, PHASEE_SBUF, PHASEE_PSUM)

def emit_phaseF(k, TS, IO):
    xt = TS["pf_x"]; xlnb = TS["pa_xlnb"]; xlnF = TS["pa_xlnF"]
    sq, sc = TS["pa_sq"], TS["pa_sc"]
    ident = TS["ident"]
    psT, psA, psB = TS["ps_bfA"], TS["ps_a"], TS["ps_b"]
    wsm = [TS["pf_wsm0"], TS["pf_wsm1"]]
    wmed = [TS["pf_wm0"], TS["pf_wm0"]]
    pF = [TS["pf_p1"], TS["pf_p2"], TS["pf_p3"]]
    s1 = TS["pf_s1"]; s2 = TS["pf_s2"]; sh1 = TS["pf_sh1"]; sh2 = TS["pf_sh2"]
    s3e = TS["pf_s3e"]
    yt = TS["pf_y"]

    t_id = k.dgp(lambda e: e.dma_start(out=ident[:], in_=IO["identw"][:, :]))
    t_s = []
    for name, sb_t in (("s1", s1), ("s2", s2), ("sh1", sh1), ("sh2", sh2)):
        t_s.append(k.dsync(lambda e, sb_t=sb_t, name=name: e.dma_start(
            out=sb_t[:], in_=ap2(IO[name], 0, [[32, 128], [1, 32]]))))
    t_s3 = k.dsync(lambda e: e.dma_start(out=s3e[:], in_=ap2(IO["s3"], 0, [[0, 128], [1, 1024]])))

    G = {"xt": [], "xlnb": [], "xlnF": [], "psT": [], "psA": [], "psB": [],
         "wmed": [[]], "wsm": [[], []], "p1": [], "p2": [], "p3": [], "y": []}

    for tc in range(8):
        src = ap2(IO["x3"], tc*512*D, [[D, 128], [128*D, 4], [1, D]])
        k.wait("sync", G["xt"])
        t_x = k.dsync(lambda e, src=src: e.dma_start(out=xt[:], in_=src))
        toks_ln = []
        for s in range(4):
            k.wait("vector", G["xlnb"])
            t_ln = ln_tile(k, TS, xt, s*D, D, xlnb[:, s*D:(s+1)*D], sc, 8*s, sq, [t_x])
            toks_ln.append(t_ln)
        G["xt"] = toks_ln[:] + [("ac", k.cnt["ac"])]
        k.wait("tensor", t_id, *toks_ln)
        t_ev = None
        for c in range(8):
            k.wait("tensor", G["psT"], G["xlnF"] if c == 0 else None, t_ev)
            t_tr = None
            for s in range(4):
                t_tr = k.t(lambda e, c=c, s=s: e.transpose(
                    psT[:, s*128:(s+1)*128], xlnb[:, s*D + c*128: s*D + (c+1)*128], ident[:]))
            k.wait("vector", t_tr)
            t_ev = k.v(lambda e, c=c: e.tensor_copy(xlnF[:, c*512:(c+1)*512], psT[:]))
        G["psT"] = [t_ev]
        G["xlnb"] = [("pe", k.cnt["pe"])]
        t_xlnF = t_ev

        def stage_big(dst, dstkey, srcsb, src_nch, wio, scale_sb, func, t_src, extra_guard=None):
            toks = []
            tw_pipe = {}
            def wld(fo):
                slot = fo % 2
                k.wait("gpsimd", G["wsm"][slot])
                blk = src_nch * 128
                src2 = ap2(IO[wio], fo*128*blk, [[blk, 128], [1, blk]])
                return (k.dgp(lambda e, src2=src2, slot=slot: e.dma_start(
                    out=wsm[slot][:, 0:src_nch*128], in_=src2)), slot)
            tw_pipe[0] = wld(0)
            for fo in range(32):
                if fo + 1 < 32:
                    tw_pipe[fo+1] = wld(fo+1)
                t_w, slot = tw_pipe.pop(fo)
                ps, gk = (psA, "psA") if fo % 2 == 0 else (psB, "psB")
                k.wait("tensor", t_w, t_src, G[gk], extra_guard if fo == 0 else None)
                t_mm = None
                for c in range(src_nch):
                    t_mm = k.t(lambda e, c=c, ps=ps, slot=slot, srcsb=srcsb: e.matmul(
                        ps[:], wsm[slot][:, c*128:(c+1)*128],
                        srcsb[:, c*512:(c+1)*512], start=(c == 0), stop=(c == src_nch-1)))
                G["wsm"][slot] = [("pe", k.cnt["pe"])]
                k.wait("scalar", t_mm, G[dstkey] if fo == 0 else None)
                t_e = k.s(lambda e, fo=fo, ps=ps, dst=dst, scale_sb=scale_sb, func=func: e.activation(
                    out=dst[:, fo*512:(fo+1)*512], in_=ps[:], func=func, scale=col(scale_sb, fo)))
                G[gk] = [t_e]
                toks.append(t_e)
            return toks
        k.wait("scalar", *t_s)
        tk1 = stage_big(pF[0], "p1", xlnF, 8, "w1T", s1, AF.Silu, t_xlnF)
        tk2 = stage_big(pF[1], "p2", xlnF, 8, "w2T", s2, AF.Identity, t_xlnF)
        k.wait("vector", tk1[-1], tk2[-1])
        k.wait("vector", G["p3"])
        t_h = k.v(lambda e: e.tensor_tensor(pF[2][:], pF[0][:], pF[1][:], op=ALU.mult))
        G["p1"] = [t_h]; G["p2"] = [t_h]
        tg1 = stage_big(pF[0], "p1", pF[2], 32, "h1T", sh1, AF.Gelu, t_h, [t_h])
        tg2 = stage_big(pF[1], "p2", pF[0], 32, "h2T", sh2, AF.Gelu, tg1[-1], tg1)
        G["p3"] = [("pe", k.cnt["pe"])]
        k.wait("tensor", tg2[-1])
        yev = {}
        for half in range(2):
            k.wait("gpsimd", G["wmed"][0])
            src3 = ap2(IO["w3T"], half*512, [[1024, 128], [128*1024, 32], [1, 512]])
            t_w3h = k.dgp(lambda e, src3=src3: e.dma_start(out=wmed[0][:], in_=src3))
            for s in range(4):
                ps, gk = (psA, "psA") if s % 2 == 0 else (psB, "psB")
                k.wait("tensor", t_w3h, G[gk])
                t_mm = None
                for c in range(32):
                    t_mm = k.t(lambda e, c=c, s=s, ps=ps: e.matmul(
                        ps[:], pF[1][:, c*512 + s*128: c*512 + (s+1)*128],
                        wmed[0][:, c*512:(c+1)*512],
                        start=(c == 0), stop=(c == 31)))
                k.wait("vector", t_mm, G["y"], t_s3, yev.get(("g", s)))
                t_ev2 = k.v(lambda e, half=half, s=s, ps=ps: e.scalar_tensor_tensor(
                    out=yt[:, s*1024 + half*512: s*1024 + (half+1)*512], in0=ps[:],
                    scalar=1.0, in1=s3e_b(TS, half), op0=ALU.bypass, op1=ALU.mult))
                G[gk] = [t_ev2]
                yev[(half, s)] = t_ev2
            G["wmed"] = [[("pe", k.cnt["pe"])]]
        for s in range(4):
            t_evs = [yev[(0, s)], yev[(1, s)]]
            t_ln = ln_tile(k, TS, yt, s*1024, D, sq[:, :D], sc, 8, TS["pb_sq2"], t_evs)
            k.wait("vector", t_ln)
            t_x4 = k.v(lambda e, s=s: e.tensor_tensor(yt[:, s*1024:(s+1)*1024], sq[:, :D], xt[:, s*1024:(s+1)*1024], op=ALU.add))
            yev[("g", s)] = t_x4
            k.wait("sync", t_x4)
            dst = ap2(IO["x4"], (tc*512 + s*128)*D, [[D, 128], [1, D]])
            k.dsync(lambda e, dst=dst, s=s: e.dma_start(out=dst, in_=yt[:, s*1024:(s+1)*1024]))
        G["y"] = [("syd", k.cnt["syd"])]
        G["xt"] = G["xt"] + [("dv", k.cnt["dv"])]
        G["p2"] = [("pe", k.cnt["pe"])]
        G["wmed"] = [[("pe", k.cnt["pe"])], [("pe", k.cnt["pe"])]]
        G["xlnF"] = [("pe", k.cnt["pe"])]

PHASEF_SBUF = [
    ("pf_x", [128, 4*1024], F32),
    ("pa_xlnb", [128, 4*1024], BF16),
    ("pa_xlnF", [128, 8*512], BF16),
    ("pa_sq", [128, 1024], F32),
    ("pb_sq2", [128, 1024], F32),
    ("pa_sc", [128, 64], F32),
    ("ident", [128, 128], BF16),
    ("eps", [128, 1], F32),
    ("pf_wm0", [128, 32*512], BF16),
    ("pf_wsm0", [128, 32*128], BF16),
    ("pf_wsm1", [128, 32*128], BF16),
    ("pf_p1", [128, 32*512], BF16),
    ("pf_p2", [128, 32*512], BF16),
    ("pf_p3", [128, 32*512], BF16),
    ("pf_s1", [128, 32], F32),
    ("pf_s2", [128, 32], F32),
    ("pf_sh1", [128, 32], F32),
    ("pf_sh2", [128, 32], F32),
    ("pf_s3e", [128, 1024], F32),
    ("pf_y", [128, 4*1024], F32),
]
PHASEF_PSUM = [
    ("ps_a", [128, 512], F32),
    ("ps_b", [128, 512], F32),
    ("ps_bfA", [128, 512], BF16),
]

def s3e_b(TS, half):
    return TS["pf_s3e"][:, half*512:(half+1)*512]

def build_phaseF():
    io = [
        ("x3", [QT, D], F32, "ExternalInput"),
        ("w1T", [32*128, D], F32, "ExternalInput"),
        ("w2T", [32*128, D], F32, "ExternalInput"),
        ("h1T", [32*128, 4096], F32, "ExternalInput"),
        ("h2T", [32*128, 4096], F32, "ExternalInput"),
        ("w3T", [4096, D], F32, "ExternalInput"),
        ("s1", [128, 32], F32, "ExternalInput"),
        ("s2", [128, 32], F32, "ExternalInput"),
        ("sh1", [128, 32], F32, "ExternalInput"),
        ("sh2", [128, 32], F32, "ExternalInput"),
        ("s3", [1, 1024], F32, "ExternalInput"),
        ("identw", [128, 128], BF16, "ExternalInput"),
        ("x4", [QT, D], F32, "ExternalOutput"),
    ]
    return build_program([emit_phaseF], io, PHASEF_SBUF, PHASEF_PSUM)

def _tile_wT(wT, nch):
    K_, M = wT.shape
    assert K_ == nch*128 and M == 4096
    X = wT.reshape(nch, 128, 32, 128).transpose(2, 1, 0, 3)
    return np.ascontiguousarray(X.reshape(32*128, nch*128)).astype(np.float32)

def host_phaseF_weights(z):
    def sc128(v):
        return np.ascontiguousarray(v.reshape(32, 128).T).astype(np.float32)
    return {
        "w1T": _tile_wT(z['L1_ffn_w1_w'].T, 8),
        "w2T": _tile_wT(z['L1_ffn_w2_w'].T, 8),
        "h1T": _tile_wT(z['L1_ffn_h1_w'].T, 32),
        "h2T": _tile_wT(z['L1_ffn_h2_w'].T, 32),
        "w3T": np.ascontiguousarray(z['L1_ffn_w3_w'].T),
        "s1": sc128(z['L1_ffn_w1_s']), "s2": sc128(z['L1_ffn_w2_s']),
        "sh1": sc128(z['L1_ffn_h1_s']), "sh2": sc128(z['L1_ffn_h2_s']),
        "s3": z['L1_ffn_w3_s'].reshape(1, 1024).astype(np.float32),
    }


_BUILD_CACHE = {}

def _get(b, f):
    if b not in _BUILD_CACHE:
        _BUILD_CACHE[b] = f()
    return _BUILD_CACHE[b]

def kernel(**inputs):
    import ml_dtypes
    bf = ml_dtypes.bfloat16
    z = inputs
    x0 = np.asarray(z['L0_attn_x'])
    ident = np.eye(128, dtype=np.float32).astype(bf)
    cores = list(range(8))
    rows_h = {h: np.array([r for j in core_blocks(h) for r in range(j*BS, (j+1)*BS)]) for h in (0, 1)}
    W = z['L0_attn_W']
    WkT = np.ascontiguousarray(W[:1024].T).astype(np.float32)
    WvT = np.ascontiguousarray(W[1024:].T).astype(np.float32)

    ncA = _get('A', build_phaseA)
    inm = []
    for c in cores:
        b, h = c // 2, c % 2
        inm.append({"x_full": np.ascontiguousarray(x0[b]),
                    "xq": np.ascontiguousarray(x0[b][rows_h[h]]),
                    "WkT": WkT, "WvT": WvT, "identw": ident})
    resA = run_bass_kernel_spmd(ncA, inm, core_ids=cores)
    rA = resA.results

    ncB = _get('B', build_phaseB)
    masks = {h: host_masks(h) for h in (0, 1)}
    inm = []
    for c in cores:
        b, h = c // 2, c % 2
        inm.append({"xq": np.ascontiguousarray(x0[b][rows_h[h]]),
                    "xlnqF": np.asarray(rA[c]["xlnqF"]),
                    "xlnq_tok": np.asarray(rA[c]["xlnq_tok"]),
                    "K": np.asarray(rA[c]["K"]), "V": np.asarray(rA[c]["V"]),
                    "masks": masks[h], "identw": ident})
    resB = run_bass_kernel_spmd(ncB, inm, core_ids=cores)
    rB = resB.results

    ncC = _get('C', build_phaseC)
    moe = host_moe_weights(z)
    inm = []
    for c in cores:
        d = dict(moe)
        d["x1"] = np.asarray(rB[c]["x1"])
        d["identw"] = ident
        inm.append(d)
    resC = run_bass_kernel_spmd(ncC, inm, core_ids=cores)
    rC = resC.results

    x2full = {}
    for b in range(4):
        xf = np.empty((T, D), np.float32)
        xf[rows_h[0]] = np.asarray(rC[2*b]["x2"])
        xf[rows_h[1]] = np.asarray(rC[2*b+1]["x2"])
        x2full[b] = xf

    ncD = _get('D', build_phaseD)
    wD = {h: host_phaseD_weights(z, h) for h in (0, 1)}
    inm = []
    for c in cores:
        b, h = c // 2, c % 2
        d = dict(wD[h])
        d["x2f"] = x2full[b]
        d["identw"] = ident
        inm.append(d)
    resD = run_bass_kernel_spmd(ncD, inm, core_ids=cores)
    rD = resD.results

    ncE = _get('E', build_phaseE)
    WoutT = np.ascontiguousarray(z['L1_attn_W_out'].T).astype(np.float32)
    inm = []
    for c in cores:
        b, h = c // 2, c % 2
        gf = np.concatenate([np.asarray(rD[2*b]["gh"]), np.asarray(rD[2*b+1]["gh"])], 0)
        gf = gf[:, h*4096:(h+1)*4096]
        inm.append({"gf": np.ascontiguousarray(gf), "WoutT": WoutT,
                    "x2rows": np.ascontiguousarray(x2full[b][h*4096:(h+1)*4096])})
    resE = run_bass_kernel_spmd(ncE, inm, core_ids=cores)
    rE = resE.results

    ncF = _get('F', build_phaseF)
    wF = host_phaseF_weights(z)
    inm = []
    for c in cores:
        d = dict(wF)
        d["x3"] = np.asarray(rE[c]["x3"])
        d["identw"] = ident
        inm.append(d)
    resF = run_bass_kernel_spmd(ncF, inm, core_ids=cores)
    rF = resF.results

    out = np.empty((4, T, D), np.float32)
    for c in cores:
        b, h = c // 2, c % 2
        out[b, h*4096:(h+1)*4096] = np.asarray(rF[c]["x4"])
    return out


# version 05
# speedup vs baseline: 1.1700x; 1.0437x over previous
import numpy as np
import concourse.bass as bass
import concourse.mybir as mybir
from concourse.bass_utils import run_bass_kernel_spmd

F32 = mybir.dt.float32
BF16 = mybir.dt.bfloat16
AF = mybir.ActivationFunctionType
ALU = mybir.AluOpType
AX = mybir.AxisListType

D = 1024
T = 8192
NB = 32
BS = 256
QT = 4096
EPS = 1e-5

def core_blocks(h):
    return sorted([j for j in range(NB) if (min(j, NB - 1 - j) % 2) == h])

ENGS = ("sync", "tensor", "vector", "scalar", "gpsimd")

class Ctx:
    def __init__(self, nc, sems, dma_sems, active, engine):
        self.nc = nc
        self.sems = sems
        self.dma_sems = dma_sems
        self.active = active
        self.engine = engine
        self.cnt = {k: 0 for k in sems}
        self.chan = {}
        self.waited = {e: {} for e in ENGS}

    def _sem(self, key):
        if key.startswith("q"):
            return self.dma_sems[int(key[1:])]
        return self.sems[key]

    def _dch(self, ch):
        if ch not in self.chan:
            self.chan[ch] = len(self.chan)
            assert self.chan[ch] < len(self.dma_sems), "DMA channel pool exhausted"
        return "q%d" % self.chan[ch]

    def _do(self, eng, sem, inc, f):
        self.cnt[sem] = self.cnt.get(sem, 0) + inc
        if self.active == eng:
            inst = f(self.engine)
            inst.then_inc(self._sem(sem), inc)
        return (sem, self.cnt[sem])

    def t(self, f): return self._do("tensor", "pe", 1, f)
    def v(self, f): return self._do("vector", "dv", 1, f)
    def s(self, f): return self._do("scalar", "ac", 1, f)
    def g(self, f): return self._do("gpsimd", "gc", 1, f)
    def dsync(self, f, ch="misc"): return self._do("sync", self._dch(ch), 16, f)
    def dgp(self, f, ch="misc"): return self._do("gpsimd", self._dch(ch), 16, f)

    def barrier(self):
        snap = dict(self.cnt)
        for eng in ENGS:
            for sem in sorted(snap):
                if snap[sem] > 0:
                    self.wait(eng, (sem, snap[sem]))

    def wait(self, eng, *toks):
        need = {}
        for tok in toks:
            if tok is None: continue
            if isinstance(tok, list):
                for t2 in tok:
                    if t2 is None: continue
                    s, v = t2
                    need[s] = max(need.get(s, 0), v)
            else:
                s, v = tok
                need[s] = max(need.get(s, 0), v)
        for s, v in sorted(need.items()):
            if self.waited[eng].get(s, 0) >= v:
                continue
            self.waited[eng][s] = v
            if self.active == eng:
                self.engine.wait_ge(self._sem(s), v)

    def dma_done_tokens(self):
        return [(("q%d" % i), self.cnt.get("q%d" % i, 0)) for i in self.chan.values()]

def ap2(apx, offset, dims):
    return bass.AP(apx.tensor, apx.offset + offset, dims)

def col(t, c0, n=1):
    return t[:, c0:c0+n]

def bcast_free(t, c0, count):
    a = t.ap()
    return bass.AP(a.tensor, a.offset + c0, [list(a.ap[0]), [0, count]])

def ln_tile(k, TS, xt, xoff, width, out_ap, sc, scoff, sq, deps):
    xs = xt[:, xoff:xoff+width]
    k.wait("vector", *deps)
    t_sum = k.v(lambda e: e.reduce_sum(col(sc, scoff+0), xs, axis=AX.X))
    k.wait("scalar", *deps)
    t_sq = k.s(lambda e: e.activation(out=sq[:, :width], in_=xs, func=AF.Square,
                                      accum_out=col(sc, scoff+1)))
    k.wait("vector", t_sum)
    t_mean = k.v(lambda e: e.tensor_scalar_mul(col(sc, scoff+2), col(sc, scoff+0), 1.0/width))
    k.wait("vector", t_sq, t_mean)
    t_m2 = k.v(lambda e: e.tensor_tensor(col(sc, scoff+3), col(sc, scoff+2), col(sc, scoff+2), op=ALU.mult))
    t_e2 = k.v(lambda e: e.tensor_scalar_mul(col(sc, scoff+4), col(sc, scoff+1), 1.0/width))
    k.wait("vector", t_m2, t_e2)
    t_var = k.v(lambda e: e.tensor_tensor(col(sc, scoff+5), col(sc, scoff+4), col(sc, scoff+3), op=ALU.subtract))
    k.wait("scalar", t_var)
    t_std = k.s(lambda e: e.activation(out=col(sc, scoff+6), in_=col(sc, scoff+5), func=AF.Sqrt,
                                       bias=col(TS["eps"], 0)))
    k.wait("vector", t_std)
    t_rstd = k.v(lambda e: e.reciprocal(col(sc, scoff+7), col(sc, scoff+6)))
    t_nm = k.v(lambda e: e.tensor_scalar_mul(col(sc, scoff+2), col(sc, scoff+2), -1.0))
    k.wait("vector", t_rstd, t_nm)
    t_out = k.v(lambda e: e.scalar_tensor_tensor(
        out=out_ap, in0=xs, scalar=col(sc, scoff+2), in1=bcast_free(sc, scoff+7, width),
        op0=ALU.add, op1=ALU.mult))
    return t_out

def load_wT(k, dram_ap, sb, kdim, mdim, sb_off=0, ch="w"):
    if kdim < 128:
        src = ap2(dram_ap, 0, [[mdim, kdim], [1, mdim]])
        return k.dgp(lambda e: e.dma_start(out=sb[0:kdim, sb_off:sb_off+mdim], in_=src), ch=ch)
    nch = kdim // 128
    src = ap2(dram_ap, 0, [[mdim, 128], [128*mdim, nch], [1, mdim]])
    return k.dgp(lambda e: e.dma_start(out=sb[:, sb_off:sb_off+nch*mdim], in_=src), ch=ch)

def emit_phaseA(k, TS, IO):
    xt, xlnb, xln32 = TS["pa_x"], TS["pa_xlnb"], TS["pa_xln32"]
    xlnF, sq, sc = TS["pa_xlnF"], TS["pa_sq"], TS["pa_sc"]
    wk, wv, ident = TS["pa_wk"], TS["pa_wv"], TS["ident"]
    psT, psA, psB = TS["ps_bfA"], TS["ps_a"], TS["ps_b"]
    kc, vc = TS["pa_kc"], TS["pa_vc"]

    t_wk = load_wT(k, IO["WkT"], wk, D, D, ch="wk")
    t_wv = load_wT(k, IO["WvT"], wv, D, D, ch="wv")
    t_id = k.dgp(lambda e: e.dma_start(out=ident[:], in_=IO["identw"][:, :]), ch="id")

    G = {"xt": [], "xlnb": [], "xlnF": [], "kc": [], "vc": [], "psT": [],
         "psA": [], "psB": []}

    def do_chunk(src_dram, tc, qpath):
        src = ap2(src_dram, tc*512*D, [[D, 128], [128*D, 4], [1, D]])
        k.wait("sync", G["xt"])
        t_x = k.dsync(lambda e: e.dma_start(out=xt[:], in_=src), ch="xt")
        toks_ln = []
        for s in range(4):
            out_ap = xln32[:, s*D:(s+1)*D] if qpath else xlnb[:, s*D:(s+1)*D]
            k.wait("vector", G["xlnb"])
            t_ln = ln_tile(k, TS, xt, s*D, D, out_ap, sc, 8*s, sq, [t_x])
            if qpath:
                k.wait("sync", t_ln)
                dst = ap2(IO["xlnq_tok"], (tc*512 + s*128)*D, [[D, 128], [1, D]])
                k.dsync(lambda e, s=s: e.dma_start(out=dst, in_=xln32[:, s*D:(s+1)*D]), ch="xln32")
                k.wait("vector", t_ln)
                t_ln = k.v(lambda e, s=s: e.tensor_copy(xlnb[:, s*D:(s+1)*D], xln32[:, s*D:(s+1)*D]))
            toks_ln.append(t_ln)
        G["xt"] = toks_ln[:]
        G["xt"].append(("ac", k.cnt["ac"]))
        k.wait("tensor", t_id, *toks_ln)
        t_ev = None
        for c in range(8):
            k.wait("tensor", G["psT"], G["xlnF"] if c == 0 else None, t_ev)
            t_tr = None
            for s in range(4):
                t_tr = k.t(lambda e, c=c, s=s: e.transpose(
                    psT[:, s*128:(s+1)*128], xlnb[:, s*D + c*128: s*D + (c+1)*128], ident[:]))
            k.wait("vector", t_tr)
            t_ev = k.v(lambda e, c=c: e.tensor_copy(xlnF[:, c*512:(c+1)*512], psT[:]))
        G["psT"] = [t_ev]
        G["xlnb"] = [("pe", k.cnt["pe"])]
        t_xlnF = t_ev
        if qpath:
            k.wait("sync", t_xlnF)
            toks = []
            for c in range(8):
                dst = ap2(IO["xlnqF"], (c*128)*QT + tc*512, [[QT, 128], [1, 512]])
                toks.append(k.dsync(lambda e, c=c, dst=dst: e.dma_start(out=dst, in_=xlnF[:, c*512:(c+1)*512]), ch="xlnF"))
            G["xlnF"] = toks
            return
        k.wait("tensor", t_wk, t_xlnF)
        for dk in range(8):
            ps, gk = (psA, "psA") if dk % 2 == 0 else (psB, "psB")
            k.wait("tensor", G[gk])
            t_mm = None
            for c in range(8):
                t_mm = k.t(lambda e, dk=dk, c=c, ps=ps: e.matmul(
                    ps[:], wk[:, (c*8+dk)*128:(c*8+dk+1)*128], xlnF[:, c*512:(c+1)*512],
                    start=(c == 0), stop=(c == 7)))
            k.wait("scalar", t_mm, G["kc"])
            t_ev2 = k.s(lambda e, dk=dk, ps=ps: e.activation(out=kc[:, dk*512:(dk+1)*512], in_=ps[:], func=AF.Copy))
            G[gk] = [t_ev2]
            k.wait("sync", t_ev2)
            dst = ap2(IO["K"], (dk*128)*T + tc*512, [[T, 128], [1, 512]])
            tk_st = k.dsync(lambda e, dk=dk, dst=dst: e.dma_start(out=dst, in_=kc[:, dk*512:(dk+1)*512]), ch="kc")
        G["kc"] = [tk_st]
        k.wait("tensor", t_wv)
        for s in range(4):
            t_evs = []
            for half in range(2):
                ps, gk = (psA, "psA") if half == 0 else (psB, "psB")
                k.wait("tensor", G[gk])
                t_mm = None
                for c in range(8):
                    t_mm = k.t(lambda e, s=s, half=half, c=c, ps=ps: e.matmul(
                        ps[:], xlnF[:, c*512 + s*128: c*512 + (s+1)*128],
                        wv[:, c*1024 + half*512: c*1024 + (half+1)*512],
                        start=(c == 0), stop=(c == 7)))
                k.wait("scalar", t_mm, G["vc"])
                t_ev3 = k.s(lambda e, s=s, half=half, ps=ps: e.activation(
                    out=vc[:, s*1024 + half*512: s*1024 + (half+1)*512], in_=ps[:], func=AF.Copy))
                G[gk] = [t_ev3]
                t_evs.append(t_ev3)
            k.wait("sync", *t_evs)
            dst = ap2(IO["V"], (tc*512 + s*128)*D, [[D, 128], [1, D]])
            tv_st = k.dsync(lambda e, s=s, dst=dst: e.dma_start(out=dst, in_=vc[:, s*1024:(s+1)*1024]), ch="vc")
        G["vc"] = [tv_st]
        G["xlnF"] = [("pe", k.cnt["pe"])]

    for tc in range(16):
        do_chunk(IO["x_full"], tc, qpath=False)
    for tc in range(8):
        do_chunk(IO["xq"], tc, qpath=True)

PHASEA_SBUF = [
    ("pa_x", [128, 4*1024], F32),
    ("pa_xlnb", [128, 4*1024], BF16),
    ("pa_xln32", [128, 4*1024], F32),
    ("pa_xlnF", [128, 8*512], BF16),
    ("pa_sq", [128, 1024], F32),
    ("pa_sc", [128, 64], F32),
    ("pa_wk", [128, 8*1024], BF16),
    ("pa_wv", [128, 8*1024], BF16),
    ("ident", [128, 128], BF16),
    ("eps", [128, 1], F32),
    ("pa_kc", [128, 8*512], BF16),
    ("pa_vc", [128, 4*1024], BF16),
]
PHASEA_PSUM = [
    ("ps_bfA", [128, 512], BF16),
    ("ps_a", [128, 512], F32),
    ("ps_b", [128, 512], F32),
]

def build_program(phases, io_decl, sbuf_decl, psum_decl):
    nc = bass.Bass("TRN2", target_bir_lowering=False, debug=False)
    IO = {}
    for name, shape, dt, kind in io_decl:
        IO[name] = nc.dram_tensor(name, shape, dt, kind=kind).ap()
    import contextlib
    stack = contextlib.ExitStack()
    TS = {}
    for name, shape, dt in sbuf_decl:
        TS[name] = stack.enter_context(nc.sbuf_tensor(name, shape, dt))
    for name, shape, dt in psum_decl:
        TS[name] = stack.enter_context(nc.psum_tensor(name, shape, dt))
    sems = {}
    for s in ("pe", "dv", "ac", "gc"):
        sems[s] = stack.enter_context(nc.semaphore(s))
    dma_sems = [stack.enter_context(nc.semaphore("q%d" % i)) for i in range(40)]
    with stack:
        with nc.Block() as block:
            def mk(eng_name):
                def run(engine):
                    k = Ctx(nc, sems, dma_sems, eng_name, engine)
                    t_eps = k.v(lambda e: e.memset(TS["eps"][:], EPS))
                    k.wait("scalar", t_eps)
                    for ph in phases:
                        ph(k, TS, IO)
                    k.wait("gpsimd", *k.dma_done_tokens())
                return run
            block.sync(mk("sync"))
            block.tensor(mk("tensor"))
            block.vector(mk("vector"))
            block.scalar(mk("scalar"))
            block.gpsimd(mk("gpsimd"))
    return nc

def build_phaseA():
    io = [
        ("x_full", [T, D], F32, "ExternalInput"),
        ("xq", [QT, D], F32, "ExternalInput"),
        ("WkT", [D, D], F32, "ExternalInput"),
        ("WvT", [D, D], F32, "ExternalInput"),
        ("identw", [128, 128], BF16, "ExternalInput"),
        ("K", [D, T], BF16, "ExternalOutput"),
        ("V", [T, D], BF16, "ExternalOutput"),
        ("xlnqF", [D, QT], BF16, "ExternalOutput"),
        ("xlnq_tok", [QT, D], F32, "ExternalOutput"),
    ]
    return build_program([emit_phaseA], io, PHASEA_SBUF, PHASEA_PSUM)

def emit_phaseB(k, TS, IO):
    Qg = TS["pb_qg"]
    Kb = [TS["pb_kb0"], TS["pb_kb1"]]
    Vb = [TS["pb_vb0"], TS["pb_vb1"]]
    msk = TS["pb_msk"]
    mstate = TS["pb_m"]
    lstate = TS["pb_l"]
    Oacc = TS["pb_oacc"]
    Psb = TS["pb_p"]
    PTsb = TS["pb_pt"]
    sc = TS["pa_sc"]
    sq = TS["pa_sq"]
    xrow = TS["pb_xrow"]
    ret = TS["pb_ret"]
    ident = TS["ident"]
    psS = [TS["ps_a"], TS["ps_b"]]
    psPT = TS["ps_bfA"]
    psO = TS["ps_o"]

    G = {"Qg": [], "Kb": [[], []], "Vb": [[], []], "msk": [], "psS": [[], []],
         "psPT": [], "psO": [], "P": [], "PT": [], "Oacc": [], "xrow": [], "ret": []}
    t_id = k.dgp(lambda e: e.dma_start(out=ident[:], in_=IO["identw"][:, :]), ch="id")

    for g in range(4):
        src = ap2(IO["xlnqF"], g*1024, [[QT, 128], [128*QT, 8], [1, 1024]])
        k.wait("sync", G["Qg"])
        t_q = k.dsync(lambda e, src=src: e.dma_start(out=Qg[:], in_=src), ch="qg")
        srcm = ap2(IO["masks"], g*8*128*512, [[512, 128], [128*512, 8], [1, 512]])
        k.wait("sync", G["msk"])
        t_m = k.dsync(lambda e, srcm=srcm: e.dma_start(out=msk[:], in_=srcm), ch="msk")
        k.wait("vector", G["Oacc"])
        t_mi = k.v(lambda e: e.memset(mstate[:], -3e38))
        t_li = k.v(lambda e: e.memset(lstate[:], 0.0))
        t_oi = k.v(lambda e: e.memset(Oacc[:], 0.0))
        state_tok = [t_mi, t_li, t_oi]
        nmb = 4*g + 4
        for mb in range(nmb):
            buf = mb % 2
            srck = ap2(IO["K"], mb*512, [[T, 128], [128*T, 8], [1, 512]])
            k.wait("sync", G["Kb"][buf])
            t_k = k.dsync(lambda e, srck=srck, buf=buf: e.dma_start(out=Kb[buf][:], in_=srck), ch="kb%d" % buf)
            srcv = ap2(IO["V"], mb*512*D, [[D, 128], [128*D, 4], [1, D]])
            k.wait("sync", G["Vb"][buf])
            t_v = k.dsync(lambda e, srcv=srcv, buf=buf: e.dma_start(out=Vb[buf][:], in_=srcv), ch="vb%d" % buf)
            for qi in range(8):
                s_slot = 4*g + qi // 2
                if mb > s_slot:
                    continue
                sb = qi % 2
                ps = psS[sb]
                k.wait("tensor", t_q, t_k, G["psS"][sb], t_id)
                t_mm = None
                for c in range(8):
                    t_mm = k.t(lambda e, c=c, qi=qi, buf=buf, ps=ps: e.matmul(
                        ps[:], Qg[:, c*1024 + qi*128: c*1024 + (qi+1)*128],
                        Kb[buf][:, c*512:(c+1)*512], start=(c == 0), stop=(c == 7)))
                tok_S = t_mm
                if mb == s_slot:
                    k.wait("vector", t_mm, t_m)
                    tok_S = k.v(lambda e, qi=qi, ps=ps: e.tensor_tensor(
                        ps[:], ps[:], msk[:, qi*512:(qi+1)*512], op=ALU.add))
                k.wait("vector", tok_S, *state_tok)
                t_bm = k.v(lambda e, ps=ps: e.reduce_max(col(sc, 0), ps[:], axis=AX.X))
                k.wait("vector", t_bm)
                t_mn = k.v(lambda e, qi=qi: e.tensor_tensor(col(sc, 1), col(mstate, qi), col(sc, 0), op=ALU.max))
                k.wait("vector", t_mn)
                t_nm = k.v(lambda e: e.tensor_scalar_mul(col(sc, 2), col(sc, 1), -1.0))
                t_dm = k.v(lambda e, qi=qi: e.tensor_tensor(col(sc, 3), col(mstate, qi), col(sc, 1), op=ALU.subtract))
                k.wait("scalar", t_nm, t_dm, tok_S, G["P"])
                t_al = k.s(lambda e: e.activation(out=col(sc, 4), in_=col(sc, 3), func=AF.Exp))
                t_p = k.s(lambda e, ps=ps: e.activation(out=Psb[:], in_=ps[:], func=AF.Exp,
                                                        bias=col(sc, 2), accum_out=col(sc, 5)))
                G["psS"][sb] = [t_p]
                k.wait("vector", t_al, t_p)
                t_l = k.v(lambda e, qi=qi: e.scalar_tensor_tensor(
                    out=col(lstate, qi), in0=col(lstate, qi), scalar=col(sc, 4),
                    in1=col(sc, 5), op0=ALU.mult, op1=ALU.add))
                t_mu = k.v(lambda e, qi=qi: e.tensor_copy(col(mstate, qi), col(sc, 1)))
                k.wait("tensor", t_p, G["psPT"])
                t_tr = None
                for kc in range(4):
                    t_tr = k.t(lambda e, kc=kc: e.transpose(
                        psPT[:, kc*128:(kc+1)*128], Psb[:, kc*128:(kc+1)*128], ident[:]))
                k.wait("vector", t_tr, G["PT"])
                t_pt = k.v(lambda e: e.tensor_copy(PTsb[:], psPT[:]))
                G["psPT"] = [t_pt]
                G["P"] = [("pe", k.cnt["pe"])]
                k.wait("tensor", t_pt, t_v, G["psO"])
                t_av = None
                for half in range(2):
                    for kc in range(4):
                        t_av = k.t(lambda e, kc=kc, half=half, buf=buf: e.matmul(
                            psO[:, half*512:(half+1)*512], PTsb[:, kc*128:(kc+1)*128],
                            Vb[buf][:, kc*1024 + half*512: kc*1024 + (half+1)*512],
                            start=(kc == 0), stop=(kc == 3)))
                G["PT"] = [("pe", k.cnt["pe"])]
                k.wait("vector", t_av, t_al)
                t_o = k.v(lambda e, qi=qi: e.scalar_tensor_tensor(
                    out=Oacc[:, qi*1024:(qi+1)*1024], in0=Oacc[:, qi*1024:(qi+1)*1024],
                    scalar=col(sc, 4), in1=psO[:], op0=ALU.mult, op1=ALU.add))
                G["psO"] = [t_o]
                state_tok = [t_l, t_mu, t_o]
            G["Kb"][buf] = [("pe", k.cnt["pe"])]
            G["Vb"][buf] = [("pe", k.cnt["pe"])]
        fin_tok = []
        for qi in range(8):
            row0 = (g*8 + qi) * 128
            k.wait("vector", *state_tok)
            t_ri = k.v(lambda e, qi=qi: e.reciprocal(col(sc, 6), col(lstate, qi)))
            src = ap2(IO["xlnq_tok"], row0*D, [[D, 128], [1, D]])
            k.wait("sync", G["xrow"])
            t_xr = k.dsync(lambda e, src=src: e.dma_start(out=xrow[:], in_=src), ch="xrow")
            k.wait("vector", t_ri, t_xr, G["ret"])
            t_ret = k.v(lambda e, qi=qi: e.scalar_tensor_tensor(
                out=ret[:], in0=Oacc[:, qi*1024:(qi+1)*1024], scalar=col(sc, 6),
                in1=xrow[:], op0=ALU.mult, op1=ALU.add))
            t_ln = ln_tile(k, TS, ret, 0, D, sq[:, :D], sc, 8, sq2_of(TS), [t_ret])
            srcq = ap2(IO["xq"], row0*D, [[D, 128], [1, D]])
            k.wait("sync", t_ret)
            t_xr2 = k.dsync(lambda e, srcq=srcq: e.dma_start(out=xrow[:], in_=srcq), ch="xrow")
            k.wait("vector", t_ln, t_xr2)
            t_x1 = k.v(lambda e: e.tensor_tensor(ret[:], sq[:, :D], xrow[:], op=ALU.add))
            G["xrow"] = [t_x1]
            k.wait("sync", t_x1)
            dst = ap2(IO["x1"], row0*D, [[D, 128], [1, D]])
            t_st = k.dsync(lambda e, dst=dst: e.dma_start(out=dst, in_=ret[:]), ch="ret")
            G["ret"] = [t_st]
            fin_tok.append(t_st)
        G["Oacc"] = [("dv", k.cnt["dv"])]
        G["Qg"] = [("pe", k.cnt["pe"])]
        G["msk"] = [("dv", k.cnt["dv"])]

def sq2_of(TS):
    return TS["pb_sq2"]

PHASEB_SBUF = [
    ("pb_qg", [128, 8*1024], BF16),
    ("pb_kb0", [128, 8*512], BF16),
    ("pb_kb1", [128, 8*512], BF16),
    ("pb_vb0", [128, 4*1024], BF16),
    ("pb_vb1", [128, 4*1024], BF16),
    ("pb_msk", [128, 8*512], F32),
    ("pb_m", [128, 8], F32),
    ("pb_l", [128, 8], F32),
    ("pb_oacc", [128, 8*1024], F32),
    ("pb_p", [128, 512], BF16),
    ("pb_pt", [128, 512], BF16),
    ("pb_xrow", [128, 1024], F32),
    ("pb_ret", [128, 1024], F32),
    ("pb_sq2", [128, 1024], F32),
    ("pa_sc", [128, 64], F32),
    ("pa_sq", [128, 1024], F32),
    ("ident", [128, 128], BF16),
    ("eps", [128, 1], F32),
]
PHASEB_PSUM = [
    ("ps_a", [128, 512], F32),
    ("ps_b", [128, 512], F32),
    ("ps_bfA", [128, 512], BF16),
    ("ps_o", [128, 1024], F32),
]

def build_phaseB():
    io = [
        ("xq", [QT, D], F32, "ExternalInput"),
        ("xlnqF", [D, QT], BF16, "ExternalInput"),
        ("xlnq_tok", [QT, D], F32, "ExternalInput"),
        ("K", [D, T], BF16, "ExternalInput"),
        ("V", [T, D], BF16, "ExternalInput"),
        ("masks", [32, 128, 512], F32, "ExternalInput"),
        ("identw", [128, 128], BF16, "ExternalInput"),
        ("x1", [QT, D], F32, "ExternalOutput"),
    ]
    return build_program([emit_phaseB], io, PHASEB_SBUF, PHASEB_PSUM)

def host_masks(h):
    blocks = core_blocks(h)
    out = np.zeros((32, 128, 512), np.float32)
    for s in range(16):
        j = blocks[s]
        for qt in range(2):
            qg = j*BS + qt*128 + np.arange(128)[:, None]
            kg = s*512 + np.arange(512)[None, :]
            out[s*2 + qt] = np.where(kg <= qg, 0.0, -1e30).astype(np.float32)
    return out

def emit_phaseC(k, TS, IO):
    xt = TS["pc_x"]
    xlnb = TS["pa_xlnb"]
    xlnF = TS["pa_xlnF"]
    sq, sc = TS["pa_sq"], TS["pa_sc"]
    ident = TS["ident"]
    psT = TS["ps_bfA"]
    psA, psB = TS["ps_a"], TS["ps_b"]
    wslots = [TS["pc_w0"], TS["pc_w1"], TS["pc_w2"]]
    hb1, hb2, hb3 = TS["pc_h1"], TS["pc_h2"], TS["pc_h3"]
    ab = TS["pc_a"]
    routed = TS["pc_routed"]
    cw = TS["pc_cw"]
    s8 = TS["pc_s8"]
    gb = TS["pc_gb"]
    gw = TS["pc_gw"]
    b1g = TS["pc_b1g"]; b1l = TS["pc_b1l"]; b3g = TS["pc_b3g"]; b3l = TS["pc_b3l"]
    b2r = TS["pc_b2r"]
    ones1 = TS["ones1"]

    NE = 9
    t_id = k.dgp(lambda e: e.dma_start(out=ident[:], in_=IO["identw"][:, :]), ch="id")
    t_on = k.v(lambda e: e.memset(ones1[:], 1.0))
    t_gb = k.dsync(lambda e: e.dma_start(out=gb[:], in_=ap2(IO["gate_b"], 0, [[0, 128], [1, 8]])), ch="gb")
    t_gw = load_wT(k, IO["gate_wT"], gw, D, 8, ch="gw")
    tb = []
    for name, sb_t in (("b1g", b1g), ("b1l", b1l), ("b3g", b3g), ("b3l", b3l)):
        src = ap2(IO[name], 0, [[8, 128], [128*8, NE], [1, 8]])
        tb.append(k.dsync(lambda e, sb_t=sb_t, src=src: e.dma_start(out=sb_t[:], in_=src), ch="bias"))
    tb.append(k.dgp(lambda e: e.dma_start(out=b2r[:], in_=ap2(IO["b2all"], 0, [[NE*1024, 1], [1, NE*1024]])), ch="b2r"))

    G = {"xt": [], "xlnb": [], "xlnF": [], "psT": [], "psA": [], "psB": [],
         "w": [[], [], []], "hb1": [], "hb2": [], "hb3": [], "ab": [], "routed": [],
         "cw": [], "s8": [], "x2st": []}

    WNAMES = ("w1gT", "w3gT", "w1lT", "w3lT", "w2T")

    for tc in range(8):
        src = ap2(IO["x1"], tc*512*D, [[D, 128], [128*D, 4], [1, D]])
        k.wait("sync", G["xt"])
        t_x = k.dsync(lambda e, src=src: e.dma_start(out=xt[:], in_=src), ch="xt")
        toks_ln = []
        for s in range(4):
            k.wait("vector", G["xlnb"])
            t_ln = ln_tile(k, TS, xt, s*D, D, xlnb[:, s*D:(s+1)*D], sc, 8*s, sq, [t_x])
            toks_ln.append(t_ln)
        G["xt"] = toks_ln[:] + [("ac", k.cnt["ac"])]
        k.wait("tensor", t_id, *toks_ln)
        t_ev = None
        for c in range(8):
            k.wait("tensor", G["psT"], G["xlnF"] if c == 0 else None, t_ev)
            t_tr = None
            for s in range(4):
                t_tr = k.t(lambda e, c=c, s=s: e.transpose(
                    psT[:, s*128:(s+1)*128], xlnb[:, s*D + c*128: s*D + (c+1)*128], ident[:]))
            k.wait("vector", t_tr)
            t_ev = k.v(lambda e, c=c: e.tensor_copy(xlnF[:, c*512:(c+1)*512], psT[:]))
        G["psT"] = [t_ev]
        G["xlnb"] = [("pe", k.cnt["pe"])]
        t_xlnF = t_ev

        k.wait("tensor", t_gw, t_xlnF)
        for s in range(4):
            k.wait("tensor", G["psB"])
            t_mm = None
            for c in range(8):
                t_mm = k.t(lambda e, c=c, s=s: e.matmul(
                    psB[:, 0:8], xlnF[:, c*512 + s*128: c*512 + (s+1)*128],
                    gw[:, c*8:(c+1)*8], start=(c == 0), stop=(c == 7)))
            k.wait("vector", t_mm, G["s8"])
            t_rm = k.v(lambda e: e.reduce_max(col(sc, 32), psB[:, 0:8], axis=AX.X))
            k.wait("vector", t_rm)
            t_nm = k.v(lambda e: e.tensor_scalar_mul(col(sc, 33), col(sc, 32), -1.0))
            k.wait("scalar", t_nm, t_mm)
            t_ex = k.s(lambda e: e.activation(out=s8[:], in_=psB[:, 0:8], func=AF.Exp,
                                              bias=col(sc, 33), accum_out=col(sc, 34)))
            G["psB"] = [t_ex]
            k.wait("vector", t_ex)
            t_ri = k.v(lambda e: e.reciprocal(col(sc, 35), col(sc, 34)))
            k.wait("vector", t_ri)
            t_sc = k.v(lambda e: e.tensor_scalar(s8[:], s8[:], col(sc, 35), None, op0=ALU.mult))
            k.wait("vector", t_sc, t_gb, tb[0])
            t_bi = k.v(lambda e: e.tensor_tensor(col(sc, 36, 8), s8[:], gb[:], op=ALU.add))
            k.wait("vector", t_bi)
            t_m1 = k.v(lambda e: e.reduce_max(col(sc, 44), col(sc, 36, 8), axis=AX.X))
            k.wait("vector", t_m1)
            t_o1 = k.v(lambda e: e.tensor_scalar(col(sc, 45, 8), col(sc, 36, 8), col(sc, 44), None, op0=ALU.is_equal))
            k.wait("vector", t_o1)
            t_w1a = k.v(lambda e: e.tensor_tensor(col(sc, 53, 8), col(sc, 45, 8), s8[:], op=ALU.mult))
            k.wait("vector", t_w1a)
            t_w1 = k.v(lambda e: e.reduce_sum(col(sc, 61), col(sc, 53, 8), axis=AX.X))
            t_b2 = k.v(lambda e: e.scalar_tensor_tensor(
                out=col(sc, 36, 8), in0=col(sc, 45, 8), scalar=-1e30, in1=col(sc, 36, 8),
                op0=ALU.mult, op1=ALU.add))
            k.wait("vector", t_b2)
            t_m2 = k.v(lambda e: e.reduce_max(col(sc, 44), col(sc, 36, 8), axis=AX.X))
            k.wait("vector", t_m2)
            t_o2 = k.v(lambda e: e.tensor_scalar(col(sc, 36, 8), col(sc, 36, 8), col(sc, 44), None, op0=ALU.is_equal))
            k.wait("vector", t_o2)
            t_w2a = k.v(lambda e: e.tensor_tensor(col(sc, 53, 8), col(sc, 36, 8), s8[:], op=ALU.mult))
            k.wait("vector", t_w2a)
            t_w2 = k.v(lambda e: e.reduce_sum(col(sc, 62), col(sc, 53, 8), axis=AX.X))
            k.wait("vector", t_w1, t_w2)
            t_c1 = k.v(lambda e: e.tensor_scalar(col(sc, 53, 8), col(sc, 45, 8), col(sc, 61), None, op0=ALU.mult))
            k.wait("vector", t_c1)
            t_cw = k.v(lambda e, s=s: e.scalar_tensor_tensor(
                out=cw[:, s*8:(s+1)*8], in0=col(sc, 36, 8), scalar=col(sc, 62),
                in1=col(sc, 53, 8), op0=ALU.mult, op1=ALU.add))
            G["s8"] = [t_w2a, t_cw]
            G["cw"].append(t_cw)

        k.wait("vector", G["routed"])
        t_r0 = k.v(lambda e: e.memset(routed[:], 0.0))
        tok_routed = t_r0

        def wload(e_i, m_i, slot):
            off = (e_i*5 + m_i) * D * D
            src = ap2(IO["moeW"], off, [[D, 128], [128*D, 8], [1, D]])
            k.wait("gpsimd", G["w"][slot])
            return k.dgp(lambda e, src=src, slot=slot: e.dma_start(
                out=wslots[slot][:], in_=src), ch="w%d" % slot)
        wq = []
        for e_i in range(NE):
            for m_i in range(5):
                wq.append((e_i, m_i))
        wtok = {}
        for i in range(min(3, len(wq))):
            wtok[i] = wload(wq[i][0], wq[i][1], i % 3)
        def wget(i):
            if i not in wtok:
                wtok[i] = wload(wq[i][0], wq[i][1], i % 3)
            return wtok[i], wslots[i % 3]
        wi = 0
        for e_i in range(NE):
            def hmm(dst, tw, wsb, bias_sb, guard_key):
                toks = []
                for fc in range(8):
                    ps, gk = (psA, "psA") if fc % 2 == 0 else (psB, "psB")
                    k.wait("tensor", tw, G[gk], t_xlnF)
                    t_mm2 = None
                    for c in range(8):
                        t_mm2 = k.t(lambda e, fc=fc, c=c, ps=ps, wsb=wsb: e.matmul(
                            ps[:], wsb[:, (c*8+fc)*128:(c*8+fc+1)*128],
                            xlnF[:, c*512:(c+1)*512], start=(c == 0), stop=(c == 7)))
                    k.wait("scalar", t_mm2, G[guard_key], *tb)
                    t_ev2 = k.s(lambda e, fc=fc, ps=ps, dst=dst, bias_sb=bias_sb: e.activation(
                        out=dst[:, fc*512:(fc+1)*512], in_=ps[:], func=AF.Identity,
                        bias=col(bias_sb, e_i*8+fc)))
                    G[gk] = [t_ev2]
                    toks.append(t_ev2)
                return toks
            tw, wsb = wget(wi)
            tg1 = hmm(hb1, tw, wsb, b1g, "hb1")
            G["w"][wi % 3] = [("pe", k.cnt["pe"])]; wi += 1
            tw, wsb = wget(wi)
            tg3 = hmm(hb2, tw, wsb, b3g, "hb2")
            G["w"][wi % 3] = [("pe", k.cnt["pe"])]; wi += 1
            k.wait("vector", *tg1); k.wait("vector", *tg3)
            t_gp = k.v(lambda e: e.tensor_tensor(hb1[:], hb1[:], hb2[:], op=ALU.mult))
            k.wait("vector", t_gp)
            t_gc = k.v(lambda e: e.tensor_scalar_min(hb1[:], hb1[:], 7.0))
            k.wait("scalar", t_gc)
            t_sg = k.s(lambda e: e.activation(out=hb2[:], in_=hb1[:], func=AF.Silu, scale=1.702))
            G["hb1"] = []
            tw, wsb = wget(wi)
            tl1 = hmm(hb3, tw, wsb, b1l, "hb3")
            G["w"][wi % 3] = [("pe", k.cnt["pe"])]; wi += 1
            tw, wsb = wget(wi)
            k.wait("scalar", t_sg)
            tl3 = hmm(hb1, tw, wsb, b3l, "hb1")
            G["w"][wi % 3] = [("pe", k.cnt["pe"])]; wi += 1
            k.wait("vector", *tl1); k.wait("vector", *tl3)
            t_lp = k.v(lambda e: e.tensor_tensor(hb3[:], hb3[:], hb1[:], op=ALU.mult))
            k.wait("vector", t_lp)
            t_lc = k.v(lambda e: e.tensor_scalar(hb3[:], hb3[:], 7.0, -7.0, op0=ALU.min, op1=ALU.max))
            k.wait("vector", t_lc)
            t_la = k.v(lambda e: e.tensor_scalar(hb3[:], hb3[:], 1.0/1.702, 1.0/1.702, op0=ALU.mult, op1=ALU.add))
            k.wait("vector", t_la, t_sg, G["ab"])
            t_a = k.v(lambda e: e.tensor_tensor(ab[:], hb2[:], hb3[:], op=ALU.mult))
            G["hb2"] = [t_a]; G["hb3"] = [t_a]
            tw2, wsb2 = wget(wi)
            rtoks = []
            for tsub in range(4):
                for half in range(2):
                    ps, gk = (psA, "psA") if half == 0 else (psB, "psB")
                    k.wait("tensor", tw2, t_a, G[gk], t_on, *tb)
                    t_mm3 = None
                    for fc in range(8):
                        t_mm3 = k.t(lambda e, fc=fc, tsub=tsub, half=half, ps=ps, wsb2=wsb2: e.matmul(
                            ps[:], ab[:, fc*512 + tsub*128: fc*512 + (tsub+1)*128],
                            wsb2[:, fc*1024 + half*512: fc*1024 + (half+1)*512],
                            start=(fc == 0), stop=False))
                    t_mm3 = k.t(lambda e, half=half, ps=ps: e.matmul(
                        ps[:], ones1[:], b2r[0:1, e_i*1024 + half*512: e_i*1024 + (half+1)*512],
                        start=False, stop=True))
                    k.wait("vector", t_mm3, tok_routed, G["cw"])
                    if e_i < 8:
                        t_ru = k.v(lambda e, tsub=tsub, half=half, ps=ps: e.scalar_tensor_tensor(
                            out=routed[:, tsub*1024 + half*512: tsub*1024 + (half+1)*512],
                            in0=ps[:], scalar=col(cw, tsub*8 + e_i),
                            in1=routed[:, tsub*1024 + half*512: tsub*1024 + (half+1)*512],
                            op0=ALU.mult, op1=ALU.add))
                    else:
                        t_ru = k.v(lambda e, tsub=tsub, half=half, ps=ps: e.tensor_tensor(
                            routed[:, tsub*1024 + half*512: tsub*1024 + (half+1)*512],
                            routed[:, tsub*1024 + half*512: tsub*1024 + (half+1)*512],
                            ps[:], op=ALU.add))
                    G[gk] = [t_ru]
                    rtoks.append(t_ru)
            tok_routed = rtoks[-1]
            G["w"][wi % 3] = [("pe", k.cnt["pe"])]; wi += 1
            G["ab"] = [("pe", k.cnt["pe"])]
        G["cw"] = []
        G["xlnF"] = [("pe", k.cnt["pe"])]


        for tsub in range(4):
            k.wait("vector", tok_routed)
            t_ln2 = ln_tile(k, TS, routed, tsub*1024, D, sq[:, :D], sc, 8, TS["pb_sq2"], [tok_routed])
            k.wait("vector", t_ln2)
            t_x2 = k.v(lambda e, tsub=tsub: e.tensor_tensor(
                routed[:, tsub*1024:(tsub+1)*1024], sq[:, :D], xt[:, tsub*1024:(tsub+1)*1024], op=ALU.add))
            k.wait("sync", t_x2)
            dst = ap2(IO["x2"], (tc*512 + tsub*128)*D, [[D, 128], [1, D]])
            tr_st = k.dsync(lambda e, dst=dst, tsub=tsub: e.dma_start(out=dst, in_=routed[:, tsub*1024:(tsub+1)*1024]), ch="routed")
        G["routed"] = [tr_st]
        G["xt"] = G["xt"] + [("dv", k.cnt["dv"])]

PHASEC_SBUF = [
    ("pc_x", [128, 4*1024], F32),
    ("pa_xlnb", [128, 4*1024], BF16),
    ("pa_xlnF", [128, 8*512], BF16),
    ("pa_sq", [128, 1024], F32),
    ("pb_sq2", [128, 1024], F32),
    ("pa_sc", [128, 64], F32),
    ("ident", [128, 128], BF16),
    ("eps", [128, 1], F32),
    ("ones1", [1, 128], BF16),
    ("pc_w0", [128, 8*1024], BF16),
    ("pc_w1", [128, 8*1024], BF16),
    ("pc_w2", [128, 8*1024], BF16),
    ("pc_h1", [128, 8*512], F32),
    ("pc_h2", [128, 8*512], F32),
    ("pc_h3", [128, 8*512], F32),
    ("pc_a", [128, 8*512], BF16),
    ("pc_routed", [128, 4*1024], F32),
    ("pc_cw", [128, 4*8], F32),
    ("pc_s8", [128, 8], F32),
    ("pc_gb", [128, 8], F32),
    ("pc_gw", [128, 8*8], BF16),
    ("pc_b1g", [128, 9*8], F32),
    ("pc_b1l", [128, 9*8], F32),
    ("pc_b3g", [128, 9*8], F32),
    ("pc_b3l", [128, 9*8], F32),
    ("pc_b2r", [1, 9*1024], BF16),
]
PHASEC_PSUM = [
    ("ps_a", [128, 512], F32),
    ("ps_b", [128, 512], F32),
    ("ps_bfA", [128, 512], BF16),
]

def build_phaseC():
    io = [
        ("x1", [QT, D], F32, "ExternalInput"),
        ("moeW", [9*5*D, D], F32, "ExternalInput"),
        ("gate_wT", [D, 8], F32, "ExternalInput"),
        ("gate_b", [1, 8], F32, "ExternalInput"),
        ("b1g", [9, 128, 8], F32, "ExternalInput"),
        ("b1l", [9, 128, 8], F32, "ExternalInput"),
        ("b3g", [9, 128, 8], F32, "ExternalInput"),
        ("b3l", [9, 128, 8], F32, "ExternalInput"),
        ("b2all", [9, 1024], F32, "ExternalInput"),
        ("identw", [128, 128], BF16, "ExternalInput"),
        ("x2", [QT, D], F32, "ExternalOutput"),
    ]
    return build_program([emit_phaseC], io, PHASEC_SBUF, PHASEC_PSUM)

def host_moe_weights(z):
    blocks = []
    bs = {"b1g": [], "b1l": [], "b3g": [], "b3l": [], "b2": []}
    for e in range(8):
        w1 = z['L0_ffn_w1'][e]; w3 = z['L0_ffn_w3'][e]; w2 = z['L0_ffn_w2'][e]
        blocks += [w1[::2].T, w3[::2].T, w1[1::2].T, w3[1::2].T, w2.T]
        bs["b1g"].append(z['L0_ffn_b1'][e][::2]); bs["b1l"].append(z['L0_ffn_b1'][e][1::2])
        bs["b3g"].append(z['L0_ffn_b3'][e][::2]); bs["b3l"].append(z['L0_ffn_b3'][e][1::2])
        bs["b2"].append(z['L0_ffn_b2'][e])
    blocks += [z['L0_ffn_sw1'][::2].T, z['L0_ffn_sw3'][::2].T,
               z['L0_ffn_sw1'][1::2].T, z['L0_ffn_sw3'][1::2].T, z['L0_ffn_sw2'].T]
    bs["b1g"].append(z['L0_ffn_sb1'][::2]); bs["b1l"].append(z['L0_ffn_sb1'][1::2])
    bs["b3g"].append(z['L0_ffn_sb3'][::2]); bs["b3l"].append(z['L0_ffn_sb3'][1::2])
    bs["b2"].append(z['L0_ffn_sb2'])
    moeW = np.ascontiguousarray(np.concatenate(blocks, 0)).astype(np.float32)
    out = {"moeW": moeW,
           "gate_wT": np.ascontiguousarray(z['L0_ffn_gate_w'].T),
           "gate_b": z['L0_ffn_gate_b'].reshape(1, 8).astype(np.float32)}
    for key in ("b1g", "b1l", "b3g", "b3l"):
        arr = np.stack(bs[key]).reshape(9, 8, 128).transpose(0, 2, 1)
        out[key] = np.ascontiguousarray(arr).astype(np.float32)
    out["b2all"] = np.ascontiguousarray(np.stack(bs["b2"])).astype(np.float32)
    return out

def emit_phaseD(k, TS, IO):
    xt = TS["pd_x"]; xlnb = TS["pa_xlnb"]; xlnF = TS["pa_xlnF"]
    sq, sc = TS["pa_sq"], TS["pa_sc"]
    ident = TS["ident"]
    psT, psA, psB = TS["ps_bfA"], TS["ps_a"], TS["ps_b"]
    wx = TS["pd_wx"]; wz = TS["pd_wz"]; wdtu = TS["pd_wdtu"]; wdt = TS["pd_wdt"]
    wbp = TS["pd_wbp"]; wcp = TS["pd_wcp"]
    dtu = TS["pd_dtu"]
    zb = TS["pd_zb"]; zb2 = TS["pd_zb2"]
    const1 = TS["const1"]
    bdt = TS["pd_bdt"]; convw = TS["pd_convw"]; Dpar = TS["pd_dpar"]
    alog = TS["pd_alog"]; absA = TS["pd_absA"]; rhat = TS["pd_rhat"]; mmask = TS["pd_m"]
    sA = TS["pd_sA"]; latB = TS["pd_latB"]; latG = TS["pd_latG"]
    T1 = TS["pd_T1"]; T2 = TS["pd_T2"]
    bpe = TS["pd_bpe"]; cpe = TS["pd_cpe"]
    hst = TS["pd_h"]
    xrp = [TS["pd_xrp0"], TS["pd_xrp1"]]
    cv = TS["pd_cv"]
    ub = [TS["pd_u0"], TS["pd_u1"]]
    dtt = [TS["pd_dt0"], TS["pd_dt1"]]
    szt = [TS["pd_sz0"], TS["pd_sz1"]]
    yb = TS["pd_y"]
    gz = TS["pd_gz"]

    t_id = k.dgp(lambda e: e.dma_start(out=ident[:], in_=IO["identw"][:, :]), ch="id")
    t_c1 = k.v(lambda e: e.memset(const1[:], 1.0))
    t_wx = load_wT(k, IO["WxTh"], wx, D, 512, ch="wk")
    t_wz = load_wT(k, IO["WzTh"], wz, D, 512, ch="wv")
    t_wdtu = load_wT(k, IO["WdtuT"], wdtu, D, 64, ch="wdtu")
    t_wdt = load_wT(k, IO["WdtTh"], wdt, 64, 512, ch="wdt")
    t_wbp = load_wT(k, IO["WpBT"], wbp, D, 16, ch="wbp")
    t_wcp = load_wT(k, IO["WpCT"], wcp, D, 16, ch="wcp")
    t_bdt = k.dsync(lambda e: e.dma_start(out=bdt[:], in_=ap2(IO["bdt_h"], 0, [[4, 128], [1, 4]])), ch="bias")
    t_cw = k.dsync(lambda e: e.dma_start(out=convw[:], in_=ap2(IO["convw_h"], 0, [[16, 128], [1, 16]])), ch="bias")
    t_dp = k.dsync(lambda e: e.dma_start(out=Dpar[:], in_=ap2(IO["D_h"], 0, [[4, 128], [1, 4]])), ch="bias")
    t_al = k.dsync(lambda e: e.dma_start(out=alog[:], in_=ap2(IO["Alog_h"], 0, [[16, 128], [128*16, 4], [1, 16]])), ch="bias")
    k.wait("scalar", t_al)
    t_aa = k.s(lambda e: e.activation(out=absA[:], in_=alog[:], func=AF.Exp))
    k.wait("vector", t_aa)
    t_den = k.v(lambda e: e.tensor_scalar(zb[:, 0:64], absA[:], -1.0, 1e-10, op0=ALU.mult, op1=ALU.add))
    k.wait("vector", t_den)
    t_r0 = k.v(lambda e: e.reciprocal(zb2[:, 0:64], zb[:, 0:64]))
    t_mm0a = k.v(lambda e: e.tensor_scalar(mmask[:], absA[:], -1.0, 1e-8, op0=ALU.mult, op1=ALU.add))
    k.wait("scalar", t_mm0a)
    t_mm0b = k.s(lambda e: e.activation(out=mmask[:], in_=mmask[:], func=AF.Sign))
    k.wait("vector", t_mm0b)
    t_mm0 = k.v(lambda e: e.tensor_scalar_max(mmask[:], mmask[:], 0.0))
    k.wait("vector", t_r0, t_mm0)
    t_1m = k.v(lambda e: e.tensor_scalar(zb[:, 0:64], mmask[:], -1.0, 1.0, op0=ALU.mult, op1=ALU.add))
    k.wait("vector", t_1m)
    t_rh = k.v(lambda e: e.tensor_tensor(rhat[:], zb2[:, 0:64], zb[:, 0:64], op=ALU.mult))

    fused = IO.get("_fused", False)
    if fused:
        _bmap = {}
        for p in (0, 1):
            for sp, j in enumerate(core_blocks(p)):
                _bmap[j] = p*QT + sp*256
        def rowmap(grow):
            j = grow // 256
            return _bmap[j] + (grow % 256)
    else:
        def rowmap(grow):
            return grow

    G = {"xt": [], "xlnb": [], "xlnF": [], "psT": [], "psA": [], "psB": [],
         "dtu": [], "zb": [], "zb2": [t_rh], "bpe": [], "cpe": [],
         "sA": [], "latB": [], "latG": [], "T1": [], "T2": [],
         "xrp": [[], []], "cv": [], "u": [[], []], "dt": [[], []], "sz": [[], []],
         "y": [], "gz": [], "hs": []}
    t_h0 = k.v(lambda e: e.memset(hst[:], 0.0))
    hs_tok = [t_h0]
    Tc = 256

    for tc in range(16):
        k.wait("sync", G["xt"])
        t_xs = []
        for s4 in range(4):
            src = ap2(IO["x2f"], rowmap(tc*512 + s4*128)*D, [[D, 128], [1, D]])
            t_xs.append(k.dsync(lambda e, src=src, s4=s4: e.dma_start(
                out=xt[:, s4*1024:(s4+1)*1024], in_=src), ch="xt"))
        t_x = t_xs[-1]
        toks_ln = []
        for s in range(4):
            k.wait("vector", G["xlnb"])
            t_ln = ln_tile(k, TS, xt, s*D, D, xlnb[:, s*D:(s+1)*D], sc, 8*s, sq, [t_x])
            toks_ln.append(t_ln)
        G["xt"] = toks_ln[:] + [("ac", k.cnt["ac"])]
        k.wait("tensor", t_id, *toks_ln)
        t_ev = None
        for c in range(8):
            k.wait("tensor", G["psT"], G["xlnF"] if c == 0 else None, t_ev)
            t_tr = None
            for s in range(4):
                t_tr = k.t(lambda e, c=c, s=s: e.transpose(
                    psT[:, s*128:(s+1)*128], xlnb[:, s*D + c*128: s*D + (c+1)*128], ident[:]))
            k.wait("vector", t_tr)
            t_ev = k.v(lambda e, c=c: e.tensor_copy(xlnF[:, c*512:(c+1)*512], psT[:]))
        G["psT"] = [t_ev]
        G["xlnb"] = [("pe", k.cnt["pe"])]
        t_xlnF = t_ev
        for wi_, wsb, twl, outdram, func in ((0, wx, t_wx, "xres", AF.Identity),
                                             (1, wz, t_wz, "sz", AF.Silu)):
            for dc in range(4):
                ps, gk = (psA, "psA") if dc % 2 == 0 else (psB, "psB")
                eb, ebk = (zb, "zb") if (wi_*4 + dc) % 2 == 0 else (zb2, "zb2")
                k.wait("tensor", twl, t_xlnF, G[gk])
                t_mm = None
                for c in range(8):
                    t_mm = k.t(lambda e, c=c, dc=dc, ps=ps, wsb=wsb: e.matmul(
                        ps[:], wsb[:, c*512 + dc*128: c*512 + (dc+1)*128],
                        xlnF[:, c*512:(c+1)*512], start=(c == 0), stop=(c == 7)))
                k.wait("scalar", t_mm, G[ebk])
                t_ev2 = k.s(lambda e, ps=ps, func=func, eb=eb: e.activation(out=eb[:, 0:512], in_=ps[:], func=func))
                G[gk] = [t_ev2]
                k.wait("sync", t_ev2)
                dst = ap2(IO[outdram], (dc*128)*T + tc*512, [[T, 128], [1, 512]])
                tz_st = k.dsync(lambda e, dst=dst, eb=eb: e.dma_start(out=dst, in_=eb[:, 0:512]), ch=ebk)
                G[ebk] = [tz_st]
        k.wait("tensor", t_wdtu, G["psA"])
        t_mm = None
        for c in range(8):
            t_mm = k.t(lambda e, c=c: e.matmul(
                psA[0:64, :], wdtu[:, c*64:(c+1)*64], xlnF[:, c*512:(c+1)*512],
                start=(c == 0), stop=(c == 7)))
        k.wait("vector", t_mm, G["dtu"])
        t_dtu = k.v(lambda e: e.tensor_copy(dtu[:], psA[0:64, :]))
        G["psA"] = [t_dtu]
        for dc in range(4):
            ps, gk = (psB, "psB") if dc % 2 == 0 else (psA, "psA")
            eb, ebk = (zb, "zb") if dc % 2 == 0 else (zb2, "zb2")
            k.wait("tensor", t_wdt, t_dtu, G[gk])
            t_mm = k.t(lambda e, dc=dc, ps=ps: e.matmul(
                ps[:], wdt[0:64, dc*128:(dc+1)*128], dtu[:], start=True, stop=True))
            k.wait("scalar", t_mm, t_bdt, G[ebk], G["cv"])
            t_z0 = k.s(lambda e, dc=dc, ps=ps, eb=eb: e.activation(out=eb[:, 0:512], in_=ps[:],
                                                            func=AF.Identity, bias=col(bdt, dc)))
            G[gk] = [t_z0]
            k.wait("vector", t_z0)
            t_m0 = k.v(lambda e, eb=eb: e.tensor_scalar_max(cv[:, 0:512], eb[:, 0:512], 0.0))
            k.wait("scalar", t_m0)
            t_ab = k.s(lambda e, eb=eb: e.activation(out=eb[:, 0:512], in_=eb[:, 0:512], func=AF.Abs))
            k.wait("scalar", t_ab)
            t_e0 = k.s(lambda e, eb=eb: e.activation(out=eb[:, 0:512], in_=eb[:, 0:512], func=AF.Exp, scale=-1.0))
            k.wait("scalar", t_e0, t_c1)
            t_l0 = k.s(lambda e, eb=eb: e.activation(out=eb[:, 0:512], in_=eb[:, 0:512], func=AF.Ln, bias=col(const1, 0)))
            k.wait("vector", t_l0, t_m0)
            t_dt = k.v(lambda e, eb=eb: e.tensor_tensor(eb[:, 0:512], cv[:, 0:512], eb[:, 0:512], op=ALU.add))
            G["cv"] = [t_dt]
            k.wait("sync", t_dt)
            dst = ap2(IO["dtd"], (dc*128)*T + tc*512, [[T, 128], [1, 512]])
            tz_st = k.dsync(lambda e, dst=dst, eb=eb: e.dma_start(out=dst, in_=eb[:, 0:512]), ch=ebk)
            G[ebk] = [tz_st]
        G["dtu"] = [("pe", k.cnt["pe"])]
        for wsb, twl, outdram in ((wbp, t_wbp, "bpd"), (wcp, t_wcp, "cpd")):
            for s in range(4):
                ps, gk = (psA, "psA") if s % 2 == 0 else (psB, "psB")
                k.wait("tensor", twl, G[gk])
                t_mm = None
                for c in range(8):
                    t_mm = k.t(lambda e, c=c, s=s, ps=ps, wsb=wsb: e.matmul(
                        ps[:, 0:16], xlnF[:, c*512 + s*128: c*512 + (s+1)*128],
                        wsb[:, c*16:(c+1)*16], start=(c == 0), stop=(c == 7)))
                k.wait("vector", t_mm, G["y"])
                t_ev3 = k.v(lambda e, ps=ps: e.tensor_copy(yb[:, 0:16], ps[:, 0:16]))
                G[gk] = [t_ev3]
                k.wait("sync", t_ev3)
                dst = ap2(IO[outdram], (tc*512 + s*128)*16, [[16, 128], [1, 16]])
                ty_st = k.dsync(lambda e, dst=dst: e.dma_start(out=dst, in_=yb[:, 0:16]), ch="yb")
                G["y"] = [ty_st]
        G["xlnF"] = [("pe", k.cnt["pe"])]
        tok_d1 = [("q%d" % i, k.cnt.get("q%d" % i, 0)) for i in k.chan.values()]
        if IO.get("_d2bar", False):
            k.barrier()

        for tci in (2*tc, 2*tc + 1):
            k.wait("sync", G["bpe"], tok_d1)
            t_bpe = k.dsync(lambda e, tci=tci: e.dma_start(
                out=bpe[:], in_=ap2(IO["bpd"], tci*Tc*16, [[0, 128], [1, Tc*16]])), ch="bpe")
            k.wait("sync", G["cpe"])
            t_cpe = k.dsync(lambda e, tci=tci: e.dma_start(
                out=cpe[:], in_=ap2(IO["cpd"], tci*Tc*16, [[0, 128], [1, Tc*16]])), ch="cpe")
            for dc in range(4):
                pb = dc % 2
                k.wait("sync", G["xrp"][pb])
                if tci == 0:
                    t_z3 = k.v(lambda e, pb=pb: e.memset(xrp[pb][:, 0:3], 0.0))
                    k.wait("sync", t_z3)
                    t_xr = k.dsync(lambda e, dc=dc, pb=pb: e.dma_start(
                        out=xrp[pb][:, 3:3+Tc], in_=ap2(IO["xres"], dc*128*T, [[T, 128], [1, Tc]])), ch="xrp%d" % pb)
                else:
                    t_xr = k.dsync(lambda e, dc=dc, tci=tci, pb=pb: e.dma_start(
                        out=xrp[pb][:], in_=ap2(IO["xres"], dc*128*T + tci*Tc - 3, [[T, 128], [1, Tc+3]])), ch="xrp%d" % pb)
                k.wait("sync", G["dt"][pb])
                t_dt2 = k.dsync(lambda e, dc=dc, tci=tci, pb=pb: e.dma_start(
                    out=dtt[pb][:], in_=ap2(IO["dtd"], dc*128*T + tci*Tc, [[T, 128], [1, Tc]])), ch="dtt%d" % pb)
                k.wait("sync", G["sz"][pb])
                t_sz = k.dsync(lambda e, dc=dc, tci=tci, pb=pb: e.dma_start(
                    out=szt[pb][:], in_=ap2(IO["sz"], dc*128*T + tci*Tc, [[T, 128], [1, Tc]])), ch="szt%d" % pb)
                k.wait("vector", t_xr, t_cw, G["cv"])
                t_c0 = k.v(lambda e, dc=dc, pb=pb: e.tensor_scalar(cv[:, 0:Tc], xrp[pb][:, 0:Tc], col(convw, dc*4+0), None, op0=ALU.mult))
                tprev = t_c0
                for tap in range(1, 4):
                    k.wait("vector", tprev)
                    tprev = k.v(lambda e, dc=dc, tap=tap, pb=pb: e.scalar_tensor_tensor(
                        out=cv[:, 0:Tc], in0=xrp[pb][:, tap:tap+Tc], scalar=col(convw, dc*4+tap),
                        in1=cv[:, 0:Tc], op0=ALU.mult, op1=ALU.add))
                G["xrp"][pb] = [tprev]
                k.wait("scalar", tprev, G["u"][pb])
                t_u = k.s(lambda e, pb=pb: e.activation(out=ub[pb][:], in_=cv[:, 0:Tc], func=AF.Silu))
                G["cv"] = [t_u]
                dt_b = ap2(dtt[pb].ap(), 0, [[Tc, 128], [1, Tc], [0, 16]])
                aA_b = ap2(absA.ap(), dc*16, [[64, 128], [0, Tc], [1, 16]])
                rh_b = ap2(rhat.ap(), dc*16, [[64, 128], [0, Tc], [1, 16]])
                mm_b = ap2(mmask.ap(), dc*16, [[64, 128], [0, Tc], [1, 16]])
                u_b = ap2(ub[pb].ap(), 0, [[Tc, 128], [1, Tc], [0, 16]])
                k.wait("vector", t_dt2, t_aa, G["sA"])
                t_lA = k.v(lambda e, dt_b=dt_b, aA_b=aA_b: e.tensor_tensor(sA[:], dt_b, aA_b, op=ALU.mult))
                k.wait("vector", t_rh, G["latG"])
                t_lG = k.v(lambda e, mm_b=mm_b, dt_b=dt_b: e.tensor_tensor(latG[:], mm_b, dt_b, op=ALU.mult))
                k.wait("scalar", t_lA, G["latB"])
                t_lB = k.s(lambda e: e.activation(out=latB[:], in_=sA[:], func=AF.Exp, scale=-1.0))
                G["sA"] = [t_lB]
                k.wait("vector", t_lB, t_rh, G["T1"])
                t_1 = k.v(lambda e, rh_b=rh_b: e.scalar_tensor_tensor(
                    out=T1[:], in0=latB[:], scalar=-1.0, in1=rh_b, op0=ALU.add, op1=ALU.mult))
                k.wait("vector", t_lG)
                t_2 = k.v(lambda e: e.tensor_tensor(sA[:], T1[:], latG[:], op=ALU.add))
                G["latG"] = [t_2]
                k.wait("vector", t_u)
                t_3 = k.v(lambda e, u_b=u_b: e.tensor_tensor(T1[:], sA[:], u_b, op=ALU.mult))
                k.wait("vector", t_bpe)
                t_4 = k.v(lambda e: e.tensor_tensor(sA[:], T1[:], bpe[:], op=ALU.mult))
                k.wait("vector", t_4, *hs_tok)
                k.wait("vector", G["T2"])
                t_sc2 = None
                for n in range(16):
                    a_v = ap2(latB.ap(), n, [[Tc*16, 128], [16, Tc]])
                    x_v = ap2(sA.ap(), n, [[Tc*16, 128], [16, Tc]])
                    o_v = ap2(T2.ap(), n, [[Tc*16, 128], [16, Tc]])
                    t_sc2 = k.v(lambda e, a_v=a_v, x_v=x_v, o_v=o_v, dc=dc: e.tensor_tensor_scan(
                        out=o_v, data0=a_v, data1=x_v, initial=col(hst, dc*16+n),
                        op0=ALU.mult, op1=ALU.add))
                G["latB"] = [t_sc2]
                G["T1"] = [t_sc2]
                G["sA"] = [t_sc2]
                k.wait("vector", t_sc2)
                t_hs = k.v(lambda e, dc=dc: e.tensor_copy(hst[:, dc*16:(dc+1)*16], T2[:, (Tc-1)*16:Tc*16]))
                hs_tok = [t_hs]
                k.wait("vector", t_cpe, t_hs)
                t_pr = k.v(lambda e: e.tensor_tensor(T1[:], T2[:], cpe[:], op=ALU.mult))
                G["T1"] = [t_pr]
                k.wait("vector", t_pr, G["y"])
                in3 = ap2(T1.ap(), 0, [[Tc*16, 128], [16, Tc], [1, 16]])
                t_y = k.v(lambda e, in3=in3: e.tensor_reduce(yb[:, 0:Tc], in3, axis=AX.X, op=ALU.add))
                G["T2"] = [t_y]
                k.wait("vector", t_y, t_dp, t_u)
                t_yd = k.v(lambda e, dc=dc, pb=pb: e.scalar_tensor_tensor(
                    out=yb[:, 0:Tc], in0=ub[pb][:], scalar=col(Dpar, dc), in1=yb[:, 0:Tc],
                    op0=ALU.mult, op1=ALU.add))
                G["u"][pb] = [t_yd]
                if IO.get("_dbgD2", False):
                    k.wait("sync", t_lA, t_lG, t_yd if False else t_3)
                    ddst = ap2(IO["dt_rd"], dc*128*T + tci*Tc, [[T, 128], [1, Tc]])
                    k.dsync(lambda e, ddst=ddst, pb=pb: e.dma_start(out=ddst, in_=dtt[pb][:]), ch="dbg")
                    udst = ap2(IO["u_rd"], dc*128*T + tci*Tc, [[T, 128], [1, Tc]])
                    k.dsync(lambda e, udst=udst, pb=pb: e.dma_start(out=udst, in_=ub[pb][:]), ch="dbg")
                if IO.get("_dbgD", False) and tci == 0 and dc == 0:
                    k.wait("sync", t_lB, t_4, t_sc2, t_hs)
                    k.dsync(lambda e: e.dma_start(out=IO["dbg_lB"][:, :], in_=latB[:]), ch="dbg")
                    k.dsync(lambda e: e.dma_start(out=IO["dbg_T1"][:, :], in_=T1[:]), ch="dbg")
                    k.dsync(lambda e: e.dma_start(out=IO["dbg_T2"][:, :], in_=T2[:]), ch="dbg")
                if IO.get("_dbgD", False):
                    k.wait("sync", t_u if True else None, t_yd)
                    dstu = ap2(IO["ud"], dc*128*T + tci*Tc, [[T, 128], [1, Tc]])
                    k.dsync(lambda e, dstu=dstu, pb=pb: e.dma_start(out=dstu, in_=ub[pb][:]), ch="dbg")
                    dsty = ap2(IO["yd_d"], dc*128*T + tci*Tc, [[T, 128], [1, Tc]])
                    k.dsync(lambda e, dsty=dsty: e.dma_start(out=dsty, in_=yb[:, 0:Tc]), ch="dbg")
                k.wait("vector", t_yd, t_sz, G["gz"])
                t_g = k.v(lambda e, pb=pb: e.tensor_tensor(gz[:], yb[:, 0:Tc], szt[pb][:], op=ALU.mult))
                G["sz"][pb] = [t_g]; G["y"] = [t_g]; G["dt"][pb] = [t_lA, t_lG]
                k.wait("sync", t_g)
                if fused:
                    p = (min(tci, NB - 1 - tci) % 2)
                    sp = core_blocks(p).index(tci)
                    dst = ap2(IO["gh"], dc*128*T + p*QT + sp*256, [[T, 128], [1, Tc]])
                else:
                    dst = ap2(IO["gh"], dc*128*T + tci*Tc, [[T, 128], [1, Tc]])
                tg_st = k.dsync(lambda e, dst=dst: e.dma_start(out=dst, in_=gz[:]), ch="gz")
                G["gz"] = [tg_st]
                if IO.get("_unitbar", False):
                    k.barrier()
            G["bpe"] = [("dv", k.cnt["dv"])]
            G["cpe"] = [("dv", k.cnt["dv"])]

PHASED_SBUF = [
    ("pd_x", [128, 4*1024], F32),
    ("pa_xlnb", [128, 4*1024], BF16),
    ("pa_xlnF", [128, 8*512], BF16),
    ("pa_sq", [128, 1024], F32),
    ("pa_sc", [128, 64], F32),
    ("ident", [128, 128], BF16),
    ("eps", [128, 1], F32),
    ("const1", [128, 1], F32),
    ("pd_wx", [128, 8*512], BF16),
    ("pd_wz", [128, 8*512], BF16),
    ("pd_wdtu", [128, 8*64], BF16),
    ("pd_wdt", [64, 512], BF16),
    ("pd_wbp", [128, 8*16], BF16),
    ("pd_wcp", [128, 8*16], BF16),
    ("pd_dtu", [64, 512], BF16),
    ("pd_zb", [128, 512], F32),
    ("pd_zb2", [128, 512], F32),
    ("pd_bdt", [128, 4], F32),
    ("pd_convw", [128, 16], F32),
    ("pd_dpar", [128, 4], F32),
    ("pd_alog", [128, 64], F32),
    ("pd_absA", [128, 64], F32),
    ("pd_rhat", [128, 64], F32),
    ("pd_m", [128, 64], F32),
    ("pd_sA", [128, 4096], F32),
    ("pd_latB", [128, 4096], F32),
    ("pd_latG", [128, 4096], F32),
    ("pd_T1", [128, 4096], F32),
    ("pd_T2", [128, 4096], F32),
    ("pd_bpe", [128, 4096], F32),
    ("pd_cpe", [128, 4096], F32),
    ("pd_h", [128, 64], F32),
    ("pd_xrp0", [128, 259], F32),
    ("pd_xrp1", [128, 259], F32),
    ("pd_cv", [128, 512], F32),
    ("pd_u0", [128, 256], F32),
    ("pd_u1", [128, 256], F32),
    ("pd_dt0", [128, 256], F32),
    ("pd_dt1", [128, 256], F32),
    ("pd_sz0", [128, 256], F32),
    ("pd_sz1", [128, 256], F32),
    ("pd_y", [128, 256], F32),
    ("pd_gz", [128, 256], BF16),
]
PHASED_PSUM = [
    ("ps_a", [128, 512], F32),
    ("ps_b", [128, 512], F32),
    ("ps_bfA", [128, 512], BF16),
]

def build_phaseD(dbg=False):
    io = [
        ("x2f", [T, D], F32, "ExternalInput"),
        ("WxTh", [D, 512], F32, "ExternalInput"),
        ("WzTh", [D, 512], F32, "ExternalInput"),
        ("WdtuT", [D, 64], F32, "ExternalInput"),
        ("WdtTh", [64, 512], F32, "ExternalInput"),
        ("WpBT", [D, 16], F32, "ExternalInput"),
        ("WpCT", [D, 16], F32, "ExternalInput"),
        ("bdt_h", [128, 4], F32, "ExternalInput"),
        ("convw_h", [128, 16], F32, "ExternalInput"),
        ("D_h", [128, 4], F32, "ExternalInput"),
        ("Alog_h", [4, 128, 16], F32, "ExternalInput"),
        ("identw", [128, 128], BF16, "ExternalInput"),
        ("xres", [512, T], F32, "ExternalOutput" if dbg else "Internal"),
        ("sz", [512, T], F32, "ExternalOutput" if dbg else "Internal"),
        ("dtd", [512, T], F32, "ExternalOutput" if dbg else "Internal"),
        ("bpd", [T*16], F32, "ExternalOutput" if dbg else "Internal"),
        ("cpd", [T*16], F32, "ExternalOutput" if dbg else "Internal"),
        ("gh", [512, T], BF16, "ExternalOutput"),
    ]
    if dbg == 2:
        io.append(("dt_rd", [512, T], F32, "ExternalOutput"))
        io.append(("u_rd", [512, T], F32, "ExternalOutput"))
        return build_program([lambda k, TS, IO: emit_phaseD(k, TS, dict(IO, _dbgD2=True))], io, PHASED_SBUF, PHASED_PSUM)
    if dbg:
        io.append(("ud", [512, T], F32, "ExternalOutput"))
        io.append(("yd_d", [512, T], F32, "ExternalOutput"))
        io.append(("dbg_lB", [128, 4096], F32, "ExternalOutput"))
        io.append(("dbg_T1", [128, 4096], F32, "ExternalOutput"))
        io.append(("dbg_T2", [128, 4096], F32, "ExternalOutput"))
    nc = build_program([emit_phaseD], io, PHASED_SBUF, PHASED_PSUM) if not dbg else None
    if dbg:
        return build_program([lambda k, TS, IO: emit_phaseD(k, TS, dict(IO, _dbgD=True))], io, PHASED_SBUF, PHASED_PSUM)
    return nc

def host_phaseD_weights(z, h):
    dh = slice(h*512, (h+1)*512)
    Wp = z['L1_attn_W_params']
    out = {
        "WxTh": np.ascontiguousarray(z['L1_attn_W_x'][dh].T),
        "WzTh": np.ascontiguousarray(z['L1_attn_W_z'][dh].T),
        "WdtuT": np.ascontiguousarray(Wp[:64].T),
        "WdtTh": np.ascontiguousarray(z['L1_attn_W_dt'][dh].T),
        "WpBT": np.ascontiguousarray(Wp[64:80].T),
        "WpCT": np.ascontiguousarray(Wp[80:96].T),
        "bdt_h": np.ascontiguousarray(z['L1_attn_b_dt'][dh].reshape(4, 128).T),
        "convw_h": np.ascontiguousarray(
            z['L1_attn_conv_w'][dh, 0, :].reshape(4, 128, 4).transpose(1, 0, 2).reshape(128, 16)),
        "D_h": np.ascontiguousarray(z['L1_attn_D_param'][dh].reshape(4, 128).T),
        "Alog_h": np.ascontiguousarray(z['L1_attn_A_log'][dh].reshape(4, 128, 16)),
    }
    return out

def emit_phaseE(k, TS, IO):
    gch = TS["pe_g"]
    gcA = TS["pe_gA"]; gcB = TS["pe_gB"]
    gsel = TS["pe_gsel"]
    wo = TS["pe_wo"]
    sq, sc = TS["pa_sq"], TS["pa_sc"]
    at = TS["pe_at"]
    xrow = TS["pb_xrow"]
    psA, psB = TS["ps_a"], TS["ps_b"]
    t_wo = load_wT(k, IO["WoutT"], wo, D, D, ch="wk")
    G = {"g": [], "at": [], "xrow": [], "psA": [], "psB": [], "gsel": []}
    t_gs = None
    if IO.get("_fused", False):
        t_gs = k.dsync(lambda e: e.dma_start(out=gsel[:], in_=ap2(IO["gselw"], 0, [[0, 128], [1, 2]])), ch="gb")
    fused = IO.get("_fused", False)
    for tc in range(8):
        if fused:
            k.wait("sync", G["g"])
            srcA = ap2(IO["gf"], tc*512, [[T, 128], [128*T, 8], [1, 512]])
            srcB = ap2(IO["gf"], QT + tc*512, [[T, 128], [128*T, 8], [1, 512]])
            t_gA = k.dsync(lambda e, srcA=srcA: e.dma_start(out=gcA[:], in_=srcA), ch="geA")
            t_gB = k.dsync(lambda e, srcB=srcB: e.dma_start(out=gcB[:], in_=srcB), ch="geB")
            k.wait("vector", t_gA, t_gB, G["gsel"], t_gs)
            t_m1 = k.v(lambda e: e.tensor_scalar(gch[:], gcA[:], col(gsel, 0), None, op0=ALU.mult))
            k.wait("vector", t_m1)
            t_g = k.v(lambda e: e.scalar_tensor_tensor(
                out=gch[:], in0=gcB[:], scalar=col(gsel, 1), in1=gch[:],
                op0=ALU.mult, op1=ALU.add))
            G["gsel"] = [t_g]
        else:
            src = ap2(IO["gf"], tc*512, [[QT, 128], [128*QT, 8], [1, 512]])
            k.wait("sync", G["g"])
            t_g = k.dsync(lambda e, src=src: e.dma_start(out=gch[:], in_=src), ch="geA")
        for s in range(4):
            t_evs = []
            for half in range(2):
                ps, gk = (psA, "psA") if half == 0 else (psB, "psB")
                k.wait("tensor", t_g, t_wo, G[gk])
                t_mm = None
                for c in range(8):
                    t_mm = k.t(lambda e, c=c, s=s, half=half, ps=ps: e.matmul(
                        ps[:], gch[:, c*512 + s*128: c*512 + (s+1)*128],
                        wo[:, c*1024 + half*512: c*1024 + (half+1)*512],
                        start=(c == 0), stop=(c == 7)))
                k.wait("scalar", t_mm, G["at"])
                t_ev = k.s(lambda e, half=half, ps=ps: e.activation(
                    out=at[:, half*512:(half+1)*512], in_=ps[:], func=AF.Identity, scale=1.0))
                G[gk] = [t_ev]
                t_evs.append(t_ev)
            t_ln = ln_tile(k, TS, at, 0, D, sq[:, :D], sc, 8, TS["pb_sq2"], t_evs)
            src2 = ap2(IO["x2rows"], (tc*512 + s*128)*D, [[D, 128], [1, D]])
            k.wait("sync", G["xrow"])
            t_xr = k.dsync(lambda e, src2=src2: e.dma_start(out=xrow[:], in_=src2), ch="xrow")
            k.wait("vector", t_ln, t_xr)
            t_x3 = k.v(lambda e: e.tensor_tensor(at[:, 0:D], sq[:, :D], xrow[:], op=ALU.add))
            G["xrow"] = [t_x3]
            k.wait("sync", t_x3)
            dst = ap2(IO["x3"], (tc*512 + s*128)*D, [[D, 128], [1, D]])
            ta_st = k.dsync(lambda e, dst=dst: e.dma_start(out=dst, in_=at[:, 0:D]), ch="at")
            G["at"] = [ta_st]
        G["g"] = [("pe", k.cnt["pe"])]

PHASEE_SBUF = [
    ("pe_g", [128, 8*512], BF16),
    ("pe_gA", [128, 8*512], BF16),
    ("pe_gB", [128, 8*512], BF16),
    ("pe_gsel", [128, 2], F32),
    ("pe_wo", [128, 8*1024], BF16),
    ("pe_at", [128, 1024], F32),
    ("pb_xrow", [128, 1024], F32),
    ("pa_sq", [128, 1024], F32),
    ("pb_sq2", [128, 1024], F32),
    ("pa_sc", [128, 64], F32),
    ("eps", [128, 1], F32),
]
PHASEE_PSUM = [
    ("ps_a", [128, 512], F32),
    ("ps_b", [128, 512], F32),
]

def build_phaseE():
    io = [
        ("gf", [D, QT], BF16, "ExternalInput"),
        ("WoutT", [D, D], F32, "ExternalInput"),
        ("x2rows", [QT, D], F32, "ExternalInput"),
        ("x3", [QT, D], F32, "ExternalOutput"),
    ]
    return build_program([emit_phaseE], io, PHASEE_SBUF, PHASEE_PSUM)

def emit_phaseF(k, TS, IO):
    xt = TS["pf_x"]; xlnb = TS["pa_xlnb"]; xlnF = TS["pa_xlnF"]
    sq, sc = TS["pa_sq"], TS["pa_sc"]
    xrow = TS["pb_xrow"]
    ident = TS["ident"]
    psT, psA, psB = TS["ps_bfA"], TS["ps_a"], TS["ps_b"]
    wsm = [TS["pf_wsm0"], TS["pf_wsm1"]]
    wmed = [TS["pf_wm0"], TS["pf_wm0"]]
    pF = [TS["pf_p1"], TS["pf_p2"], TS["pf_p3"]]
    s1 = TS["pf_s1"]; s2 = TS["pf_s2"]; sh1 = TS["pf_sh1"]; sh2 = TS["pf_sh2"]
    s3e = TS["pf_s3e"]
    yt = TS["pf_y"]

    t_id = k.dgp(lambda e: e.dma_start(out=ident[:], in_=IO["identw"][:, :]), ch="id")
    t_s = []
    for name, sb_t in (("s1", s1), ("s2", s2), ("sh1", sh1), ("sh2", sh2)):
        t_s.append(k.dsync(lambda e, sb_t=sb_t, name=name: e.dma_start(
            out=sb_t[:], in_=ap2(IO[name], 0, [[32, 128], [1, 32]])), ch="bias"))
    t_s3 = k.dsync(lambda e: e.dma_start(out=s3e[:], in_=ap2(IO["s3"], 0, [[0, 128], [1, 1024]])), ch="b2r")

    G = {"xt": [], "xlnb": [], "xlnF": [], "psT": [], "psA": [], "psB": [],
         "wmed": [[]], "wsm": [[], []], "p1": [], "p2": [], "p3": [], "y": [], "xrow": []}

    for tc in range(8):
        toks_ln = []
        for s in range(4):
            src = ap2(IO["x3"], (tc*512 + s*128)*D, [[D, 128], [1, D]])
            k.wait("sync", G["xt"])
            t_x = k.dsync(lambda e, src=src: e.dma_start(out=xt[:, 0:1024], in_=src), ch="xt")
            k.wait("vector", G["xlnb"])
            t_ln = ln_tile(k, TS, xt, 0, D, xlnb[:, s*D:(s+1)*D], sc, 8*s, sq, [t_x])
            toks_ln.append(t_ln)
            G["xt"] = [t_ln, ("ac", k.cnt["ac"])]
        k.wait("tensor", t_id, *toks_ln)
        t_ev = None
        for c in range(8):
            k.wait("tensor", G["psT"], G["xlnF"] if c == 0 else None, t_ev)
            t_tr = None
            for s in range(4):
                t_tr = k.t(lambda e, c=c, s=s: e.transpose(
                    psT[:, s*128:(s+1)*128], xlnb[:, s*D + c*128: s*D + (c+1)*128], ident[:]))
            k.wait("vector", t_tr)
            t_ev = k.v(lambda e, c=c: e.tensor_copy(xlnF[:, c*512:(c+1)*512], psT[:]))
        G["psT"] = [t_ev]
        G["xlnb"] = [("pe", k.cnt["pe"])]
        t_xlnF = t_ev

        def stage_big(dst, dstkey, srcsb, src_nch, wio, scale_sb, func, t_src, extra_guard=None):
            toks = []
            tw_pipe = {}
            def wld(fo):
                slot = fo % 2
                k.wait("gpsimd", G["wsm"][slot])
                blk = src_nch * 128
                src2 = ap2(IO[wio], fo*128*blk, [[blk, 128], [1, blk]])
                return (k.dgp(lambda e, src2=src2, slot=slot: e.dma_start(
                    out=wsm[slot][:, 0:src_nch*128], in_=src2), ch="wsm%d" % slot), slot)
            tw_pipe[0] = wld(0)
            for fo in range(32):
                if fo + 1 < 32:
                    tw_pipe[fo+1] = wld(fo+1)
                t_w, slot = tw_pipe.pop(fo)
                ps, gk = (psA, "psA") if fo % 2 == 0 else (psB, "psB")
                k.wait("tensor", t_w, t_src, G[gk], extra_guard if fo == 0 else None)
                t_mm = None
                for c in range(src_nch):
                    t_mm = k.t(lambda e, c=c, ps=ps, slot=slot, srcsb=srcsb: e.matmul(
                        ps[:], wsm[slot][:, c*128:(c+1)*128],
                        srcsb[:, c*512:(c+1)*512], start=(c == 0), stop=(c == src_nch-1)))
                G["wsm"][slot] = [("pe", k.cnt["pe"])]
                k.wait("scalar", t_mm, G[dstkey] if fo == 0 else None)
                t_e = k.s(lambda e, fo=fo, ps=ps, dst=dst, scale_sb=scale_sb, func=func: e.activation(
                    out=dst[:, fo*512:(fo+1)*512], in_=ps[:], func=func, scale=col(scale_sb, fo)))
                G[gk] = [t_e]
                toks.append(t_e)
            return toks
        k.wait("scalar", *t_s)
        tk1 = stage_big(pF[0], "p1", xlnF, 8, "w1T", s1, AF.Silu, t_xlnF)
        tk2 = stage_big(pF[1], "p2", xlnF, 8, "w2T", s2, AF.Identity, t_xlnF)
        k.wait("vector", tk1[-1], tk2[-1])
        k.wait("vector", G["p3"])
        t_h = k.v(lambda e: e.tensor_tensor(pF[2][:], pF[0][:], pF[1][:], op=ALU.mult))
        G["p1"] = [t_h]; G["p2"] = [t_h]
        tg1 = stage_big(pF[0], "p1", pF[2], 32, "h1T", sh1, AF.Gelu, t_h, [t_h])
        tg2 = stage_big(pF[1], "p2", pF[0], 32, "h2T", sh2, AF.Gelu, tg1[-1], tg1)
        G["p3"] = [("pe", k.cnt["pe"])]
        k.wait("tensor", tg2[-1])
        yev = {}
        for half in range(2):
            k.wait("gpsimd", G["wmed"][0])
            src3 = ap2(IO["w3T"], half*512, [[1024, 128], [128*1024, 32], [1, 512]])
            t_w3h = k.dgp(lambda e, src3=src3: e.dma_start(out=wmed[0][:], in_=src3), ch="wmed")
            for s in range(4):
                ps, gk = (psA, "psA") if s % 2 == 0 else (psB, "psB")
                k.wait("tensor", t_w3h, G[gk])
                t_mm = None
                for c in range(32):
                    t_mm = k.t(lambda e, c=c, s=s, ps=ps: e.matmul(
                        ps[:], pF[1][:, c*512 + s*128: c*512 + (s+1)*128],
                        wmed[0][:, c*512:(c+1)*512],
                        start=(c == 0), stop=(c == 31)))
                k.wait("vector", t_mm, G["y"], t_s3, yev.get(("g", s)))
                t_ev2 = k.v(lambda e, half=half, s=s, ps=ps: e.scalar_tensor_tensor(
                    out=yt[:, s*1024 + half*512: s*1024 + (half+1)*512], in0=ps[:],
                    scalar=1.0, in1=s3e_b(TS, half), op0=ALU.bypass, op1=ALU.mult))
                G[gk] = [t_ev2]
                yev[(half, s)] = t_ev2
            G["wmed"] = [[("pe", k.cnt["pe"])]]
        for s in range(4):
            t_evs = [yev[(0, s)], yev[(1, s)]]
            t_ln = ln_tile(k, TS, yt, s*1024, D, sq[:, :D], sc, 8, TS["pb_sq2"], t_evs)
            src4 = ap2(IO["x3"], (tc*512 + s*128)*D, [[D, 128], [1, D]])
            k.wait("sync", G["xrow"])
            t_xr4 = k.dsync(lambda e, src4=src4: e.dma_start(out=xrow[:], in_=src4), ch="xrow")
            k.wait("vector", t_ln, t_xr4)
            t_x4 = k.v(lambda e, s=s: e.tensor_tensor(yt[:, s*1024:(s+1)*1024], sq[:, :D], xrow[:], op=ALU.add))
            G["xrow"] = [t_x4]
            yev[("g", s)] = t_x4
            k.wait("sync", t_x4)
            dst = ap2(IO["x4"], (tc*512 + s*128)*D, [[D, 128], [1, D]])
            ty4_st = k.dsync(lambda e, dst=dst, s=s: e.dma_start(out=dst, in_=yt[:, s*1024:(s+1)*1024]), ch="yt")
        G["y"] = [ty4_st]
        G["p2"] = [("pe", k.cnt["pe"])]
        G["wmed"] = [[("pe", k.cnt["pe"])], [("pe", k.cnt["pe"])]]
        G["xlnF"] = [("pe", k.cnt["pe"])]

PHASEF_SBUF = [
    ("pf_x", [128, 1024], F32),
    ("pb_xrow", [128, 1024], F32),
    ("pa_xlnb", [128, 4*1024], BF16),
    ("pa_xlnF", [128, 8*512], BF16),
    ("pa_sq", [128, 1024], F32),
    ("pb_sq2", [128, 1024], F32),
    ("pa_sc", [128, 64], F32),
    ("ident", [128, 128], BF16),
    ("eps", [128, 1], F32),
    ("pf_wm0", [128, 32*512], BF16),
    ("pf_wsm0", [128, 32*128], BF16),
    ("pf_wsm1", [128, 32*128], BF16),
    ("pf_p1", [128, 32*512], BF16),
    ("pf_p2", [128, 32*512], BF16),
    ("pf_p3", [128, 32*512], BF16),
    ("pf_s1", [128, 32], F32),
    ("pf_s2", [128, 32], F32),
    ("pf_sh1", [128, 32], F32),
    ("pf_sh2", [128, 32], F32),
    ("pf_s3e", [128, 1024], F32),
    ("pf_y", [128, 4*1024], F32),
]
PHASEF_PSUM = [
    ("ps_a", [128, 512], F32),
    ("ps_b", [128, 512], F32),
    ("ps_bfA", [128, 512], BF16),
]

def s3e_b(TS, half):
    return TS["pf_s3e"][:, half*512:(half+1)*512]

def build_phaseF():
    io = [
        ("x3", [QT, D], F32, "ExternalInput"),
        ("w1T", [32*128, D], F32, "ExternalInput"),
        ("w2T", [32*128, D], F32, "ExternalInput"),
        ("h1T", [32*128, 4096], F32, "ExternalInput"),
        ("h2T", [32*128, 4096], F32, "ExternalInput"),
        ("w3T", [4096, D], F32, "ExternalInput"),
        ("s1", [128, 32], F32, "ExternalInput"),
        ("s2", [128, 32], F32, "ExternalInput"),
        ("sh1", [128, 32], F32, "ExternalInput"),
        ("sh2", [128, 32], F32, "ExternalInput"),
        ("s3", [1, 1024], F32, "ExternalInput"),
        ("identw", [128, 128], BF16, "ExternalInput"),
        ("x4", [QT, D], F32, "ExternalOutput"),
    ]
    return build_program([emit_phaseF], io, PHASEF_SBUF, PHASEF_PSUM)

def _tile_wT(wT, nch):
    K_, M = wT.shape
    assert K_ == nch*128 and M == 4096
    X = wT.reshape(nch, 128, 32, 128).transpose(2, 1, 0, 3)
    return np.ascontiguousarray(X.reshape(32*128, nch*128)).astype(np.float32)

def host_phaseF_weights(z):
    def sc128(v):
        return np.ascontiguousarray(v.reshape(32, 128).T).astype(np.float32)
    return {
        "w1T": _tile_wT(z['L1_ffn_w1_w'].T, 8),
        "w2T": _tile_wT(z['L1_ffn_w2_w'].T, 8),
        "h1T": _tile_wT(z['L1_ffn_h1_w'].T, 32),
        "h2T": _tile_wT(z['L1_ffn_h2_w'].T, 32),
        "w3T": np.ascontiguousarray(z['L1_ffn_w3_w'].T),
        "s1": sc128(z['L1_ffn_w1_s']), "s2": sc128(z['L1_ffn_w2_s']),
        "sh1": sc128(z['L1_ffn_h1_s']), "sh2": sc128(z['L1_ffn_h2_s']),
        "s3": z['L1_ffn_w3_s'].reshape(1, 1024).astype(np.float32),
    }

PERSIST_SBUF = [
    ("pa_xlnb", [128, 4*1024], BF16),
    ("pa_xlnF", [128, 8*512], BF16),
    ("pa_sq", [128, 1024], F32),
    ("pb_sq2", [128, 1024], F32),
    ("pa_sc", [128, 64], F32),
    ("pb_xrow", [128, 1024], F32),
    ("ident", [128, 128], BF16),
    ("eps", [128, 1], F32),
    ("const1", [128, 1], F32),
    ("ones1", [1, 128], BF16),
]
PERSIST_NAMES = {n for n, _, _ in PERSIST_SBUF}
FUSED_PSUM = [
    ("ps_a", [128, 512], F32),
    ("ps_b", [128, 512], F32),
    ("ps_bfA", [128, 512], BF16),
    ("ps_o", [128, 1024], F32),
]

def _phase_priv(decls):
    return [t for t in decls if t[0] not in PERSIST_NAMES and not t[0].startswith("ps_")]

def build_fused(dbg=False):
    import contextlib
    from math import prod as _prod
    nc = bass.Bass("TRN2", target_bir_lowering=False, debug=False)
    IO = {"_fused": True}
    ext_in = [
        ("x_full", [T, D], F32), ("xq", [QT, D], F32),
        ("WkT", [D, D], F32), ("WvT", [D, D], F32),
        ("identw", [128, 128], BF16), ("masks", [32, 128, 512], F32),
        ("moeW", [9*5*D, D], F32), ("gate_wT", [D, 8], F32), ("gate_b", [1, 8], F32),
        ("b1g", [9, 128, 8], F32), ("b1l", [9, 128, 8], F32),
        ("b3g", [9, 128, 8], F32), ("b3l", [9, 128, 8], F32), ("b2all", [9, 1024], F32),
        ("WxTh", [D, 512], F32), ("WzTh", [D, 512], F32), ("WdtuT", [D, 64], F32),
        ("WdtTh", [64, 512], F32), ("WpBT", [D, 16], F32), ("WpCT", [D, 16], F32),
        ("bdt_h", [128, 4], F32), ("convw_h", [128, 16], F32), ("D_h", [128, 4], F32),
        ("Alog_h", [4, 128, 16], F32),
        ("WoutT", [D, D], F32),
        ("w1T", [32*128, D], F32), ("w2T", [32*128, D], F32),
        ("h1T", [32*128, 4096], F32), ("h2T", [32*128, 4096], F32),
        ("w3T", [4096, D], F32),
        ("s1", [128, 32], F32), ("s2", [128, 32], F32),
        ("sh1", [128, 32], F32), ("sh2", [128, 32], F32), ("s3", [1, 1024], F32),
        ("gselw", [1, 2], F32),
    ]
    for name, shape, dt in ext_in:
        IO[name] = nc.dram_tensor(name, shape, dt, kind="ExternalInput").ap()
    dbgk = "ExternalOutput" if dbg else "Internal"
    internals = [
        ("K", [D, T], BF16, "Internal"), ("V", [T, D], BF16, "Internal"),
        ("xlnqF", [D, QT], BF16, "Internal"), ("xlnq_tok", [QT, D], F32, "Internal"),
        ("x1", [QT, D], F32, dbgk), ("x2own", [QT, D], F32, "Internal"),
        ("x2g", [2*QT, D], F32, "Internal"),
        ("xres", [512, T], F32, "Internal"), ("sz", [512, T], F32, "Internal"),
        ("dtd", [512, T], F32, "Internal"),
        ("bpd", [T*16], F32, "Internal"), ("cpd", [T*16], F32, "Internal"),
        ("gh2", [512, T], BF16, "Internal"), ("gf2", [2*512, T], BF16, "Internal"),
        ("x3own", [QT, D], F32, dbgk),
    ]
    hdl = {}
    for name, shape, dt, kind in internals:
        hdl[name] = nc.dram_tensor(name, shape, dt, kind=kind)
        IO[name] = hdl[name].ap()
    IO["x4"] = nc.dram_tensor("x4", [QT, D], F32, kind="ExternalOutput").ap()

    stack = contextlib.ExitStack()
    TS = {}
    for name, shape, dt in PERSIST_SBUF:
        TS[name] = stack.enter_context(nc.sbuf_tensor(name, shape, dt))
    def perbytes(shape, dt):
        n = 1
        for x in shape[1:]: n *= x
        sz = n * mybir.dt.size(dt)
        return (sz + 31) // 32 * 32
    phase_privs = [_phase_priv(x) for x in (PHASEA_SBUF, PHASEB_SBUF, PHASEC_SBUF,
                                            PHASED_SBUF, PHASEE_SBUF, PHASEF_SBUF)]
    ovl = max(sum(perbytes(sh, dt) for _, sh, dt in pp) for pp in phase_privs)
    backing = stack.enter_context(nc.sbuf_tensor("ovl", [128, ovl], mybir.dt.uint8))
    base_addr = nc.lookup_mloc(backing).addr
    for pp in phase_privs:
        cur = 0
        for name, shape, dt in pp:
            if name in TS:
                cur += perbytes(shape, dt)
                continue
            h = nc.virtual_sbuf_tensor(name, shape, dt)
            nc._tensor(h.name, shape, dt, type="SB")
            mloc = nc.lookup_mloc(h)
            mloc.allocated = True
            mloc.addr = base_addr + cur
            TS[name] = h
            cur += perbytes(shape, dt)
    for name, shape, dt in FUSED_PSUM:
        TS[name] = stack.enter_context(nc.psum_tensor(name, shape, dt))
    sems = {}
    for sname in ("pe", "dv", "ac", "gc", "cc1", "cc2"):
        sems[sname] = stack.enter_context(nc.semaphore(sname))
    dma_sems = [stack.enter_context(nc.semaphore("q%d" % i)) for i in range(48)]

    RG = [[0, 1], [2, 3], [4, 5], [6, 7]]
    if dbg:
        IO["x2dbg"] = nc.dram_tensor("x2dbg", [QT, D], F32, kind="ExternalOutput").ap()
    def cc_x2(k, TS, IO):
        k.barrier()
        if dbg:
            td = k.dgp(lambda e: e.dma_start(out=IO["x2dbg"][:, :], in_=IO["x2own"][:, :]), ch="dbg")
            k.wait("gpsimd", td)
        tok = k._do("gpsimd", "cc1", 1, lambda e: e.collective_compute(
            "AllGather", ALU.bypass, replica_groups=RG,
            ins=[hdl["x2own"].ap().opt()], outs=[hdl["x2g"].ap().opt()]))
        k.wait("gpsimd", tok)
        k.barrier()
    def cc_g(k, TS, IO):
        k.barrier()
        tok = k._do("gpsimd", "cc2", 1, lambda e: e.collective_compute(
            "AllGather", ALU.bypass, replica_groups=RG,
            ins=[hdl["gh2"].ap().opt()], outs=[hdl["gf2"].ap().opt()]))
        k.wait("gpsimd", tok)
        k.barrier()

    def phA(k, TS, IO): emit_phaseA(k, TS, IO)
    def phB(k, TS, IO): emit_phaseB(k, TS, IO)
    def phC(k, TS, IO):
        io2 = dict(IO); io2["x2"] = IO["x2own"]
        emit_phaseC(k, TS, io2)
    def phD(k, TS, IO):
        io2 = dict(IO); io2["x2f"] = IO["x2g"]; io2["gh"] = IO["gh2"]
        emit_phaseD(k, TS, io2)
    def phE(k, TS, IO):
        io2 = dict(IO); io2["gf"] = IO["gf2"]; io2["x2rows"] = IO["x2own"]; io2["x3"] = IO["x3own"]
        emit_phaseE(k, TS, io2)
    def phF(k, TS, IO):
        io2 = dict(IO); io2["x3"] = IO["x3own"]
        emit_phaseF(k, TS, io2)

    seq = [phA, phB, phC, cc_x2, phD, cc_g, phE, phF]
    def with_bar(ph):
        def f(k, TS, IO):
            ph(k, TS, IO)
            k.barrier()
        return f
    phases = [with_bar(p) if p not in (cc_x2, cc_g) else p for p in seq]

    with stack:
        with nc.Block() as block:
            def mk(eng_name):
                def run(engine):
                    k = Ctx(nc, sems, dma_sems, eng_name, engine)
                    t_eps = k.v(lambda e: e.memset(TS["eps"][:], EPS))
                    k.wait("scalar", t_eps)
                    for ph in phases:
                        ph(k, TS, IO)
                    k.wait("gpsimd", *k.dma_done_tokens())
                return run
            block.sync(mk("sync"))
            block.tensor(mk("tensor"))
            block.vector(mk("vector"))
            block.scalar(mk("scalar"))
            block.gpsimd(mk("gpsimd"))
    return nc

def host_fused_inputs(z, ident):
    x0 = np.asarray(z['L0_attn_x'])
    rows_h = {h: np.array([r for j in core_blocks(h) for r in range(j*BS, (j+1)*BS)]) for h in (0, 1)}
    W = z['L0_attn_W']
    WkT = np.ascontiguousarray(W[:1024].T).astype(np.float32)
    WvT = np.ascontiguousarray(W[1024:].T).astype(np.float32)
    moe = host_moe_weights(z)
    wD = {h: host_phaseD_weights(z, h) for h in (0, 1)}
    wF = host_phaseF_weights(z)
    WoutT = np.ascontiguousarray(z['L1_attn_W_out'].T).astype(np.float32)
    masks = {h: host_masks(h) for h in (0, 1)}
    inm = []
    for c in range(8):
        b, h = c // 2, c % 2
        d = {"x_full": np.ascontiguousarray(x0[b]),
             "xq": np.ascontiguousarray(x0[b][rows_h[h]]),
             "WkT": WkT, "WvT": WvT, "identw": ident, "masks": masks[h],
             "WoutT": WoutT,
             "gselw": np.array([[1.0 - h, float(h)]], np.float32)}
        d.update(moe); d.update(wD[h]); d.update(wF)
        inm.append(d)
    return inm, rows_h


_BUILD_CACHE = {}

def _get(b, f):
    if b not in _BUILD_CACHE:
        _BUILD_CACHE[b] = f()
    return _BUILD_CACHE[b]

def kernel(**inputs):
    import ml_dtypes
    bf = ml_dtypes.bfloat16
    z = inputs
    x0 = np.asarray(z['L0_attn_x'])
    ident = np.eye(128, dtype=np.float32).astype(bf)
    cores = list(range(8))
    rows_h = {h: np.array([r for j in core_blocks(h) for r in range(j*BS, (j+1)*BS)]) for h in (0, 1)}
    W = z['L0_attn_W']
    WkT = np.ascontiguousarray(W[:1024].T).astype(np.float32)
    WvT = np.ascontiguousarray(W[1024:].T).astype(np.float32)

    ncA = _get('A', build_phaseA)
    inm = []
    for c in cores:
        b, h = c // 2, c % 2
        inm.append({"x_full": np.ascontiguousarray(x0[b]),
                    "xq": np.ascontiguousarray(x0[b][rows_h[h]]),
                    "WkT": WkT, "WvT": WvT, "identw": ident})
    resA = run_bass_kernel_spmd(ncA, inm, core_ids=cores)
    rA = resA.results

    ncB = _get('B', build_phaseB)
    masks = {h: host_masks(h) for h in (0, 1)}
    inm = []
    for c in cores:
        b, h = c // 2, c % 2
        inm.append({"xq": np.ascontiguousarray(x0[b][rows_h[h]]),
                    "xlnqF": np.asarray(rA[c]["xlnqF"]),
                    "xlnq_tok": np.asarray(rA[c]["xlnq_tok"]),
                    "K": np.asarray(rA[c]["K"]), "V": np.asarray(rA[c]["V"]),
                    "masks": masks[h], "identw": ident})
    resB = run_bass_kernel_spmd(ncB, inm, core_ids=cores)
    rB = resB.results

    ncC = _get('C', build_phaseC)
    moe = host_moe_weights(z)
    inm = []
    for c in cores:
        d = dict(moe)
        d["x1"] = np.asarray(rB[c]["x1"])
        d["identw"] = ident
        inm.append(d)
    resC = run_bass_kernel_spmd(ncC, inm, core_ids=cores)
    rC = resC.results

    x2full = {}
    for b in range(4):
        xf = np.empty((T, D), np.float32)
        xf[rows_h[0]] = np.asarray(rC[2*b]["x2"])
        xf[rows_h[1]] = np.asarray(rC[2*b+1]["x2"])
        x2full[b] = xf

    ncD = _get('D', build_phaseD)
    wD = {h: host_phaseD_weights(z, h) for h in (0, 1)}
    inm = []
    for c in cores:
        b, h = c // 2, c % 2
        d = dict(wD[h])
        d["x2f"] = x2full[b]
        d["identw"] = ident
        inm.append(d)
    resD = run_bass_kernel_spmd(ncD, inm, core_ids=cores)
    rD = resD.results

    ncE = _get('E', build_phaseE)
    WoutT = np.ascontiguousarray(z['L1_attn_W_out'].T).astype(np.float32)
    inm = []
    for c in cores:
        b, h = c // 2, c % 2
        gf = np.concatenate([np.asarray(rD[2*b]["gh"]), np.asarray(rD[2*b+1]["gh"])], 0)
        gf = gf[:, h*4096:(h+1)*4096]
        inm.append({"gf": np.ascontiguousarray(gf), "WoutT": WoutT,
                    "x2rows": np.ascontiguousarray(x2full[b][h*4096:(h+1)*4096])})
    resE = run_bass_kernel_spmd(ncE, inm, core_ids=cores)
    rE = resE.results

    ncF = _get('F', build_phaseF)
    wF = host_phaseF_weights(z)
    inm = []
    for c in cores:
        d = dict(wF)
        d["x3"] = np.asarray(rE[c]["x3"])
        d["identw"] = ident
        inm.append(d)
    resF = run_bass_kernel_spmd(ncF, inm, core_ids=cores)
    rF = resF.results

    out = np.empty((4, T, D), np.float32)
    for c in cores:
        b, h = c // 2, c % 2
        out[b, h*4096:(h+1)*4096] = np.asarray(rF[c]["x4"])
    return out
